# version 19
# baseline (speedup 1.0000x reference)
import numpy as np
import ml_dtypes
import concourse.bass as bass
import concourse.mybir as mybir
from concourse.bass_utils import run_bass_kernel_spmd

F32 = mybir.dt.float32
BF16 = mybir.dt.bfloat16
AF = mybir.ActivationFunctionType
ALU = mybir.AluOpType
AX = mybir.AxisListType
BF = ml_dtypes.bfloat16

T = 2048
DM = 2048
NKT = 16
NNB = 4
NB = 512
NH = 16
NKV = 8
CH = 128
EPS = 1e-5
NE = 8
INTER = 8192
PROJ = 6160
KB = 1024

DEBUG = False


class Ctr:
    def __init__(self, nc, name):
        self.h = nc.alloc_semaphore(name)
        self.n = 0

    def inc(self, ins, k=1):
        ins.then_inc(self.h, k)
        self.n += k
        return self.n


def build_program(debug=False):
    nc = bass.Bass("TRN2", target_bir_lowering=False,
                   detect_race_conditions=False)
    C = {}

    def din(name, shape, dtype=F32):
        C[name] = nc.dram_tensor(name, shape, dtype, kind="ExternalInput").ap()
        return C[name]

    x_in = din("x_in", [T, DM])
    xprev_in = din("xprev_in", [T, DM])
    W_in = din("W_in", [DM, PROJ])
    W_out_w = din("W_out_w", [DM, DM])
    w1 = din("w1", [DM, INTER])
    w3 = din("w3", [DM, INTER])
    w2 = din("w2", [INTER, DM])
    Wg = din("Wg", [NE, DM, DM])
    Wu = din("Wu", [NE, DM, DM])
    Wd = din("Wd", [NE, DM, DM])
    Wr = din("Wr", [DM, NE])
    ln1w_in = din("ln1w_in", [128, 16])
    ln2w_in = din("ln2w_in", [128, 16])
    normw_in = din("normw_in", [128, 16])
    cw_in = din("cw_in", [128, 32, 4])
    cb_in = din("cb_in", [128, 32])
    dtb_in = din("dtb_in", [16, 1])
    alog_in = din("alog_in", [16, 1])
    db_in = din("db_in", [128, 16])
    flag_in = din("flag_in", [128, 1])
    temp_in = din("temp_in", [1, 1])
    identb_in = din("identb_in", [128, 128], BF16)
    identf_in = din("identf_in", [128, 128], F32)
    triu_in = din("triu_in", [128, 128], BF16)
    maskneg_in = din("maskneg_in", [128, 128], F32)
    ones1f_in = din("ones1f_in", [1, 128], F32)
    ones1b_in = din("ones1b_in", [1, 128], BF16)
    onescol_in = din("onescol_in", [128, 1], BF16)
    ones128b_in = din("ones128b_in", [128, 128], BF16)
    maskneg4_in = din("maskneg4_in", [128, 512], F32)
    triu4_in = din("triu4_in", [128, 512], BF16)
    triuN_in = din("triuN_in", [128, 128], BF16)
    constv_in = din("constv_in", [128, 4], F32)

    out = nc.dram_tensor("out", [T, DM], F32, kind="ExternalOutput").ap()

    def scr(name, shape, dtype=BF16):
        kind = "ExternalOutput" if debug else "Internal"
        C[name] = nc.dram_tensor(name, shape, dtype, kind=kind).ap()
        return C[name]

    xn_sp = scr("xn_sp", [DM, T])
    zx_sp = scr("zx_sp", [PROJ, T])
    yu_sp = scr("yu_sp", [DM, T])
    am_sp = scr("am_sp", [DM, T])
    hm_sp = scr("hm_sp", [INTER, T])
    ffn_sp = scr("ffn_sp", [NE, DM, T])
    ffnsum_sp = scr("ffnsum_sp", [DM, T])

    dbg = {}
    if debug:
        def dout(name, shape, dtype=F32):
            dbg[name] = nc.dram_tensor(name, shape, dtype, kind="ExternalOutput").ap()
        dout("d_s0", [16, 128, 128])
        dout("d_w8", [16, 128, NE])
        dout("d_rs2", [1, T])
        dout("d_rsb", [128, T])
        dout("d_rsrow", [1, T])
        dout("d_inh", [8, T])
        dout("d_rt", [8, 1])

    ARENA_BYTES = 202 * KB
    arena = nc.alloc_sbuf_tensor("arena", [128, ARENA_BYTES], mybir.dt.uint8)
    BASE = nc.lookup_mloc(arena).addr

    def at(name, shape, dtype, off):
        per = int(np.prod(shape[1:])) * mybir.dt.size(dtype)
        assert off + per <= ARENA_BYTES, (name, off, per)
        return nc.alloc_sbuf_tensor_at(name, shape, dtype, offset=BASE + off)

    identb = at("identb", [128, 128], BF16, 0)
    triu = at("triu", [128, 128], BF16, 256)
    identf = at("identf", [128, 128], F32, 512)
    maskneg = at("maskneg", [128, 128], F32, 1024)
    ones1f = at("ones1f", [1, 128], F32, 1536)
    ones1b = at("ones1b", [1, 128], BF16, 2048)
    onescol = at("onescol", [128, 1], BF16, 2304)
    ln1w = at("ln1w", [128, 16], F32, 2368)
    ln2w = at("ln2w", [128, 16], F32, 2432)
    normw = at("normw", [128, 16], F32, 2496)
    cw = at("cw", [128, 32, 4], F32, 2560)
    cb = at("cb", [128, 32], F32, 3072)
    dtb = at("dtb", [16, 1], F32, 3200)
    alog = at("alog", [16, 1], F32, 3232)
    aneg = at("aneg", [16, 1], F32, 3264)
    db = at("db", [128, 16], F32, 3296)
    flagc = at("flagc", [128, 1], F32, 3360)
    tempv = at("tempv", [1, 1], F32, 3392)
    rtemp = at("rtemp", [8, 1], F32, 3424)
    tmp16a = at("tmp16a", [16, 1], F32, 3456)
    stat = at("stat", [128, 12], F32, 3488)
    tails = at("tails", [128, 32, 3], BF16, 3552)
    pretail = at("pretail", [128, 16, 3], BF16, 3744)
    grow = at("grow", [1, 16], F32, 3840)
    rsyrow = at("rsyrow", [1, 128], F32, 3904)
    gam_s = at("gam_s", [128, 1], F32, 4416)
    rsyrow2 = at("rsyrow2", [1, 128], F32, 12 * KB)

    ones128b = at("ones128b", [128, 128], BF16, 4608)
    constv = at("constv", [128, 4], F32, 4864)
    nc.const_aps.register(F32, EPS, constv[:, 0:1])
    nc.const_aps.register(F32, 1.0, constv[:, 1:2])
    ut1 = at("ut1", [128, 128], F32, 13 * KB)
    csr2 = at("csr2", [1, 2, 128], F32, 13 * KB + 512)
    rs_row = at("rs_row", [1, T], F32, 5 * KB)
    w8row = at("w8row", [1, T], BF16, 5 * KB)
    rs2acc = at("rs2acc", [1, T], F32, 5 * KB)
    O_XT = 15 * KB

    xtile = at("xtile", [128, DM], F32, O_XT)
    u_t = at("u_t", [128, 16, 128], BF16, O_XT + 4 * KB)
    O_WS = O_XT + 8 * KB

    wsl0 = at("wsl0", [128, 16, 128], BF16, O_WS)
    wsl1 = at("wsl1", [128, 16, 128], BF16, O_WS + 4 * KB)
    ffn_t = at("ffn_t", [128, 8, 512], BF16, O_WS)
    O_RSB = O_WS + 8 * KB

    RSB = at("RSB", [128, T], F32, O_RSB)
    O_S0 = O_RSB + 8 * KB

    S0 = at("S0", [128, 16, 128], F32, O_S0)
    outtm = at("outtm", [128, DM], F32, O_S0)
    O_DT = O_S0 + 8 * KB

    dtrawH = at("dtrawH", [16, 1024], BF16, O_DT)
    dttmp = at("dttmp", [16, 512], F32, O_DT + 2 * KB)
    dtsp = at("dtsp", [16, 512], F32, O_DT + 4 * KB)
    dtda = at("dtda", [16, 512], BF16, O_DT + 6 * KB)
    dt16 = at("dt16", [16, 512], BF16, O_DT + 7 * KB)
    am_t = at("am_t", [128, 2, 512], BF16, O_DT)
    wsl2 = at("wsl2", [128, 16, 128], BF16, O_DT)
    wsl3 = at("wsl3", [128, 16, 128], BF16, O_DT + 4 * KB)
    O_W8 = O_DT + 8 * KB

    w8tm = at("w8tm", [128, 16, 8], F32, O_W8)
    O_STR = O_W8 + 4 * KB

    rowb = at("rowb", [128, DM], BF16, O_STR)
    xnt = at("xnt", [128, 2, 512], BF16, O_STR + 4 * KB)
    sgt = at("sgt", [128, 2, 512], BF16, O_STR + 4 * KB)
    O_A = O_STR + 6 * KB

    A_xn = at("A_xn", [128, 16, T], BF16, O_A)
    A_xno = at("A_xno", [128, 16, 1024], BF16, O_A)
    A_ctmp = at("A_ctmp", [128, 2, 1024], F32, O_A + 32 * KB)
    A_rawo = at("A_rawo", [128, 4, 1027], BF16, O_A + 40 * KB)
    A_rawc = at("A_rawc", [128, 16, 128], BF16, O_A + 32 * KB)
    A_sswp = at("A_sswp", [128, 7424], BF16, O_A + 49 * KB)
    A_yub = at("A_yub", [128, 16, 512], BF16, O_A)
    A_ctm2 = at("A_ctm2", [128, 2, 128], F32, O_A + 16 * KB)
    A_raw = at("A_raw", [128, 32, 515], BF16, O_A + 17 * KB + 32)
    A_ssw2 = at("A_ssw2", [128, 7424], BF16, O_A + 49 * KB + 256)
    A_yur = at("A_yur", [128, 2, 16, 512], BF16, O_A)
    A_h2 = at("A_h2", [128, 16, T], BF16, O_A)
    A_hm = at("A_hm", [128, 64, 512], BF16, O_A)
    O_B = O_A + 64 * KB

    B_h1 = at("B_h1", [128, 16, T], BF16, O_B)
    B_h1o = at("B_h1o", [128, 16, 1024], BF16, O_B)
    B_silo = at("B_silo", [128, 16, 1024], BF16, O_B + 32 * KB)
    B_silnb = at("B_silnb", [128, 32, 512], BF16, O_B)
    B_siz = at("B_siz", [128, 2, 16, 128], BF16, O_B + 32 * KB)
    B_zr = at("B_zr", [128, 16, 512], BF16, O_B + 40 * KB)
    gctw = at("gctw", [128, 4, 512], F32, O_B + 56 * KB)
    cmb_t = at("cmb_t", [128, 2, 512], F32, O_B + 56 * KB)
    pf_t = at("pf_t", [128, 8, 512], BF16, O_B)
    pacc = at("pacc", [128, 2, 512], F32, O_B + 8 * KB)
    pout = at("pout", [128, 512], BF16, O_B + 12 * KB)
    B_inh = at("B_inh", [8, T], F32, O_B)
    B_rt = at("B_rt", [128, 9, 8], F32, O_B + 8 * KB)
    B_ofm = at("B_ofm", [128, 16, 512], F32, O_B)
    w2r = at("w2r", [128, 4, 1024], BF16, O_B + 32 * KB)
    O_END = O_B + 64 * KB
    assert O_END <= ARENA_BYTES

    def ssd_views(tag, base_off):
        v = {}
        v["xv_tm"] = at(f"{tag}xv", [128, 8, 128], BF16, base_off)
        v["b_tm"] = at(f"{tag}bt", [128, 8, 128], BF16, base_off + 2 * KB)
        v["sbf"] = at(f"{tag}sb", [128, 16, 128], BF16, base_off + 4 * KB)
        v["ltp"] = at(f"{tag}lp", [128, 2, 128], F32, base_off + 8 * KB)
        v["lt"] = at(f"{tag}lt", [128, 2, 128], F32, base_off + 9 * KB)
        v["ecsb"] = at(f"{tag}ec", [128, 2, 128], BF16, base_off + 10 * KB)
        v["cp"] = at(f"{tag}cp", [128, 2, 128], BF16, base_off + 10 * KB + 512)
        v["mt"] = at(f"{tag}mt", [128, 2, 128], BF16, base_off + 11 * KB)
        v["xb"] = at(f"{tag}xb", [128, 4, 128], BF16, base_off + 11 * KB + 512)
        v["cs_tm"] = at(f"{tag}cs", [128, 16], F32, base_off + 12 * KB + 512)
        v["ncs"] = at(f"{tag}nc", [128, 16], F32, base_off + 12 * KB + 576)
        v["dt_tm"] = at(f"{tag}dt", [128, 16], F32, base_off + 12 * KB + 640)
        v["dec_tm"] = at(f"{tag}de", [128, 16], F32, base_off + 12 * KB + 704)
        v["da_tm"] = at(f"{tag}da", [128, 16], BF16, base_off + 12 * KB + 768)
        v["da_tmF"] = at(f"{tag}dF", [128, 16], F32, base_off + 13 * KB + 960)
        v["da_bc"] = at(f"{tag}db", [128, 2, 128], BF16, base_off + 12 * KB + 800)
        v["cs_fm"] = at(f"{tag}cf", [16, 128], F32, base_off + 13 * KB + 320)
        v["dec_fm"] = at(f"{tag}df", [16, 128], F32, base_off + 13 * KB + 832)
        return v

    SSD_PRE = ssd_views("p_", O_A + 49 * KB)
    SSD_MAIN = ssd_views("m_", O_A + 49 * KB + 256)
    w8b = at("w8b", [128, T], BF16, O_XT)

    ltp4 = at("ltp4", [128, 512], F32, O_A + 17 * KB + 32)
    dbt4 = at("dbt4", [128, 512], BF16, O_A + 19 * KB + 32)
    ec4 = at("ec4", [128, 2, 512], BF16, O_A + 20 * KB + 32)
    cp4 = at("cp4", [128, 512], BF16, O_A + 22 * KB + 32)
    mt4 = at("mt4", [128, 512], BF16, O_A + 16 * KB)
    lt4 = at("lt4", [128, 2, 512], F32, O_W8)
    ut4 = at("ut4", [128, 4, 128], F32, 13 * KB + 512)
    maskneg4 = at("maskneg4", [128, 512], F32, O_STR)
    triu4 = at("triu4", [128, 512], BF16, O_STR + 2 * KB)
    triuN = at("triuN", [128, 128], BF16, O_STR + 3 * KB)
    rsytmp = at("rsytmp", [1, 128], F32, O_STR + 3 * KB + 256)
    da_bc4 = at("da_bc4", [128, 512], BF16, O_DT + 7 * KB)
    sgt4a = at("sgt4a", [128, 2, 4, 128], BF16, O_STR + 4 * KB)

    import contextlib
    psctx = contextlib.ExitStack()
    PS = [psctx.enter_context(nc.psum_tensor(f"PS{i}", [128, 512], F32))
          for i in range(8)]
    nc._keep_ps = psctx

    PT = [PS[6].ap().bitcast(BF16), PS[7].ap().bitcast(BF16)]

    qg = DPool(nc, "qg", 12)
    ql = DPool(nc, "ql", 12)
    qs = DPool(nc, "qs", 12)
    sg = Ctr(nc, "sg")
    sv = Ctr(nc, "sv")
    ss = Ctr(nc, "ss")
    stc = Ctr(nc, "stc")

    names = dict(locals())
    names.pop("C")
    obj = type("K", (), {})()
    for k, v in names.items():
        setattr(obj, k, v)
    return obj


class EList(list):
    ctr = None


class Plan:
    def __init__(self):
        self.y = EList()
        self.g = EList()
        self.t = EList()
        self.s = EList()
        self.v = EList()


def W(lst, ctr, val):
    if val > 0:
        lst.append(lambda e, h=ctr.h, x=val: e.wait_ge(h, x))


def OP(lst, fn, ctr=None, k=1):
    auto = getattr(lst, "ctr", None)
    if auto is not None:
        c = ctr if ctr is not None else auto
        assert c is auto, "self-synced lists must use their own engine ctr"
        prev = c.n
        c.n += k

        def f2(e, f=fn, h=c.h, kk=k, p=prev):
            if p > 0:
                e.wait_ge(h, p)
            f(e).then_inc(h, kk)
        lst.append(f2)
        return c.n
    if ctr is None:
        lst.append(fn)
        return None
    ctr.n += k
    lst.append(lambda e, f=fn, h=ctr.h, kk=k: f(e).then_inc(h, kk))
    return ctr.n


class DPool:

    def __init__(self, nc, name, n):
        self.sems = [Ctr(nc, f"{name}{i}") for i in range(n)]
        self.i = 0

    def next(self):
        c = self.sems[self.i]
        self.i = (self.i + 1) % len(self.sems)
        return c


def DMA(lst, pool, dst, src):
    c = pool.next()
    prev = c.n
    c.n += 16
    tgt = c.n

    def f(e, d=dst, s=src, h=c.h, p=prev, k=16):
        if p > 0:
            e.wait_ge(h, p)
        e.dma_start(out=d, in_=s).then_inc(h, k)
    lst.append(f)
    return (c, tgt)


def WE(lst, evt):
    if not evt:
        return
    if isinstance(evt, list):
        for e in evt:
            WE(lst, e)
        return
    c, v = evt
    lst.append(lambda e, h=c.h, x=v: e.wait_ge(h, x))


def SNAP(pool):
    return [(c, c.n) for c in pool.sems if c.n > 0]


def emit(nc, plan):
    with nc.Block() as block:
        @block.sync
        def _(y):
            for f in plan.y:
                f(y)

        @block.gpsimd
        def _(g):
            for f in plan.g:
                f(g)

        @block.tensor
        def _(t):
            for f in plan.t:
                f(t)

        @block.scalar
        def _(s):
            for f in plan.s:
                f(s)

        @block.vector
        def _(v):
            for f in plan.v:
                f(v)


def ln_tiles(K, P, src_ap, dst, half_only, ev):
    tiles = range(8) if half_only is not None else range(16)
    stat = K.stat
    E_xn = 0
    E_rsrow_t = 0
    for i in tiles:
        ti = (half_only * 8 + i) if half_only is not None else i
        col0 = (i * 128) if half_only is not None else (ti * 128)
        W(P.y, K.sv, ev.get("xt_free_v", 0))
        W(P.y, K.ss, ev.get("xt_free_s", 0))
        ld = DMA(P.y, K.ql, K.xtile[:], src_ap[ti * 128:(ti + 1) * 128, :])
        WE(P.s, ld)
        WE(P.s, ev.get("consts_l"))
        W(P.s, K.stc, ev.get("rowb_free_t", 0))
        e_sq = OP(P.s, lambda s: s.activation(K.rowb[:], K.xtile[:], AF.Square),
                  K.ss)
        WE(P.v, ld)
        OP(P.v, lambda v: v.reduce_sum(stat[:, 0:1], K.xtile[:], axis=AX.X))
        W(P.v, K.ss, e_sq)
        OP(P.v, lambda v: v.reduce_sum(stat[:, 1:2], K.rowb[:], axis=AX.X))
        OP(P.v, lambda v: v.tensor_scalar_mul(stat[:, 2:3], stat[:, 0:1], 1.0 / DM))
        OP(P.v, lambda v: v.tensor_scalar_mul(stat[:, 3:4], stat[:, 1:2], 1.0 / DM))
        OP(P.v, lambda v: v.tensor_mul(stat[:, 4:5], stat[:, 2:3], stat[:, 2:3]))
        OP(P.v, lambda v: v.tensor_sub(stat[:, 5:6], stat[:, 3:4], stat[:, 4:5]))
        OP(P.v, lambda v: v.tensor_scalar_add(stat[:, 6:7], stat[:, 5:6], EPS))
        OP(P.v, lambda v: v.reciprocal(stat[:, 7:8], stat[:, 6:7]))
        e_var = OP(P.v, lambda v: v.tensor_mul(
            stat[:, 6:7], stat[:, 5:6], stat[:, 7:8]), K.sv)
        W(P.s, K.sv, e_var)
        OP(P.s, lambda s: s.activation(stat[:, 8:9], stat[:, 5:6], AF.Ln, bias=EPS))
        e_rstd = OP(P.s, lambda s: s.activation(stat[:, 9:10], stat[:, 8:9],
                                                AF.Exp, scale=-0.5), K.ss)
        OP(P.s, lambda s: s.activation(stat[:, 10:11], stat[:, 6:7], AF.Ln, bias=EPS))
        e_rs = OP(P.s, lambda s: s.activation(stat[:, 11:12], stat[:, 10:11],
                                              AF.Exp, scale=-0.5), K.ss)
        W(P.v, K.ss, e_rstd)
        W(P.v, K.stc, ev.get("rowb_free_t", 0))
        e_xn = OP(P.v, lambda v: v.tensor_scalar(
            K.rowb[:], K.xtile[:], stat[:, 2:3], stat[:, 9:10],
            ALU.subtract, ALU.mult), K.sv)
        e_ev = 0
        for k in range(16):
            par = k & 1
            W(P.t, K.sv, e_xn)
            W(P.t, K.sv, ev.get(f"pt{par}_free", 0))
            e_tp = OP(P.t, lambda t, kk=k, pp=par: t.transpose(
                K.PT[pp][:, 0:128], K.rowb[:, kk * 128:(kk + 1) * 128],
                K.identb[:]), K.stc)
            W(P.v, K.stc, e_tp)
            e_ev = OP(P.v, lambda v, kk=k, pp=par, c0=col0: v.tensor_copy(
                dst[:, kk, c0:c0 + 128], K.PT[pp][:, 0:128]), K.sv)
            ev[f"pt{par}_free"] = e_ev
            ev["rowb_free_t"] = e_tp
        W(P.s, K.sv, e_var)
        W(P.t, K.ss, e_rs)
        W(P.t, K.sv, ev.get("ps5_free", 0))
        e_tr = OP(P.t, lambda t: t.transpose(
            K.PS[5][0:1, 0:128], stat[:, 11:12], K.identf[:]), K.stc)
        W(P.v, K.stc, e_tr)
        e_rr = OP(P.v, lambda v, c=ti * 128: v.tensor_copy(
            K.rs_row[0:1, c:c + 128], K.PS[5][0:1, 0:128]), K.sv)
        ev["ps5_free"] = e_rr
        E_rsrow_t = e_rr
        ev["xt_free_v"] = e_xn
        ev["xt_free_s"] = e_rs
        ev["rowb_free_v"] = e_ev
        E_xn = e_ev
    return E_xn, E_rsrow_t


def build_rsb(K, P, cols0, ncols, e_ready_v):
    e_last = 0
    evs = {}
    for j in range(ncols // 512):
        c = cols0 + j * 512
        bank = K.PS[j % 2]
        W(P.t, K.sv, e_ready_v)
        W(P.t, K.sv, evs.get(j % 2, 0))
        e_mm = OP(P.t, lambda t, b=bank, cc=c: t.matmul(
            b[:, 0:512], K.ones1f[0:1, :], K.rs_row[0:1, cc:cc + 512],
            start=True, stop=True), K.stc)
        W(P.v, K.stc, e_mm)
        e_last = OP(P.v, lambda v, b=bank, cc=c: v.tensor_copy(
            K.RSB[:, cc:cc + 512], b[:, 0:512]), K.sv)
        evs[j % 2] = e_last
    return e_last


def softplus_block(K, P, dst_sp, dst_da, src_bf, n, src_evt):
    WE(P.s, [src_evt])
    OP(P.s, lambda s: s.activation(K.dttmp[:, 0:n], src_bf, AF.Abs, bias=K.dtb[:]))
    OP(P.s, lambda s: s.activation(K.dttmp[:, 0:n], K.dttmp[:, 0:n], AF.Exp,
                                   scale=-1.0))
    e_l = OP(P.s, lambda s: s.activation(K.dttmp[:, 0:n], K.dttmp[:, 0:n], AF.Ln,
                                         bias=1.0), K.ss)
    WE(P.v, [src_evt])
    OP(P.v, lambda v: v.tensor_scalar(dst_sp, src_bf, K.dtb[:], 0.0,
                                      ALU.add, ALU.max))
    W(P.v, K.ss, e_l)
    OP(P.v, lambda v: v.tensor_add(dst_sp, dst_sp, K.dttmp[:, 0:n]))
    e_da = OP(P.v, lambda v: v.tensor_scalar_mul(dst_da, dst_sp, K.aneg[:]),
              K.sv)
    return e_da


def chunk_prep(K, P, SW, da_ap, dt_ap, ev, need_dec):
    W(P.v, K.ss, K.ss.n)
    if ev.get("single_pt"):
        p = 0
    else:
        p = ev.get("prep_par", 0)
        ev["prep_par"] = 1 - p
    W(P.t, K.sv, ev.get(f"pt{p}_free", 0))
    W(P.v, K.sv, ev.get("da_src_v", 0))
    W(P.t, K.sv, ev.get("da_src_v", 0))
    e_tp = OP(P.t, lambda t, pp=p, a=da_ap: t.transpose(
        K.PT[pp][:, 0:16], a, K.identb[0:16, 0:16]), K.stc)
    W(P.v, K.stc, e_tp)
    OP(P.v, lambda v, pp=p: v.tensor_copy(SW["da_tmF"][:], K.PT[pp][:, 0:16]))
    e_da = OP(P.v, lambda v, pp=p: v.tensor_copy(
        SW["da_tm"][:], K.PT[pp][:, 0:16]), K.sv)
    ev[f"pt{p}_free"] = e_da
    W(P.t, K.sv, ev.get("ps3_free", 0))
    W(P.t, K.sv, ev.get("ps3h_free", 0))
    e_tp2 = OP(P.t, lambda t, a=dt_ap: t.transpose(
        K.PS[3][:, 0:16], a, K.identf[0:16, 0:16]), K.stc)
    W(P.v, K.stc, e_tp2)
    e_dt = OP(P.v, lambda v: v.tensor_copy(SW["dt_tm"][:], K.PS[3][:, 0:16]),
              K.sv)
    W(P.t, K.sv, e_dt)
    W(P.t, K.sv, e_da)
    e_cs = OP(P.t, lambda t: t.matmul(K.PS[3][:, 16:32], K.triu[:],
                                      SW["da_tm"][:], start=True, stop=True),
              K.stc)
    W(P.v, K.stc, e_cs)
    e_csv = OP(P.v, lambda v: v.tensor_copy(SW["cs_tm"][:], K.PS[3][:, 16:32]),
               K.sv)
    e_ncs = OP(P.v, lambda v: v.tensor_scalar_mul(SW["ncs"][:], SW["cs_tm"][:],
                                                  -1.0), K.sv)
    W(P.t, K.sv, e_csv)
    e_tp3 = OP(P.t, lambda t: t.transpose(K.PS[3][0:16, 32:160], SW["cs_tm"][:],
                                          K.identf[:]), K.stc)
    W(P.v, K.stc, e_tp3)
    e_csf = OP(P.v, lambda v: v.tensor_copy(SW["cs_fm"][:],
                                            K.PS[3][0:16, 32:160]), K.sv)
    W(P.s, K.sv, e_csf)
    e_g = OP(P.s, lambda s: s.activation(K.tmp16a[:], SW["cs_fm"][:, 127:128],
                                         AF.Exp), K.ss)
    W(P.t, K.ss, e_g)
    e_tp4 = OP(P.t, lambda t: t.transpose(K.PS[3][0:1, 160:176], K.tmp16a[:],
                                          K.identf[0:16, 0:16]), K.stc)
    W(P.v, K.stc, e_tp4)
    e_gr = OP(P.v, lambda v: v.tensor_copy(K.grow[:], K.PS[3][0:1, 160:176]),
              K.sv)
    ev["grow_v"] = e_gr
    if need_dec:
        e_d = OP(P.s, lambda s: s.activation(
            SW["dec_fm"][:], SW["cs_fm"][:], AF.Exp, scale=-1.0,
            bias=SW["cs_fm"][:, 127:128]), K.ss)
        W(P.t, K.ss, e_d)
        W(P.t, K.sv, e_gr)
        e_tp5 = OP(P.t, lambda t: t.transpose(K.PS[3][:, 176:192],
                                              SW["dec_fm"][:], K.identf[0:16, 0:16]),
                   K.stc)
        W(P.v, K.stc, e_tp5)
        e_dec = OP(P.v, lambda v: v.tensor_copy(SW["dec_tm"][:],
                                                K.PS[3][:, 176:192]), K.sv)
        ev["dec_v"] = e_dec
        ev["ps3_free"] = e_dec
    else:
        ev["ps3_free"] = e_gr
    ev["cs_v"] = e_csv
    ev["ncs_v"] = e_ncs
    ev["dt_v"] = e_dt
    ev["da_v"] = e_da
    return ev


def kv_transposes(K, P, SW, src_xv_ap_fn, src_bv_ap_fn, ev):
    e_x = e_b = 0
    for kv in range(8):
        if ev.get("single_pt"):
            p = 0
        else:
            p = ev.get("kvpar", 0)
            ev["kvpar"] = 1 - p
        W(P.t, K.sv, ev.get(f"pt{p}_free", 0))
        W(P.t, K.ss, ev.get("sil_s", 0))
        e1 = OP(P.t, lambda t, pp=p, a=src_xv_ap_fn(kv): t.transpose(
            K.PT[pp][:, 0:128], a, K.identb[:]), K.stc)
        W(P.v, K.stc, e1)
        e_x = OP(P.v, lambda v, pp=p, kk=kv: v.tensor_copy(
            SW["xv_tm"][:, kk, :], K.PT[pp][:, 0:128]), K.sv)
        ev[f"pt{p}_free"] = e_x
        if ev.get("single_pt"):
            p = 0
        else:
            p = ev.get("kvpar", 0)
            ev["kvpar"] = 1 - p
        W(P.t, K.sv, ev.get(f"pt{p}_free", 0))
        e2 = OP(P.t, lambda t, pp=p, a=src_bv_ap_fn(kv): t.transpose(
            K.PT[pp][:, 0:128], a, K.identb[:]), K.stc)
        W(P.v, K.stc, e2)
        e_b = OP(P.v, lambda v, pp=p, kk=kv: v.tensor_copy(
            SW["b_tm"][:, kk, :], K.PT[pp][:, 0:128]), K.sv)
        ev[f"pt{p}_free"] = e_b
    ev["xvtm_v"] = e_x
    ev["btm_v"] = e_b
    ev["sil_war_t"] = K.stc.n
    return ev


def states_scan_chunk(K, P, SW, ev):
    for h in range(NH):
        kv = h // 2
        sl = 2 * (h & 1)
        W(P.v, K.sv, max(ev["xvtm_v"], ev["dt_v"], ev["dec_v"], ev["btm_v"]))
        e_x = OP(P.v, lambda v, kk=kv, s=sl, hh=h: v.tensor_scalar_mul(
            SW["xb"][:, s, :], SW["xv_tm"][:, kk, :], SW["dt_tm"][:, hh:hh + 1]),
            K.sv)
        e_bd = OP(P.v, lambda v, kk=kv, s=sl, hh=h: v.tensor_scalar_mul(
            SW["xb"][:, s + 1, :], SW["b_tm"][:, kk, :],
            SW["dec_tm"][:, hh:hh + 1]), K.sv)
        W(P.t, K.sv, max(e_x, e_bd, ev.get("ps1_free", 0)))
        W(P.t, K.sv, ev["grow_v"])
        OP(P.t, lambda t, s=sl: t.matmul(K.PS[1][:, 0:128], SW["xb"][:, s + 1, :],
                                         SW["xb"][:, s, :], start=True, stop=True))
        e_mm = OP(P.t, lambda t, hh=h: t.matmul(
            K.PS[1][:, 128:129], K.ones1f[0:1, :], K.grow[0:1, hh:hh + 1],
            start=True, stop=True), K.stc)
        W(P.v, K.stc, e_mm)
        OP(P.v, lambda v, hh=h: v.tensor_scalar_mul(
            K.ut1[:], K.S0[:, hh, :], K.PS[1][:, 128:129]))
        e_s = OP(P.v, lambda v, hh=h: v.tensor_add(
            K.S0[:, hh, :], K.ut1[:], K.PS[1][:, 0:128]), K.sv)
        ev["ps1_free"] = e_s
    return ev


def conv_block(K, P, raw_ap3, sil_dst, g0, nkt, ncols, tmp2, ev):
    for ktl in range(nkt):
        g = g0 + ktl
        W(P.v, K.sv, ev.get("raw_v", 0))
        WE(P.v, ev.get("raw_l"))
        W(P.v, K.ss, ev.get("ctmp_free", 0))
        OP(P.v, lambda v, k=ktl, gg=g: v.tensor_scalar_mul(
            tmp2[:, 0, :], raw_ap3[:, k, 0:ncols], K.cw[:, gg, 0:1]))
        OP(P.v, lambda v, k=ktl, gg=g: v.scalar_tensor_tensor(
            tmp2[:, 1, :], raw_ap3[:, k, 1:1 + ncols], K.cw[:, gg, 1:2],
            tmp2[:, 0, :], ALU.mult, ALU.add))
        OP(P.v, lambda v, k=ktl, gg=g: v.scalar_tensor_tensor(
            tmp2[:, 0, :], raw_ap3[:, k, 2:2 + ncols], K.cw[:, gg, 2:3],
            tmp2[:, 1, :], ALU.mult, ALU.add))
        e_c = OP(P.v, lambda v, k=ktl, gg=g: v.scalar_tensor_tensor(
            tmp2[:, 1, :], raw_ap3[:, k, 3:3 + ncols], K.cw[:, gg, 3:4],
            tmp2[:, 0, :], ALU.mult, ALU.add), K.sv)
        W(P.s, K.sv, e_c)
        W(P.s, K.stc, ev.get("sil_war_t", 0))
        e_s = OP(P.s, lambda s, gg=g: s.activation(
            sil_dst(gg), tmp2[:, 1, :], AF.Silu, bias=K.cb[:, gg:gg + 1]), K.ss)
        ev["ctmp_free"] = e_s
        ev["sil_s"] = e_s
        ev["raw_read_v"] = e_c
    return ev


def phase_pre(K, P, ev):
    consts = [
        (K.identb, K.identb_in), (K.triu, K.triu_in), (K.identf, K.identf_in),
        (K.maskneg, K.maskneg_in), (K.ones1f, K.ones1f_in),
        (K.ones1b, K.ones1b_in), (K.onescol, K.onescol_in),
        (K.ln1w, K.ln1w_in), (K.ln2w, K.ln2w_in), (K.normw, K.normw_in),
        (K.cw, K.cw_in), (K.cb, K.cb_in), (K.dtb, K.dtb_in),
        (K.alog, K.alog_in), (K.db, K.db_in), (K.flagc, K.flag_in),
        (K.tempv, K.temp_in), (K.ones128b, K.ones128b_in),
        (K.constv, K.constv_in),
    ]
    cl = []
    for d, s in consts:
        cl.append(DMA(P.y, K.ql, d[:], s))
    ev["consts_l"] = cl
    WE(P.s, cl)
    e1 = OP(P.s, lambda s: s.activation(K.tmp16a[:], K.alog[:], AF.Exp), K.ss)
    W(P.v, K.ss, e1)
    ev["aneg_v"] = OP(P.v, lambda v: v.tensor_scalar_mul(K.aneg[:], K.tmp16a[:],
                                                         -1.0), K.sv)
    ev["s0_init"] = OP(P.v, lambda v: v.memset(K.S0[:], 0.0), K.sv)

    for hp in (0, 1):
        e_xn, e_rr = ln_tiles(K, P, K.xprev_in, K.A_xno, hp, ev)
        e_rsb = build_rsb(K, P, hp * 1024, 1024, e_rr)
        e_h1 = 0
        for k in range(NKT):
            WE(P.v, ev["consts_l"])
            W(P.v, K.sv, e_rsb)
            W(P.v, K.stc, ev.get("h1o_war_t", 0))
            e_h1 = OP(P.v, lambda v, kk=k, c0=hp * 1024: v.scalar_tensor_tensor(
                K.B_h1o[:, kk, :], K.A_xno[:, kk, :], K.ln1w[:, kk:kk + 1],
                K.RSB[:, c0:c0 + 1024], ALU.mult, ALU.mult), K.sv)
        ev["h1o_v"] = e_h1
        for fq in range(4):
            for ml in range(4):
                g = fq * 4 + ml
                mcol = 2048 + fq * 512 + ml * 128
                slab = K.wsl0 if (ml & 1) == 0 else K.wsl1
                W(P.g, K.stc, ev.get(f"slab{ml & 1}_free", 0))
                lm = DMA(P.g, K.qg, slab[:],
                         K.W_in[:, mcol:mcol + 128].rearrange(
                             "(kt p) m -> p kt m", p=128))
                e_mm_last = 0
                for nn2 in range(2):
                    bank = K.PS[nn2]
                    WE(P.t, lm)
                    W(P.t, K.sv, e_h1)
                    W(P.t, K.sv, ev.get(f"rawb{nn2}_free", 0))
                    for k in range(NKT):
                        e_mm_last = OP(P.t, lambda t, sl=slab, kk=k, b=bank,
                                       c0=nn2 * 512: t.matmul(
                            b[:, 0:512], sl[:, kk, :],
                            K.B_h1o[:, kk, c0:c0 + 512],
                            start=(kk == 0), stop=(kk == NKT - 1)),
                            K.stc if k == NKT - 1 else None)
                    W(P.v, K.stc, e_mm_last)
                    e_ev = OP(P.v, lambda v, m=ml, b=bank, c0=3 + nn2 * 512:
                              v.tensor_copy(K.A_rawo[:, m, c0:c0 + 512],
                                            b[:, 0:512]), K.sv)
                    ev[f"rawb{nn2}_free"] = e_ev
                ev[f"slab{ml & 1}_free"] = e_mm_last
            if hp == 0:
                e_pad = OP(P.v, lambda v: v.memset(K.A_rawo[:, :, 0:3], 0.0),
                           K.sv)
            else:
                W(P.v, K.sv, ev.get("pretail_v", 0))
                e_pad = OP(P.v, lambda v, f=fq: v.tensor_copy(
                    K.A_rawo[:, :, 0:3], K.pretail[:, f * 4:f * 4 + 4, :]),
                    K.sv)
            ev["raw_l"] = None
            ev["raw_v"] = max(e_pad, ev.get(f"rawb0_free", 0),
                              ev.get(f"rawb1_free", 0))
            e_pt = OP(P.v, lambda v, f=fq: v.tensor_copy(
                K.pretail[:, f * 4:f * 4 + 4, :], K.A_rawo[:, :, 1024:1027]),
                K.sv)
            if fq == 3:
                ev["pretail_v"] = e_pt
            ev = conv_block(
                K, P, K.A_rawo,
                lambda gg, c0=hp * 0: K.B_silo[:, gg, :],
                fq * 4, 4, 1024, K.A_ctmp, ev)

        if hp == 1:
            for m in range(16):
                mcol = 4096 + m * 128
                slab = K.wsl0 if (m & 1) == 0 else K.wsl1
                W(P.g, K.stc, ev.get(f"slab{m & 1}_free", 0))
                lm = DMA(P.g, K.qg, slab[:],
                         K.W_in[:, mcol:mcol + 128].rearrange(
                             "(kt p) m -> p kt m", p=128))
                WE(P.t, lm)
                W(P.t, K.sv, ev.get("rawc_free", 0))
                e_mm = 0
                for k in range(NKT):
                    e_mm = OP(P.t, lambda t, sl=slab, kk=k: t.matmul(
                        K.PS[2][:, 0:128], sl[:, kk, :],
                        K.B_h1o[:, kk, 896:1024],
                        start=(kk == 0), stop=(kk == NKT - 1)),
                        K.stc if k == NKT - 1 else None)
                ev[f"slab{m & 1}_free"] = e_mm
                W(P.v, K.stc, e_mm)
                W(P.v, K.ss, ev.get("sil_s", 0))
                e_ev = OP(P.v, lambda v, mm_=m: v.tensor_copy(
                    K.A_rawc[:, mm_, :], K.PS[2][:, 0:128]), K.sv)
                ev["rawc_free"] = e_ev
            ev["rawc_v"] = ev["rawc_free"]

        slab = K.wsl0
        W(P.g, K.stc, ev.get("slab0_free", 0))
        lm = DMA(P.g, K.qg, slab[:, 0:16, 0:16],
                 K.W_in[:, 6144:6160].rearrange("(kt p) m -> p kt m", p=128))
        e_mm = 0
        for nn2 in range(2):
            WE(P.t, lm)
            W(P.t, K.sv, ev.get("dt_ps_free", 0))
            for k in range(NKT):
                e_mm = OP(P.t, lambda t, sl=slab, kk=k, b=nn2: t.matmul(
                    K.PS[4][0:16, 0:512], sl[:, kk, 0:16],
                    K.B_h1o[:, kk, b * 512:b * 512 + 512],
                    start=(kk == 0), stop=(kk == NKT - 1)),
                    K.stc if k == NKT - 1 else None)
            W(P.v, K.stc, e_mm)
            e_dt = OP(P.v, lambda v, b=nn2: v.tensor_copy(
                K.dtrawH[:, b * 512:b * 512 + 512], K.PS[4][0:16, 0:512]),
                K.sv)
            ev["dt_ps_free"] = e_dt
        ev["slab0_free"] = e_mm
        ev["dtraw_v"] = ev["dt_ps_free"]
        ev["h1o_war_t"] = K.stc.n

        for nb2 in range(2):
            c0 = nb2 * 512
            W(P.v, K.sv, ev["aneg_v"])
            e_da = softplus_block(
                K, P, K.dtsp[:, 0:512], K.dtda[:],
                K.dtrawH[:, c0:c0 + 512], 512, (K.sv, ev["dtraw_v"]))
            ev["da_src_v"] = e_da
            for c in range(4):
                t0 = c0 + c * 128
                ev = chunk_prep(K, P, K.SSD_PRE,
                                K.dtda[:, c * 128:(c + 1) * 128],
                                K.dtsp[:, c * 128:(c + 1) * 128], ev, True)
                ev = kv_transposes(
                    K, P, K.SSD_PRE,
                    lambda kv, tt=t0: K.B_silo[:, kv, tt:tt + 128],
                    lambda kv, tt=t0: K.B_silo[:, 8 + kv, tt:tt + 128], ev)
                ev = states_scan_chunk(K, P, K.SSD_PRE, ev)

    W(P.v, K.sv, max(ev["pretail_v"], ev["rawc_v"]))
    WE(P.v, ev["consts_l"])
    OP(P.v, lambda v: v.tensor_scalar_mul(K.tails[:, 0:16, :],
                                          K.pretail[:], K.flagc[:]))
    OP(P.v, lambda v: v.tensor_scalar_mul(
        K.tails[:, 16:32, :], K.A_rawc[:, :, 125:128], K.flagc[:]))
    e_s0 = OP(P.v, lambda v: v.tensor_scalar_mul(K.S0[:], K.S0[:],
                                                 K.flagc[:]), K.sv)
    ev["tails_v"] = e_s0
    ev["s0_v"] = e_s0
    if K.debug and "d_s0" in K.dbg:
        W(P.y, K.sv, e_s0)
        DMA(P.y, K.qs, K.dbg["d_s0"].rearrange("h p n -> p h n"), K.S0[:])
    return ev


def barrier(K, P, ev):
    for lst in (P.t, P.v, P.s, P.g, P.y):
        W(lst, K.sv, K.sv.n)
        W(lst, K.ss, K.ss.n)
        W(lst, K.stc, K.stc.n)


GL = 0.5 - float(np.log(4.0 / 3.0))
GU = 0.5
GCENT = float(1.0 / np.e)


def phase_ph0(K, P, ev):
    barrier(K, P, ev)
    e_xn, e_rr = ln_tiles(K, P, K.x_in, K.A_xn, None, ev)
    ev["xn_v"] = e_xn
    xs = []
    for k in range(NKT):
        W(P.y, K.sv, e_xn)
        xs.append(DMA(P.y, K.qs, K.xn_sp[k * 128:(k + 1) * 128, :],
                      K.A_xn[:, k, :]))
    ev["xnsp_q"] = xs
    e_rsb = build_rsb(K, P, 0, T, e_rr)
    ev["rsb_v"] = e_rsb
    if K.debug:
        W(P.y, K.sv, e_rsb)
        DMA(P.y, K.qs, K.dbg["d_rsb"], K.RSB[:])
        DMA(P.y, K.qs, K.dbg["d_rsrow"], K.rs_row[:])

    W(P.g, K.stc, K.stc.n)
    lw = DMA(P.g, K.qg, K.wsl0[:, :, 0:8],
             K.Wr.rearrange("(kt p) e -> p kt e", p=128))
    WE(P.v, ev["consts_l"])
    e_rc = OP(P.v, lambda v: v.reciprocal(K.rtemp[0:1, 0:1], K.tempv[:]), K.sv)
    W(P.t, K.sv, e_rc)
    e_bb = OP(P.t, lambda t: t.matmul(K.PS[4][0:8, 0:1], K.ones1f[0:1, 0:8],
                                      K.rtemp[0:1, 0:1], start=True, stop=True),
              K.stc)
    W(P.v, K.stc, e_bb)
    e_rt = OP(P.v, lambda v: v.tensor_copy(K.rtemp[:, 0:1], K.PS[4][0:8, 0:1]),
              K.sv)
    e_sig = 0
    for nn in range(NNB):
        WE(P.t, lw)
        W(P.t, K.sv, e_xn)
        W(P.t, K.ss, e_sig)
        W(P.t, K.sv, e_rt)
        e_mm = 0
        for k in range(NKT):
            e_mm = OP(P.t, lambda t, kk=k, c0=nn * 512: t.matmul(
                K.PS[4][0:8, 0:512], K.wsl0[:, kk, 0:8],
                K.A_xn[:, kk, c0:c0 + 512], start=(kk == 0),
                stop=(kk == NKT - 1)), K.stc if k == NKT - 1 else None)
        W(P.s, K.stc, e_mm)
        W(P.s, K.sv, e_rt)
        e_sig = OP(P.s, lambda s, c0=nn * 512: s.activation(
            K.B_inh[:, c0:c0 + 512], K.PS[4][0:8, 0:512], AF.Sigmoid,
            scale=K.rtemp[:, 0:1]), K.ss)
    ev["inh_s"] = e_sig
    if K.debug:
        W(P.y, K.ss, e_sig)
        W(P.y, K.sv, e_rt)
        DMA(P.y, K.qs, K.dbg["d_inh"], K.B_inh[:])
        DMA(P.y, K.qs, K.dbg["d_rt"], K.rtemp[:])

    R = K.B_rt
    e_ones = OP(P.v, lambda v: v.memset(R[:, 8, :], 1.0), K.sv)
    e_tp_prev = 0
    for tt in range(16):
        W(P.t, K.ss, e_sig)
        W(P.t, K.sv, ev.get("ps5_free", 0))
        e_tp = OP(P.t, lambda t, c0=tt * 128: t.transpose(
            K.PS[5][:, 0:8], K.B_inh[:, c0:c0 + 128], K.identf[0:8, 0:8]),
            K.stc)
        W(P.v, K.stc, e_tp)
        OP(P.v, lambda v: v.tensor_copy(R[:, 0, :], K.PS[5][:, 0:8]))
        e_inh = OP(P.v, lambda v: v.engine_nop(), K.sv)
        ev["ps5_free"] = e_inh
        OP(P.v, lambda v: v.tensor_single_scalar(R[:, 1, :], R[:, 0, :], GL,
                                                 ALU.is_ge))
        OP(P.v, lambda v: v.tensor_single_scalar(R[:, 7, :], R[:, 0, :], GU,
                                                 ALU.is_le))
        OP(P.v, lambda v: v.tensor_mul(R[:, 1, :], R[:, 1, :], R[:, 7, :]))
        e_d = OP(P.v, lambda v: v.tensor_single_scalar(
            R[:, 6, :], R[:, 0, :], GCENT, ALU.subtract), K.sv)
        W(P.s, K.sv, e_d)
        OP(P.s, lambda s: s.activation(R[:, 2, :], R[:, 6, :], AF.Abs))
        OP(P.s, lambda s: s.activation(R[:, 3, :], R[:, 2, :], AF.Exp,
                                       scale=-10.0))
        e_e = OP(P.s, lambda s: s.activation(R[:, 4, :], R[:, 2, :], AF.Exp,
                                             scale=-10.0 / 3.0), K.ss)
        W(P.v, K.ss, e_e)
        OP(P.v, lambda v: v.tensor_mul(R[:, 3, :], R[:, 3, :], R[:, 1, :]))
        OP(P.v, lambda v: v.reduce_sum(R[:, 5, 0:1], R[:, 3, :], axis=AX.X))
        OP(P.v, lambda v: v.tensor_single_scalar(R[:, 5, 5:6], R[:, 5, 0:1],
                                                 1e-8, ALU.is_lt))
        e_m1 = OP(P.v, lambda v: v.reduce_max(R[:, 5, 2:3], R[:, 4, :],
                                              axis=AX.X), K.sv)
        W(P.v, K.sv, e_m1)
        OP(P.v, lambda v: v.tensor_single_scalar(R[:, 2, :], R[:, 4, :],
                                                 R[:, 5, 2:3], ALU.is_ge))
        OP(P.v, lambda v: v.scalar_tensor_tensor(R[:, 6, :], R[:, 2, :], -1e30,
                                                 R[:, 4, :], ALU.mult, ALU.add))
        e_m2 = OP(P.v, lambda v: v.reduce_max(R[:, 5, 3:4], R[:, 6, :],
                                              axis=AX.X), K.sv)
        W(P.v, K.sv, e_m2)
        OP(P.v, lambda v: v.tensor_single_scalar(R[:, 7, :], R[:, 6, :],
                                                 R[:, 5, 3:4], ALU.is_ge))
        OP(P.v, lambda v: v.tensor_add(R[:, 7, :], R[:, 7, :], R[:, 2, :]))
        OP(P.v, lambda v: v.tensor_mul(R[:, 7, :], R[:, 7, :], R[:, 4, :]))
        OP(P.v, lambda v: v.reduce_sum(R[:, 5, 4:5], R[:, 7, :], axis=AX.X))
        OP(P.v, lambda v: v.tensor_single_scalar(R[:, 5, 4:5], R[:, 5, 4:5],
                                                 1e-8, ALU.max))
        e_r1 = OP(P.v, lambda v: v.reciprocal(R[:, 5, 6:7], R[:, 5, 4:5]),
                  K.sv)
        W(P.v, K.sv, e_r1)
        OP(P.v, lambda v: v.tensor_single_scalar(R[:, 7, :], R[:, 7, :],
                                                 R[:, 5, 6:7], ALU.mult))
        OP(P.v, lambda v: v.tensor_single_scalar(R[:, 5, 1:2], R[:, 5, 0:1],
                                                 1e-8, ALU.max))
        e_r2 = OP(P.v, lambda v: v.reciprocal(R[:, 5, 7:8], R[:, 5, 1:2]),
                  K.sv)
        W(P.v, K.sv, e_r2)
        OP(P.v, lambda v: v.tensor_single_scalar(R[:, 3, :], R[:, 3, :],
                                                 R[:, 5, 7:8], ALU.mult))
        e_ne = OP(P.v, lambda v: v.tensor_single_scalar(R[:, 2, :], R[:, 8, :],
                                                        R[:, 5, 5:6], ALU.mult),
                  K.sv)
        W(P.v, K.sv, e_ne)
        OP(P.v, lambda v: v.tensor_sub(R[:, 6, :], R[:, 7, :], R[:, 3, :]))
        OP(P.v, lambda v: v.tensor_mul(R[:, 6, :], R[:, 6, :], R[:, 2, :]))
        OP(P.v, lambda v: v.tensor_add(R[:, 3, :], R[:, 3, :], R[:, 6, :]))
        OP(P.v, lambda v: v.reduce_sum(R[:, 5, 0:1], R[:, 3, :], axis=AX.X))
        OP(P.v, lambda v: v.tensor_single_scalar(R[:, 5, 1:2], R[:, 5, 0:1],
                                                 1e-8, ALU.max))
        e_r3 = OP(P.v, lambda v: v.reciprocal(R[:, 5, 6:7], R[:, 5, 1:2]),
                  K.sv)
        W(P.v, K.sv, e_r3)
        OP(P.v, lambda v: v.tensor_single_scalar(R[:, 3, :], R[:, 3, :],
                                                 R[:, 5, 6:7], ALU.mult))
        e_w8 = OP(P.v, lambda v, t0=tt: v.tensor_copy(K.w8tm[:, t0, :],
                                                      R[:, 3, :]), K.sv)
        ev["w8_v"] = e_w8
    if K.debug and "d_w8" in K.dbg:
        W(P.y, K.sv, ev["w8_v"])
        DMA(P.y, K.qs, K.dbg["d_w8"].rearrange("t p e -> p t e"), K.w8tm[:])
    return ev


def phase_ph1(K, P, ev):
    e_dn_prev = K.stc.n
    e_hp_last = 0
    for e in range(NE):
        for tt in range(16):
            W(P.t, K.sv, max(ev.get("ps5_free", 0), ev.get("w8_v", 0)))
            e_tp = OP(P.t, lambda t, t0=tt, ee=e: t.transpose(
                K.PS[5][0:1, 0:128], K.w8tm[:, t0, ee:ee + 1], K.identf[:]),
                K.stc)
            W(P.v, K.stc, e_tp)
            e_wr = OP(P.v, lambda v, t0=tt: v.tensor_copy(
                K.w8row[0:1, t0 * 128:(t0 + 1) * 128], K.PS[5][0:1, 0:128]),
                K.sv)
            ev["ps5_free"] = e_wr
        e_w8b = 0
        for nn in range(NNB):
            W(P.t, K.sv, ev["ps5_free"])
            W(P.t, K.sv, ev.get("w8b_ps_free", 0))
            e_bm = OP(P.t, lambda t, c0=nn * 512: t.matmul(
                K.PS[4][:, 0:512], K.ones1b[0:1, :], K.w8row[0:1, c0:c0 + 512],
                start=True, stop=True), K.stc)
            W(P.v, K.stc, e_bm)
            e_w8b = OP(P.v, lambda v, c0=nn * 512: v.tensor_copy(
                K.w8b[:, c0:c0 + 512], K.PS[4][:, 0:512]), K.sv)
            ev["w8b_ps_free"] = e_w8b
        ev["w8b_v"] = e_w8b

        g_done = {}
        u_done = {}
        for m in range(16):
            mc = m * 128
            gsl = K.wsl0 if (m & 1) == 0 else K.wsl2
            usl = K.wsl1 if (m & 1) == 0 else K.wsl3
            W(P.g, K.stc, g_done.get(m - 2, e_dn_prev))
            gl = DMA(P.g, K.qg, gsl[:],
                     K.Wg[e][:, mc:mc + 128].rearrange("(kt p) m -> p kt m",
                                                       p=128))
            W(P.g, K.stc, u_done.get(m - 2, e_dn_prev))
            ul = DMA(P.g, K.qg, usl[:],
                     K.Wu[e][:, mc:mc + 128].rearrange("(kt p) m -> p kt m",
                                                       p=128))
            for nn in range(NNB):
                c0 = nn * 512
                WE(P.t, gl)
                W(P.t, K.sv, ev["xn_v"])
                W(P.t, K.sv, ev.get(f"hp_free{nn}", 0))
                W(P.t, K.ss, ev.get(f"sg_free{nn}", 0))
                W(P.t, K.sv, ev.get(f"eo_free{nn}", 0))
                e_g = 0
                for k in range(NKT):
                    e_g = OP(P.t, lambda t, sl=gsl, kk=k, b=nn, cc=c0: t.matmul(
                        K.PS[b][:, 0:512], sl[:, kk, :],
                        K.A_xn[:, kk, cc:cc + 512], start=(kk == 0),
                        stop=(kk == NKT - 1)), K.stc if k == NKT - 1 else None)
                g_done[m] = e_g
                WE(P.t, ul)
                W(P.t, K.ss, ev.get(f"sg_free{nn}", 0))
                e_u = 0
                for k in range(NKT):
                    e_u = OP(P.t, lambda t, sl=usl, kk=k, b=4 + nn, cc=c0: t.matmul(
                        K.PS[b][:, 0:512], sl[:, kk, :],
                        K.A_xn[:, kk, cc:cc + 512], start=(kk == 0),
                        stop=(kk == NKT - 1)), K.stc if k == NKT - 1 else None)
                u_done[m] = e_u
                W(P.s, K.stc, e_g)
                W(P.s, K.sv, ev.get(f"sgt_slot{nn & 1}", 0))
                e_sg = OP(P.s, lambda s, b=nn, sl=nn & 1: s.activation(
                    K.sgt[:, sl, :], K.PS[b][:, 0:512], AF.Silu), K.ss)
                ev[f"sg_free{nn}"] = e_sg
                W(P.v, K.ss, e_sg)
                W(P.v, K.stc, e_u)
                W(P.v, K.stc, e_dn_prev)
                e_hp = OP(P.v, lambda v, b=4 + nn, sl=nn & 1, m0=m, cc=c0:
                          v.tensor_tensor(K.B_h1[:, m0, cc:cc + 512],
                                          K.sgt[:, sl, :], K.PS[b][:, 0:512],
                                          ALU.mult), K.sv)
                ev[f"sgt_slot{nn & 1}"] = e_hp
                ev[f"hp_free{nn}"] = e_hp
                e_hp_last = e_hp

        e_dn = 0
        dn_done = {}
        for m in range(16):
            mc = m * 128
            slab = [K.wsl0, K.wsl1, K.wsl2, K.wsl3][m % 4]
            W(P.g, K.stc, [g_done[14], u_done[14], g_done[15], u_done[15]][m % 4])
            W(P.g, K.stc, dn_done.get(m - 4, 0))
            dl = DMA(P.g, K.qg, slab[:],
                     K.Wd[e][:, mc:mc + 128].rearrange("(kt p) m -> p kt m",
                                                       p=128))
            for nn in range(NNB):
                c0 = nn * 512
                WE(P.t, dl)
                W(P.t, K.sv, e_hp_last)
                W(P.t, K.sv, ev.get(f"eo_free{nn}", 0))
                W(P.t, K.ss, ev.get(f"sg_free{nn}", 0))
                e_d = 0
                for k in range(NKT):
                    e_d = OP(P.t, lambda t, sl=slab, kk=k, b=nn, cc=c0:
                             t.matmul(K.PS[b][:, 0:512], sl[:, kk, :],
                                      K.B_h1[:, kk, cc:cc + 512],
                                      start=(kk == 0), stop=(kk == NKT - 1)),
                             K.stc if k == NKT - 1 else None)
                e_dn = e_d
                dn_done[m] = e_d
                W(P.v, K.stc, e_d)
                W(P.v, K.sv, ev["w8b_v"])
                WE(P.v, ev.get("rowb_st"))
                e_eo = OP(P.v, lambda v, b=nn, cc=c0: v.tensor_tensor(
                    K.rowb[:, cc:cc + 512], K.PS[b][:, 0:512],
                    K.w8b[:, cc:cc + 512], ALU.mult), K.sv)
                ev[f"eo_free{nn}"] = e_eo
            W(P.y, K.sv, ev["eo_free3"])
            ev["rowb_st"] = DMA(P.y, K.qs, K.ffn_sp[e][mc:mc + 128, :],
                                K.rowb[:])
        e_dn_prev = e_dn
    ev["ph1_t"] = K.stc.n
    ev["ph1_v"] = K.sv.n
    ev["ffn_st_all"] = SNAP(K.qs)
    return ev


def phase_ph2a(K, P, ev):
    e_h1 = 0
    for k in range(NKT):
        W(P.v, K.stc, ev["ph1_t"])
        W(P.v, K.sv, ev["rsb_v"])
        e_h1 = OP(P.v, lambda v, kk=k: v.scalar_tensor_tensor(
            K.B_h1[:, kk, :], K.A_xn[:, kk, :], K.ln1w[:, kk:kk + 1],
            K.RSB[:], ALU.mult, ALU.mult), K.sv)
    ev["h1_v"] = e_h1

    e_slab = {0: ev["ph1_t"], 1: ev["ph1_t"]}
    for m in range(49):
        mc = m * 128
        mw = 16 if m == 48 else 128
        pr = m & 1
        slab = K.wsl0 if pr == 0 else K.wsl1
        W(P.g, K.stc, e_slab[pr])
        lm = DMA(P.g, K.qg, slab[:, :, 0:mw],
                 K.W_in[:, mc:mc + mw].rearrange("(kt p) m -> p kt m", p=128))
        e_mm = 0
        for nn in range(NNB):
            c0 = nn * 512
            WE(P.t, lm)
            W(P.t, K.sv, e_h1)
            W(P.t, K.sv, ev.get(f"zx_bank{nn}", 0))
            for k in range(NKT):
                e_mm = OP(P.t, lambda t, sl=slab, kk=k, b=nn, cc=c0, w=mw:
                          t.matmul(K.PS[b][0:w, 0:512], sl[:, kk, 0:w],
                                   K.B_h1[:, kk, cc:cc + 512],
                                   start=(kk == 0), stop=(kk == NKT - 1)),
                          K.stc if k == NKT - 1 else None)
            W(P.v, K.stc, e_mm)
            if m == 48:
                WE(P.v, ev.get("dt16_st"))
                e_ev = OP(P.v, lambda v, b=nn: v.tensor_copy(
                    K.dt16[:, 0:512], K.PS[b][0:16, 0:512]), K.sv)
                W(P.y, K.sv, e_ev)
                ev["dt16_st"] = DMA(P.y, K.qs,
                                    K.zx_sp[6144:6160, c0:c0 + 512],
                                    K.dt16[:, 0:512])
                ev[f"zx_bank{nn}"] = e_ev
            else:
                WE(P.v, ev.get("rowb_st"))
                e_ev = OP(P.v, lambda v, b=nn, cc=c0: v.tensor_copy(
                    K.rowb[:, cc:cc + 512], K.PS[b][:, 0:512]), K.sv)
                ev[f"zx_bank{nn}"] = e_ev
                ev["rowb_v"] = e_ev
        e_slab[pr] = e_mm
        if m < 48:
            W(P.y, K.sv, ev["rowb_v"])
            ev["rowb_st"] = DMA(P.y, K.qs, K.zx_sp[mc:mc + 128, :], K.rowb[:])
    ev["ph2a_t"] = K.stc.n
    ev["ph2a_st"] = SNAP(K.qs)
    return ev


def presum_unit(K, P, ev, mp):
    m, n = mp // 4, mp % 4
    mc, cc = m * 128, n * 512
    lf = []
    for e in range(NE):
        W(P.y, K.sv, ev.get("pf_free", 0))
        WE(P.y, ev["ffn_st_all"])
        lf.append(DMA(P.y, K.ql, K.pf_t[:, e, :],
                      K.ffn_sp[e][mc:mc + 128, cc:cc + 512]))
    WE(P.v, lf)
    WE(P.v, ev.get("psum_st"))
    OP(P.v, lambda v: v.tensor_add(K.pacc[:, 0, :], K.pf_t[:, 0, :],
                                   K.pf_t[:, 1, :]), K.sv)
    for e in range(2, NE):
        OP(P.v, lambda v, ee=e: v.tensor_add(
            K.pacc[:, (ee + 1) % 2, :], K.pacc[:, ee % 2, :],
            K.pf_t[:, ee, :]), K.sv)
    e_po = OP(P.v, lambda v: v.tensor_copy(K.pout[:], K.pacc[:, 0, :]),
              K.sv)
    ev["pf_free"] = e_po
    W(P.y, K.sv, e_po)
    ev["psum_st"] = DMA(P.y, K.qs, K.ffnsum_sp[mc:mc + 128, cc:cc + 512],
                        K.pout[:])



def phase_ph2b(K, P, ev):
    SW = K.SSD_MAIN
    W(P.g, K.sv, ev["s0_v"])
    W(P.g, K.stc, ev["ph2a_t"])
    e_sbf0 = OP(P.g, lambda e: e.tensor_copy(SW["sbf"][:], K.S0[:]))
    for h in range(NH):
        ev[f"sbf_v{h}"] = e_sbf0
    ev["prep_par"] = 0
    ev["single_pt"] = True
    e_zs = 0

    WE(P.y, ev.get("rowb_st"))
    ld_m4 = DMA(P.y, K.ql, K.maskneg4[:], K.maskneg4_in)
    ld_t4 = DMA(P.y, K.ql, K.triu4[:], K.triu4_in)
    ld_tn = DMA(P.y, K.ql, K.triuN[:], K.triuN_in)
    ev["c2b_v"] = [ld_m4, ld_t4]
    ev["c2b_t"] = [ld_tn]
    if TRUNC2B == "consts":
        return ev

    for nb in range(NNB):
        c0 = nb * 512
        rl = []
        for fh in range(2):
            W(P.y, K.sv, ev.get("raw_read_v", 0))
            W(P.y, K.ss, ev.get("araw_s", 0))
            W(P.y, K.stc, ev.get("araw_t", 0))
            W(P.y, K.sv, ev.get("araw_v", 0))
            r0 = 2048 + fh * 2048
            if nb == 0:
                ld = DMA(P.y, K.ql, K.A_raw[:, fh * 16:(fh + 1) * 16, 3:515],
                         K.zx_sp[r0:r0 + 2048, 0:512].rearrange(
                             "(kt p) t -> p kt t", p=128))
                rl.append(ld)
                W(P.v, K.sv, ev["tails_v"])
                WE(P.v, ld)
                e_tl = OP(P.v, lambda v, f=fh: v.tensor_copy(
                    K.A_raw[:, f * 16:(f + 1) * 16, 0:3],
                    K.tails[:, f * 16:(f + 1) * 16, :]), K.sv)
                ev["raw_v"] = e_tl
            else:
                ld = DMA(P.y, K.ql, K.A_raw[:, fh * 16:(fh + 1) * 16, :],
                         K.zx_sp[r0:r0 + 2048, c0 - 3:c0 + 512].rearrange(
                             "(kt p) t -> p kt t", p=128))
                rl.append(ld)
                ev["raw_v"] = 0
        ev["raw_l"] = rl
        W(P.y, K.ss, e_zs)
        zl = DMA(P.y, K.ql, K.B_zr[:],
                 K.zx_sp[0:2048, c0:c0 + 512].rearrange("(kt p) t -> p kt t",
                                                        p=128))
        W(P.y, K.sv, ev.get("dtraw_free", 0))
        dtl = DMA(P.y, K.ql, K.dtrawH[:, 0:512],
                  K.zx_sp[6144:6160, c0:c0 + 512])
        W(P.v, K.sv, ev["aneg_v"])
        e_da = softplus_block(K, P, K.dtsp[:, 0:512], K.dtda[:],
                              K.dtrawH[:, 0:512], 512, dtl)
        ev["da_src_v"] = e_da
        ev["dtraw_free"] = e_da

        e_cread = 0
        for g in range(32):
            gp = (g & 1) * 2
            W(P.v, K.sv, ev.get("raw_v", 0))
            WE(P.v, ev["raw_l"])
            W(P.v, K.ss, ev.get("ctmp_s", 0))
            W(P.v, K.stc, ev.get("sil_war_t", 0))
            OP(P.v, lambda v, gg=g, b=gp: v.tensor_scalar_mul(
                K.gctw[:, b, :], K.A_raw[:, gg, 0:512],
                K.cw[:, gg, 0:1]), K.sv)
            OP(P.v, lambda v, gg=g, b=gp: v.scalar_tensor_tensor(
                K.gctw[:, b + 1, :], K.A_raw[:, gg, 1:513],
                K.cw[:, gg, 1:2], K.gctw[:, b, :], ALU.mult, ALU.add), K.sv)
            OP(P.v, lambda v, gg=g, b=gp: v.scalar_tensor_tensor(
                K.gctw[:, b, :], K.A_raw[:, gg, 2:514],
                K.cw[:, gg, 2:3], K.gctw[:, b + 1, :], ALU.mult, ALU.add),
                K.sv)
            e_cread = OP(P.v, lambda v, gg=g, b=gp: v.scalar_tensor_tensor(
                K.gctw[:, b + 1, :], K.A_raw[:, gg, 3:515],
                K.cw[:, gg, 3:4], K.gctw[:, b, :], ALU.mult, ALU.add), K.sv)
            W(P.s, K.sv, e_cread)
            W(P.s, K.stc, ev.get("sil_war_t", 0))
            e_sil = OP(P.s, lambda s, gg=g, b=gp: s.activation(
                K.B_silnb[:, gg, :], K.gctw[:, b + 1, :], AF.Silu,
                bias=K.cb[:, gg:gg + 1]), K.ss)
            ev["ctmp_s"] = e_sil
        ev["raw_read_v"] = e_cread
        ev["sil_s"] = e_sil
        if TRUNC2B == f"{nb}.conv":
            return ev
        for c in range(4):
            tc = c * 128
            pc = c & 1
            WE(P.s, [zl])
            W(P.s, K.sv, ev.get(f"sizpar{pc}_v", 0))
            e_zs = OP(P.s, lambda s, p_=pc, t1=tc: s.activation(
                K.B_siz[:, p_], K.B_zr[:, :, t1:t1 + 128], AF.Silu), K.ss)
            W(P.t, K.sv, max(ev.get("gt_free0", 0), ev.get("gt_free1", 0)))
            ev = chunk_prep(K, P, SW, K.dtda[:, tc:tc + 128],
                            K.dtsp[:, tc:tc + 128], ev, False)
            ev = kv_transposes(K, P, SW,
                               lambda kv, t1=tc: K.B_silnb[:, kv, t1:t1 + 128],
                               lambda kv, t1=tc: K.B_silnb[:, 8 + kv, t1:t1 + 128],
                               ev)
            if TRUNC2B == f"{nb}.{c}.kv":
                return ev
            stA = {}
            ev["gt_free0"] = max(ev.get("gt_free0", 0), K.sv.n)
            ev["gt_free1"] = max(ev.get("gt_free1", 0), K.sv.n)
            ev["zs_e"] = e_zs

            def front(g):
                pg = g & 1
                GTB = K.PS[3] if pg == 0 else K.PS[6]
                W(P.v, K.stc, ev.get("db_free", 0))
                if not ev.get("c2b_done"):
                    WE(P.v, ev.get("dt16_st"))
                    WE(P.v, ev["c2b_v"])
                    ev["c2b_done"] = True
                for k in range(4):
                    OP(P.v, lambda v, g_=g, k_=k: v.tensor_scalar_mul(
                        K.da_bc4[:, k_ * 128:(k_ + 1) * 128], K.ones128b[:],
                        SW["da_tmF"][:, 4 * g_ + k_:4 * g_ + k_ + 1]), K.sv)
                e_dbt = OP(P.v, lambda v: v.tensor_tensor(
                    K.dbt4[:], K.da_bc4[:], K.triu4[:], ALU.mult), K.sv)
                W(P.t, K.sv, e_dbt)
                W(P.t, K.ss, ev.get("ecA_free", 0))
                e_A = OP(P.t, lambda t: t.matmul(
                    K.PS[0][:, 0:512], K.ones128b[:], K.dbt4[:],
                    start=True, stop=True), K.stc)
                W(P.t, K.sv, ev.get("B_free", 0))
                if ev.get("c2b_t"):
                    WE(P.t, ev["c2b_t"])
                    ev["c2b_t"] = None
                OP(P.t, lambda t: t.matmul(
                    K.PS[4][:, 0:512], K.ones128b[:], K.dbt4[:],
                    start=True, stop=False))
                e_B = OP(P.t, lambda t: t.matmul(
                    K.PS[4][:, 0:512], K.triuN[:], K.da_bc4[:],
                    start=False, stop=True), K.stc)
                ev["db_free"] = e_B
                W(P.s, K.stc, e_A)
                W(P.s, K.sv, ev.get(f"ec_free{pg}", 0))
                e_ec4 = OP(P.s, lambda s, p_=pg: s.activation(
                    K.ec4[:, p_, :], K.PS[0][:, 0:512], AF.Exp), K.ss)
                ev["ecA_free"] = e_ec4
                W(P.v, K.stc, e_B)
                W(P.v, K.ss, ev.get("ltp4_free", 0))
                e_lp4 = OP(P.v, lambda v: v.tensor_tensor(
                    K.ltp4[:], K.PS[4][:, 0:512], K.maskneg4[:], ALU.add), K.sv)
                ev["B_free"] = e_lp4
                W(P.s, K.sv, max(e_lp4, ev.get(f"lt4_free{pg}", 0)))
                e_lt4 = OP(P.s, lambda s, p_=pg: s.activation(
                    K.lt4[:, p_, :], K.ltp4[:], AF.Exp), K.ss)
                ev["ltp4_free"] = e_lt4
                ev["lt4_s"] = e_lt4
                W(P.t, K.sv, ev.get(f"gt_free{pg}", 0))
                W(P.t, K.ss, ev["sil_s"])
                e_gt = 0
                for k in range(4):
                    e_gt = OP(P.t, lambda t, k_=k, h_=4 * g + k, t1=tc, b=GTB:
                        t.matmul(b[:, k_ * 128:(k_ + 1) * 128],
                                 K.B_silnb[:, 8 + h_ // 2, t1:t1 + 128],
                                 K.B_silnb[:, 16 + h_, t1:t1 + 128],
                                 start=True, stop=True), K.stc)
                stA[g] = dict(lt=e_lt4, ec=e_ec4, gt=e_gt)

            def back(g):
                pg = g & 1
                GTB = K.PS[3] if pg == 0 else K.PS[6]
                W(P.v, K.stc, max(stA[g]["gt"], ev.get("yb_free", 0)))
                W(P.v, K.ss, stA[g]["lt"])
                e_mt4 = OP(P.v, lambda v, b=GTB, p_=pg: v.tensor_tensor(
                    K.mt4[:], b[:, 0:512], K.lt4[:, p_, :], ALU.mult), K.sv)
                ev[f"gt_free{pg}"] = e_mt4
                W(P.v, K.ss, stA[g]["ec"])
                e_cp = 0
                for k in range(4):
                    e_cp = OP(P.v, lambda v, k_=k, h_=4 * g + k, t1=tc, p_=pg:
                        v.tensor_tensor(
                            K.cp4[:, k_ * 128:(k_ + 1) * 128],
                            K.B_silnb[:, 16 + h_, t1:t1 + 128],
                            K.ec4[:, p_, k_ * 128:(k_ + 1) * 128], ALU.mult),
                        K.sv)
                ev[f"ec_free{pg}"] = e_cp
                W(P.t, K.sv, max(ev.get("ps1_free", 0), ev["grow_v"]))
                OP(P.t, lambda t, g_=g: t.matmul(
                    K.PS[1][:, 128:132], K.ones1f[0:1, :],
                    K.grow[0:1, 4 * g_:4 * g_ + 4], start=True, stop=True))
                W(P.t, K.sg, ev[f"sbf_v{g}"])
                e_y = {}
                for k in range(4):
                    h = 4 * g + k
                    hp = k & 1
                    kv = h // 2
                    W(P.v, K.stc, ev.get(f"cpx_war{hp}", 0))
                    e_x = OP(P.v, lambda v, h_=h, s=2 * hp, k_=kv:
                        v.tensor_scalar_mul(SW["xb"][:, s, :],
                                            SW["xv_tm"][:, k_, :],
                                            SW["dt_tm"][:, h_:h_ + 1]), K.sv)
                    e_bd = OP(P.v, lambda v, s=2 * hp + 1, k_=kv, p_=pg, kk=k:
                        v.tensor_scalar_mul(
                            SW["xb"][:, s, :], SW["b_tm"][:, k_, :],
                            K.lt4[:, p_, kk * 128 + 127:kk * 128 + 128]), K.sv)
                    YB = K.PS[2] if hp == 0 else K.PS[7]
                    W(P.t, K.sv, max(e_mt4, e_cp, e_x,
                                     ev.get(f"y_free{hp}", 0)))
                    OP(P.t, lambda t, s=2 * hp, kk=k, b=YB: t.matmul(
                        b[:, 0:128], SW["xb"][:, s, :],
                        K.mt4[:, kk * 128:(kk + 1) * 128], start=True,
                        stop=False))
                    e_yk = OP(P.t, lambda t, h_=h, kk=k, b=YB: t.matmul(
                        b[:, 0:128], SW["sbf"][:, h_, :],
                        K.cp4[:, kk * 128:(kk + 1) * 128], start=False,
                        stop=True), K.stc)
                    e_y[k] = e_yk
                    W(P.v, K.stc, e_yk)
                    W(P.v, K.ss, ev["zs_e"])
                    OP(P.v, lambda v, h_=h, kk=k, t1=tc, b=YB:
                        v.scalar_tensor_tensor(
                            K.ut4[:, kk, :], K.B_silnb[:, h_ // 2, t1:t1 + 128],
                            K.db[:, h_:h_ + 1], b[:, 0:128], ALU.mult,
                            ALU.add), K.sv)
                    ev[f"y_free{hp}"] = K.sv.n
                    W(P.t, K.sv, max(e_bd, ev.get("ps1_free", 0)))
                    e_smm = OP(P.t, lambda t, s=2 * hp: t.matmul(
                        K.PS[1][:, 0:128], SW["xb"][:, s + 1, :],
                        SW["xb"][:, s, :], start=True, stop=True), K.stc)
                    W(P.v, K.stc, e_smm)
                    OP(P.v, lambda v, h_=h, kk=k: v.tensor_scalar_mul(
                        K.ut1[:], K.S0[:, h_, :], K.PS[1][:, 128 + kk:129 + kk]))
                    e_sup = OP(P.v, lambda v, h_=h: v.tensor_add(
                        K.S0[:, h_, :], K.ut1[:], K.PS[1][:, 0:128]), K.sv)
                    ev["ps1_free"] = e_sup
                    ev[f"cpx_war{hp}"] = K.stc.n
                ev["yb_free"] = K.stc.n
                ev[f"lt4_free{pg}"] = K.sv.n
                W(P.g, K.stc, e_y[3])
                W(P.g, K.sv, ev["ps1_free"])
                ev[f"sbf_v{g}"] = OP(P.g, lambda e, g_=g: e.tensor_copy(
                    SW["sbf"][:, 4 * g_:4 * g_ + 4, :],
                    K.S0[:, 4 * g_:4 * g_ + 4, :]))
                e_u4 = OP(P.v, lambda v, g_=g, p_=pc: v.tensor_tensor(
                    K.u_t[:, 4 * g_:4 * g_ + 4, :], K.ut4[:],
                    K.B_siz[:, p_, 4 * g_:4 * g_ + 4, :], ALU.mult), K.sv)
                ev[f"sizpar{pc}_v"] = e_u4
                W(P.s, K.sv, e_u4)
                W(P.s, K.stc, ev.get(f"sq_free{pg}", 0))
                e_sq4 = OP(P.s, lambda s, g_=g, p_=pg: s.activation(
                    K.sgt4a[:, p_], K.u_t[:, 4 * g_:4 * g_ + 4, :],
                    AF.Square), K.ss)
                W(P.t, K.ss, e_sq4)
                if g == 0:
                    W(P.t, K.sv, ev.get("rs_read", 0))
                e_rs = OP(P.t, lambda t, g_=g, p_=pg: t.matmul(
                    K.PS[5][0:1, 0:512], K.onescol[:], K.sgt4a[:, p_],
                    start=(g_ == 0), stop=(g_ == 3)), K.stc)
                ev[f"sq_free{pg}"] = e_rs
                ev["rs_last"] = e_rs

            for tg2, fn2 in (("f0", lambda: front(0)), ("f1", lambda: front(1)),
                             ("b0", lambda: back(0)), ("f2", lambda: front(2)),
                             ("b1", lambda: back(1)), ("f3", lambda: front(3)),
                             ("b2", lambda: back(2)), ("b3", lambda: back(3))):
                fn2()
                if TRUNC2B == f"{nb}.{c}.{tg2}":
                    return ev
            W(P.v, K.stc, ev["rs_last"])
            OP(P.v, lambda v: v.tensor_copy(K.rsyrow[:], K.PS[5][0:1, 0:128]))
            OP(P.v, lambda v: v.tensor_tensor(
                K.rsytmp[:], K.rsyrow[:], K.PS[5][0:1, 128:256], ALU.add))
            OP(P.v, lambda v: v.tensor_tensor(
                K.rsyrow2[:], K.rsytmp[:], K.PS[5][0:1, 256:384], ALU.add))
            e_rsv = OP(P.v, lambda v: v.tensor_tensor(
                K.rsyrow[:], K.rsyrow2[:], K.PS[5][0:1, 384:512], ALU.add),
                K.sv)
            ev["rs_read"] = e_rsv
            W(P.s, K.sv, e_rsv)
            OP(P.s, lambda s: s.activation(K.rsytmp[:], K.rsyrow[:],
                                           AF.Ln, scale=1.0 / DM, bias=EPS))
            e_rsy = OP(P.s, lambda s: s.activation(K.rsyrow2[:], K.rsytmp[:],
                                                   AF.Exp, scale=-0.5), K.ss)
            W(P.t, K.ss, e_rsy)
            W(P.t, K.sv, max(ev.get("y_free0", 0), ev.get("y_free1", 0)))
            e_rb = OP(P.t, lambda t: t.matmul(
                K.PS[2][:, 256:384], K.ones1f[0:1, :], K.rsyrow2[0:1, :],
                start=True, stop=True), K.stc)
            for h in range(NH):
                W(P.v, K.stc, e_rb)
                WE(P.v, ev.get("yub_st"))
                e_yu_last = OP(P.v, lambda v, hh=h, t1=tc: v.scalar_tensor_tensor(
                    K.A_yub[:, hh, t1:t1 + 128], K.u_t[:, hh, :],
                    K.normw[:, hh:hh + 1], K.PS[2][:, 256:384],
                    ALU.mult, ALU.mult), K.sv)
            ev["y_free0"] = e_yu_last
            ev["sil_war_t"] = K.stc.n
            if TRUNC2B == f"{nb}.{c}.yu":
                return ev
        ev["araw_s"] = K.ss.n
        ev["araw_t"] = K.stc.n
        ev["araw_v"] = K.sv.n
        W(P.y, K.sv, e_yu_last)
        ysts = []
        for half in range(2):
            r0 = half * 1024
            ysts.append(DMA(P.y, K.qs,
                            K.yu_sp[r0:r0 + 1024, c0:c0 + 512].rearrange(
                                "(kt p) t -> p kt t", p=128),
                            K.A_yub[:, half * 8:(half + 1) * 8, :]))
        ev["yub_st"] = ysts
    ev["ph2b_st"] = SNAP(K.qs)
    ev["ph2b_t"] = K.stc.n
    return ev


def phase_ph2c(K, P, ev):
    W(P.v, K.stc, ev["ph2b_t"])
    e_clr = OP(P.v, lambda v: v.memset(K.rs2acc[:], 0.0), K.sv)
    e_yl = {}
    e_slab = {0: ev["ph2b_t"], 1: ev["ph2b_t"]}
    e_sq_last = 0
    for nn in range(NNB):
        c0 = nn * 512
        pn = nn & 1
        W(P.y, K.stc, ev["ph2b_t"] if nn < 2 else e_yl.get(nn - 2, 0))
        WE(P.y, ev["ph2b_st"])
        ly = DMA(P.y, K.ql, K.A_yur[:, pn],
                 K.yu_sp[:, c0:c0 + 512].rearrange("(kt p) t -> p kt t", p=128))
        e_mm_all = 0
        for m in range(16):
            mc = m * 128
            pm = m & 1
            slab = K.wsl0 if pm == 0 else K.wsl1
            W(P.g, K.stc, e_slab[pm])
            lm = DMA(P.g, K.qg, slab[:],
                     K.W_out_w[:, mc:mc + 128].rearrange("(kt p) m -> p kt m",
                                                         p=128))
            W(P.y, K.sv, ev.get(f"xnt_free{pm}", 0))
            WE(P.y, ev["xnsp_q"][mc // 128])
            lx = DMA(P.y, K.ql, K.xnt[:, pm],
                     K.xn_sp[mc:mc + 128, c0:c0 + 512])
            WE(P.t, lm)
            WE(P.t, ly)
            W(P.t, K.sv, ev.get(f"am_bank{pm}", 0))
            e_mm = 0
            for k in range(NKT):
                e_mm = OP(P.t, lambda t, sl=slab, kk=k, b=pm, p_=pn: t.matmul(
                    K.PS[b][:, 0:512], sl[:, kk, :], K.A_yur[:, p_, kk, :],
                    start=(kk == 0), stop=(kk == NKT - 1)),
                    K.stc if k == NKT - 1 else None)
            e_slab[pm] = e_mm
            e_mm_all = e_mm
            W(P.v, K.stc, e_mm)
            WE(P.v, lx)
            WE(P.v, ev.get(f"am_st{pm}"))
            e_am = OP(P.v, lambda v, b=pm: v.tensor_tensor(
                K.am_t[:, b, :], K.PS[b][:, 0:512], K.xnt[:, b],
                ALU.add), K.sv)
            ev[f"am_bank{pm}"] = e_am
            ev[f"xnt_free{pm}"] = e_am
            W(P.y, K.sv, e_am)
            ev[f"am_st{pm}"] = DMA(P.y, K.qs,
                                   K.am_sp[mc:mc + 128, c0:c0 + 512],
                                   K.am_t[:, pm])
            W(P.s, K.sv, e_am)
            W(P.s, K.stc, ev.get("sq_slab_t", 0))
            e_sq = OP(P.s, lambda s, b=pm: s.activation(
                K.sgt[:, b, :], K.am_t[:, b, :], AF.Square), K.ss)
            e_sq_last = e_sq
            W(P.t, K.ss, e_sq)
            W(P.t, K.sv, e_clr)
            e_rg = OP(P.t, lambda t, b=pm, first=(m == 0), last=(m == 15):
                      t.matmul(K.PS[2][0:1, 0:512], K.onescol[:],
                               K.sgt[:, b, :], start=first, stop=last),
                      K.stc if m == 15 else None)
            if m == 15:
                W(P.v, K.stc, e_rg)
                e_r2 = OP(P.v, lambda v, cc=c0: v.tensor_copy(
                    K.rs2acc[0:1, cc:cc + 512], K.PS[2][0:1, 0:512]), K.sv)
                ev["rs2_v"] = e_r2
            ev["sq_slab_t"] = K.stc.n
            presum_unit(K, P, ev, nn * 16 + m)
        e_yl[nn] = e_mm_all
    W(P.s, K.sv, ev["rs2_v"])
    W(P.s, K.sg, K.sg.n)
    OP(P.s, lambda s: s.activation(K.outtm[0:1, :], K.rs2acc[:], AF.Ln,
                                   scale=1.0 / DM, bias=EPS))
    e_r2b = OP(P.s, lambda s: s.activation(K.rs2acc[:], K.outtm[0:1, :],
                                           AF.Exp, scale=-0.5), K.ss)
    if K.debug and "d_rs2" in K.dbg:
        W(P.y, K.ss, e_r2b)
        DMA(P.y, K.qs, K.dbg["d_rs2"], K.rs2acc[:])
    e_rsb2 = 0
    for nn in range(NNB):
        W(P.t, K.ss, e_r2b)
        W(P.t, K.sv, ev["h1_v"])
        W(P.t, K.sv, ev.get("rsb2_ps", 0))
        e_bm = OP(P.t, lambda t, cc=nn * 512: t.matmul(
            K.PS[4][:, 0:512], K.ones1f[0:1, :], K.rs2acc[0:1, cc:cc + 512],
            start=True, stop=True), K.stc)
        W(P.v, K.stc, e_bm)
        e_rsb2 = OP(P.v, lambda v, cc=nn * 512: v.tensor_copy(
            K.RSB[:, cc:cc + 512], K.PS[4][:, 0:512]), K.sv)
        ev["rsb2_ps"] = e_rsb2
    ev["rsb2_v"] = e_rsb2
    ev["ph2c_t"] = K.stc.n
    ev["ph2c_ss"] = e_sq_last
    ev["am_st_all"] = SNAP(K.qs)
    ev["ffnsum_st_all"] = SNAP(K.qs)
    return ev


def phase_ph3a(K, P, ev):
    e_h2 = 0
    for k in range(NKT):
        mc = k * 128
        WE(P.y, ev["am_st_all"])
        W(P.y, K.sv, ev.get("rowb3_v", 0))
        W(P.y, K.stc, ev["ph2c_t"])
        lr = DMA(P.y, K.ql, K.rowb[:], K.am_sp[mc:mc + 128, :])
        WE(P.v, lr)
        W(P.v, K.sv, ev["rsb2_v"])
        e_h2 = OP(P.v, lambda v, kk=k: v.scalar_tensor_tensor(
            K.A_h2[:, kk, :], K.rowb[:], K.ln2w[:, kk:kk + 1], K.RSB[:],
            ALU.mult, ALU.mult), K.sv)
        ev["rowb3_v"] = e_h2
    ev["h2_v"] = e_h2

    e_slab = {0: ev["ph2c_t"], 1: ev["ph2c_t"]}
    for m in range(64):
        mc = m * 128
        W(P.g, K.stc, e_slab[0])
        l1 = DMA(P.g, K.qg, K.wsl0[:],
                 K.w1[:, mc:mc + 128].rearrange("(kt p) m -> p kt m", p=128))
        W(P.g, K.stc, e_slab[1])
        l3 = DMA(P.g, K.qg, K.wsl1[:],
                 K.w3[:, mc:mc + 128].rearrange("(kt p) m -> p kt m", p=128))
        for nn in range(NNB):
            c0 = nn * 512
            WE(P.t, l1)
            W(P.t, K.sv, e_h2)
            W(P.t, K.ss, ev.get(f"g_bank{nn}", 0))
            e_g = 0
            for k in range(NKT):
                e_g = OP(P.t, lambda t, kk=k, b=nn, cc=c0: t.matmul(
                    K.PS[b][:, 0:512], K.wsl0[:, kk, :],
                    K.A_h2[:, kk, cc:cc + 512], start=(kk == 0),
                    stop=(kk == NKT - 1)), K.stc if k == NKT - 1 else None)
            e_slab[0] = e_g
            WE(P.t, l3)
            W(P.t, K.sv, ev.get(f"u_bank{nn}", 0))
            e_u = 0
            for k in range(NKT):
                e_u = OP(P.t, lambda t, kk=k, b=4 + nn, cc=c0: t.matmul(
                    K.PS[b][:, 0:512], K.wsl1[:, kk, :],
                    K.A_h2[:, kk, cc:cc + 512], start=(kk == 0),
                    stop=(kk == NKT - 1)), K.stc if k == NKT - 1 else None)
            e_slab[1] = e_u
            W(P.s, K.stc, e_g)
            W(P.s, K.sv, ev.get(f"sgt3_{nn & 1}", 0))
            e_sg = OP(P.s, lambda s, b=nn, sl=nn & 1: s.activation(
                K.sgt[:, sl, :], K.PS[b][:, 0:512], AF.Silu), K.ss)
            ev[f"g_bank{nn}"] = e_sg
            W(P.v, K.ss, e_sg)
            W(P.v, K.stc, e_u)
            WE(P.v, ev.get("hm_st"))
            e_hm = OP(P.v, lambda v, b=4 + nn, sl=nn & 1, cc=c0:
                      v.tensor_tensor(K.rowb[:, cc:cc + 512], K.sgt[:, sl, :],
                                      K.PS[b][:, 0:512], ALU.mult), K.sv)
            ev[f"u_bank{nn}"] = e_hm
            ev[f"sgt3_{nn & 1}"] = e_hm
            ev["hm_v"] = e_hm
        W(P.y, K.sv, ev["hm_v"])
        ev["hm_st"] = DMA(P.y, K.qs, K.hm_sp[mc:mc + 128, :], K.rowb[:])
    ev["ph3a_t"] = K.stc.n
    ev["hm_st_all"] = SNAP(K.qs)
    return ev


def phase_ph3b(K, P, ev):
    e_q_t = ev["ph3a_t"]
    bank_free = {ml: K.sv.n for ml in range(8)}
    for q in range(NNB):
        c0 = q * 512
        lh = []
        for j in range(8):
            W(P.y, K.stc, e_q_t)
            WE(P.y, ev["hm_st_all"])
            lh.append(DMA(P.y, K.ql, K.A_hm[:, j * 8:(j + 1) * 8, :],
                          K.hm_sp[j * 1024:(j + 1) * 1024,
                                  c0:c0 + 512].rearrange(
                              "(kt p) t -> p kt t", p=128)))
        e_mg_prev = e_q_t
        for mg in range(2):
            ring = {}
            for k in range(64):
                slot = k % 4
                W(P.g, K.stc, ring.get(k - 4, e_mg_prev))
                lw = DMA(P.g, K.qg, K.w2r[:, slot, :],
                         K.w2[k * 128:(k + 1) * 128,
                              mg * 1024:(mg + 1) * 1024])
                WE(P.t, lw)
                if k == 0:
                    WE(P.t, lh)
                    for ml in range(8):
                        W(P.t, K.sv, bank_free.get(ml, 0))
                e_k = 0
                for ml in range(8):
                    e_k = OP(P.t, lambda t, s=slot, m_=ml, kk=k: t.matmul(
                        K.PS[m_][:, 0:512], K.w2r[:, s, m_ * 128:(m_ + 1) * 128],
                        K.A_hm[:, kk, :], start=(kk == 0), stop=(kk == 63)),
                        K.stc if ml == 7 else None)
                ring[k] = e_k
            for ml in range(8):
                m = mg * 8 + ml
                mc = m * 128
                pm = ml & 1
                W(P.y, K.sv, ev.get(f"cmb_free{pm}", 0))
                WE(P.y, ev["ffnsum_st_all"])
                lf = DMA(P.y, K.ql, K.ffn_t[:, pm, :],
                         K.ffnsum_sp[mc:mc + 128, c0:c0 + 512])
                WE(P.y, ev["am_st_all"])
                la = DMA(P.y, K.ql, K.am_t[:, pm], K.am_sp[mc:mc + 128,
                                                           c0:c0 + 512])
                W(P.v, K.stc, ring[63])
                WE(P.v, lf)
                WE(P.v, la)
                W(P.v, K.stc, ev.get("ofm_rd_t", 0))
                OP(P.v, lambda v, m_=ml, b=pm: v.tensor_tensor(
                    K.cmb_t[:, b, :], K.PS[m_][:, 0:512], K.am_t[:, b],
                    ALU.add), K.sv)
                e_cb = OP(P.v, lambda v, m0=m, b=pm: v.tensor_tensor(
                    K.B_ofm[:, m0, :], K.cmb_t[:, b, :], K.ffn_t[:, b, :],
                    ALU.add), K.sv)
                bank_free[ml] = e_cb
                ev[f"cmb_free{pm}"] = e_cb
                ev["ofm_v"] = e_cb
            e_mg_prev = ring[63]
        for tl in range(4):
            tt = q * 4 + tl
            W(P.y, K.sv, ev.get("xt3_free", 0))
            lx = DMA(P.y, K.ql, K.xtile[:], K.x_in[tt * 128:(tt + 1) * 128, :])
            for m in range(16):
                pb = 6 + (m & 1)
                W(P.t, K.sv, ev["ofm_v"])
                W(P.t, K.sv, bank_free.get(pb, 0))
                W(P.t, K.sv, ev.get(f"pt3_free{pb}", 0))
                e_tp = OP(P.t, lambda t, m0=m, t1=tl * 128, b=pb: t.transpose(
                    K.PS[b][:, 0:128], K.B_ofm[:, m0, t1:t1 + 128],
                    K.identf[:]), K.stc)
                W(P.v, K.stc, e_tp)
                WE(P.v, lx)
                WE(P.v, ev.get("out_st"))
                e_ot = OP(P.v, lambda v, m0=m, b=pb: v.tensor_tensor(
                    K.outtm[:, m0 * 128:(m0 + 1) * 128], K.PS[b][:, 0:128],
                    K.xtile[:, m0 * 128:(m0 + 1) * 128], ALU.add), K.sv)
                ev[f"pt3_free{pb}"] = e_ot
                bank_free[pb] = e_ot
                ev["out_v"] = e_ot
            ev["xt3_free"] = ev["out_v"]
            W(P.y, K.sv, ev["out_v"])
            ev["out_st"] = DMA(P.y, K.qs, K.out[tt * 128:(tt + 1) * 128, :],
                               K.outtm[:])
        ev["ofm_rd_t"] = K.stc.n
        e_q_t = K.stc.n
    WE(P.g, SNAP(K.qs))
    WE(P.g, SNAP(K.ql))
    WE(P.g, SNAP(K.qg))
    W(P.g, K.sv, K.sv.n)
    W(P.g, K.ss, K.ss.n)
    W(P.g, K.stc, K.stc.n)
    return ev


def build_full(debug=False, stop=None):
    K = build_program(debug)
    P = Plan()
    P.v.ctr = K.sv
    P.s.ctr = K.ss
    P.g.ctr = K.sg
    ev = {}
    phases = [("pre", phase_pre), ("ph0", phase_ph0), ("ph1", phase_ph1),
              ("ph2a", phase_ph2a), ("ph2b", phase_ph2b),
              ("ph2c", phase_ph2c), ("ph3a", phase_ph3a),
              ("ph3b", phase_ph3b)]
    for name, fn in phases:
        ev = fn(K, P, ev)
        if stop == name:
            break
    emit(K.nc, P)
    return K


def _host_inputs(inputs):
    x = np.ascontiguousarray(inputs["L0_attn_hidden_states"])
    f32 = np.float32
    common = {
        "W_in": np.ascontiguousarray(inputs["L0_attn_W_in"], f32),
        "W_out_w": np.ascontiguousarray(inputs["L0_attn_W_out"], f32),
        "w1": np.ascontiguousarray(inputs["L0_attn_w1"], f32),
        "w3": np.ascontiguousarray(inputs["L0_attn_w3"], f32),
        "w2": np.ascontiguousarray(inputs["L0_attn_w2"], f32),
        "Wg": np.ascontiguousarray(inputs["L0_ffn_Wg"], f32),
        "Wu": np.ascontiguousarray(inputs["L0_ffn_Wu"], f32),
        "Wd": np.ascontiguousarray(inputs["L0_ffn_Wd"], f32),
        "Wr": np.ascontiguousarray(inputs["L0_ffn_Wr"], f32),
        "ln1w_in": np.ascontiguousarray(
            np.asarray(inputs["L0_attn_ln1_w"], f32).reshape(16, 128).T),
        "ln2w_in": np.ascontiguousarray(
            np.asarray(inputs["L0_attn_ln2_w"], f32).reshape(16, 128).T),
        "normw_in": np.ascontiguousarray(
            np.asarray(inputs["L0_attn_norm_w"], f32).reshape(16, 128).T),
        "cw_in": np.ascontiguousarray(
            np.asarray(inputs["L0_attn_conv_w"], f32).T.reshape(32, 128, 4)
            .transpose(1, 0, 2)),
        "cb_in": np.ascontiguousarray(
            np.asarray(inputs["L0_attn_conv_b"], f32).reshape(32, 128).T),
        "dtb_in": np.ascontiguousarray(
            np.asarray(inputs["L0_attn_dt_bias"], f32).reshape(16, 1)),
        "alog_in": np.ascontiguousarray(
            np.asarray(inputs["L0_attn_A_log"], f32).reshape(16, 1)),
        "db_in": np.ascontiguousarray(np.broadcast_to(
            np.asarray(inputs["L0_attn_D"], f32)[None, :], (128, 16))),
        "temp_in": np.asarray(inputs["L0_ffn_temperature"],
                              f32).reshape(1, 1),
        "identb_in": np.eye(128, dtype=f32).astype(BF),
        "identf_in": np.eye(128, dtype=f32),
        "triu_in": np.triu(np.ones((128, 128), f32)).astype(BF),
        "maskneg_in": np.where(np.triu(np.ones((128, 128), bool)), 0.0,
                               -1e30).astype(f32),
        "ones1f_in": np.ones((1, 128), f32),
        "ones1b_in": np.ones((1, 128), f32).astype(BF),
        "onescol_in": np.ones((128, 1), f32).astype(BF),
        "ones128b_in": np.ones((128, 128), f32).astype(BF),
        "maskneg4_in": np.ascontiguousarray(np.tile(
            np.where(np.triu(np.ones((128, 128), bool)), 0.0,
                     -1e30).astype(f32), (1, 4))),
        "triu4_in": np.ascontiguousarray(np.tile(
            np.triu(np.ones((128, 128), f32)), (1, 4))).astype(BF),
        "triuN_in": (-np.triu(np.ones((128, 128), f32))).astype(BF),
        "constv_in": np.ascontiguousarray(np.broadcast_to(
            np.array([EPS, 1.0, 0.0, 0.0], f32)[None, :], (128, 4))),
    }
    in_maps = []
    for c in range(8):
        b, half = c // 2, c % 2
        m = dict(common)
        m["x_in"] = np.ascontiguousarray(x[b, half * T:(half + 1) * T, :], f32)
        prev = 0 if half == 0 else 0
        m["xprev_in"] = np.ascontiguousarray(
            x[b, (half - 1) * T:half * T, :] if half == 1 else m["x_in"], f32)
        m["flag_in"] = np.full((128, 1), 1.0 if half == 1 else 0.0, f32)
        in_maps.append(m)
    return in_maps


TRUNC2B = None
_NC_CACHE = {}


def kernel(**inputs):
    if "nc" not in _NC_CACHE:
        _NC_CACHE["nc"] = build_full(DEBUG)
    K = _NC_CACHE["nc"]
    in_maps = _host_inputs(inputs)
    res = run_bass_kernel_spmd(K.nc, in_maps, core_ids=list(range(8)))
    outs = np.empty((4, 4096, 2048), np.float32)
    for c in range(8):
        b, half = c // 2, c % 2
        outs[b, half * T:(half + 1) * T, :] = res.results[c]["out"]
    return outs


# version 20
# speedup vs baseline: 1.0865x; 1.0865x over previous
import numpy as np
import ml_dtypes
import concourse.bass as bass
import concourse.mybir as mybir
from concourse.bass_utils import run_bass_kernel_spmd

F32 = mybir.dt.float32
BF16 = mybir.dt.bfloat16
AF = mybir.ActivationFunctionType
ALU = mybir.AluOpType
AX = mybir.AxisListType
BF = ml_dtypes.bfloat16

T = 2048
DM = 2048
NKT = 16
NNB = 4
NB = 512
NH = 16
NKV = 8
CH = 128
EPS = 1e-5
NE = 8
INTER = 8192
PROJ = 6160
KB = 1024

DEBUG = False


class Ctr:
    def __init__(self, nc, name):
        self.h = nc.alloc_semaphore(name)
        self.n = 0

    def inc(self, ins, k=1):
        ins.then_inc(self.h, k)
        self.n += k
        return self.n


def build_program(debug=False):
    nc = bass.Bass("TRN2", target_bir_lowering=False,
                   detect_race_conditions=False)
    C = {}

    def din(name, shape, dtype=F32):
        C[name] = nc.dram_tensor(name, shape, dtype, kind="ExternalInput").ap()
        return C[name]

    x_in = din("x_in", [T, DM])
    xprev_in = din("xprev_in", [T, DM])
    W_in = din("W_in", [DM, PROJ])
    W_out_w = din("W_out_w", [DM, DM])
    w1 = din("w1", [DM, INTER])
    w3 = din("w3", [DM, INTER])
    w2 = din("w2", [INTER, DM])
    Wg = din("Wg", [NE, DM, DM])
    Wu = din("Wu", [NE, DM, DM])
    Wd = din("Wd", [NE, DM, DM])
    Wr = din("Wr", [DM, NE])
    ln1w_in = din("ln1w_in", [128, 16])
    ln2w_in = din("ln2w_in", [128, 16])
    normw_in = din("normw_in", [128, 16])
    cw_in = din("cw_in", [128, 32, 4])
    cb_in = din("cb_in", [128, 32])
    dtb_in = din("dtb_in", [16, 1])
    alog_in = din("alog_in", [16, 1])
    db_in = din("db_in", [128, 16])
    flag_in = din("flag_in", [128, 1])
    temp_in = din("temp_in", [1, 1])
    identb_in = din("identb_in", [128, 128], BF16)
    identf_in = din("identf_in", [128, 128], F32)
    triu_in = din("triu_in", [128, 128], BF16)
    maskneg_in = din("maskneg_in", [128, 128], F32)
    ones1f_in = din("ones1f_in", [1, 128], F32)
    ones1b_in = din("ones1b_in", [1, 128], BF16)
    onescol_in = din("onescol_in", [128, 1], BF16)
    ones128b_in = din("ones128b_in", [128, 128], BF16)
    maskneg4_in = din("maskneg4_in", [128, 512], F32)
    triu4_in = din("triu4_in", [128, 512], BF16)
    triuN_in = din("triuN_in", [128, 128], BF16)
    constv_in = din("constv_in", [128, 4], F32)

    out = nc.dram_tensor("out", [T, DM], F32, kind="ExternalOutput").ap()

    def scr(name, shape, dtype=BF16):
        kind = "ExternalOutput" if debug else "Internal"
        C[name] = nc.dram_tensor(name, shape, dtype, kind=kind).ap()
        return C[name]

    xn_sp = scr("xn_sp", [DM, T])
    zx_sp = scr("zx_sp", [PROJ, T])
    yu_sp = scr("yu_sp", [DM, T])
    am_sp = scr("am_sp", [DM, T])
    hm_sp = scr("hm_sp", [INTER, T])
    ffn_sp = scr("ffn_sp", [NE, DM, T])
    ffnsum_sp = scr("ffnsum_sp", [DM, T])

    dbg = {}
    if debug:
        def dout(name, shape, dtype=F32):
            dbg[name] = nc.dram_tensor(name, shape, dtype, kind="ExternalOutput").ap()
        dout("d_s0", [16, 128, 128])
        dout("d_w8", [16, 128, NE])
        dout("d_rs2", [1, T])
        dout("d_rsb", [128, T])
        dout("d_rsrow", [1, T])
        dout("d_inh", [8, T])
        dout("d_rt", [8, 1])

    ARENA_BYTES = 202 * KB
    arena = nc.alloc_sbuf_tensor("arena", [128, ARENA_BYTES], mybir.dt.uint8)
    BASE = nc.lookup_mloc(arena).addr

    def at(name, shape, dtype, off):
        per = int(np.prod(shape[1:])) * mybir.dt.size(dtype)
        assert off + per <= ARENA_BYTES, (name, off, per)
        return nc.alloc_sbuf_tensor_at(name, shape, dtype, offset=BASE + off)

    identb = at("identb", [128, 128], BF16, 0)
    triu = at("triu", [128, 128], BF16, 256)
    identf = at("identf", [128, 128], F32, 512)
    maskneg = at("maskneg", [128, 128], F32, 1024)
    ones1f = at("ones1f", [1, 128], F32, 1536)
    ones1b = at("ones1b", [1, 128], BF16, 2048)
    onescol = at("onescol", [128, 1], BF16, 2304)
    ln1w = at("ln1w", [128, 16], F32, 2368)
    ln2w = at("ln2w", [128, 16], F32, 2432)
    normw = at("normw", [128, 16], F32, 2496)
    cw = at("cw", [128, 32, 4], F32, 2560)
    cb = at("cb", [128, 32], F32, 3072)
    dtb = at("dtb", [16, 1], F32, 3200)
    alog = at("alog", [16, 1], F32, 3232)
    aneg = at("aneg", [16, 1], F32, 3264)
    db = at("db", [128, 16], F32, 3296)
    flagc = at("flagc", [128, 1], F32, 3360)
    tempv = at("tempv", [1, 1], F32, 3392)
    rtemp = at("rtemp", [8, 1], F32, 3424)
    tmp16a = at("tmp16a", [16, 1], F32, 3456)
    stat = at("stat", [128, 12], F32, 3488)
    tails = at("tails", [128, 32, 3], BF16, 3552)
    pretail = at("pretail", [128, 16, 3], BF16, 3744)
    grow = at("grow", [1, 16], F32, 3840)
    rsyrow = at("rsyrow", [1, 128], F32, 3904)
    gam_s = at("gam_s", [128, 1], F32, 4416)
    rsyrow2 = at("rsyrow2", [1, 128], F32, 12 * KB)

    ones128b = at("ones128b", [128, 128], BF16, 4608)
    constv = at("constv", [128, 4], F32, 4864)
    nc.const_aps.register(F32, EPS, constv[:, 0:1])
    nc.const_aps.register(F32, 1.0, constv[:, 1:2])
    ut1 = at("ut1", [128, 128], F32, 13 * KB)
    csr2 = at("csr2", [1, 2, 128], F32, 13 * KB + 512)
    rs_row = at("rs_row", [1, T], F32, 5 * KB)
    w8row = at("w8row", [1, T], BF16, 5 * KB)
    rs2acc = at("rs2acc", [1, T], F32, 5 * KB)
    O_XT = 15 * KB

    xtile = at("xtile", [128, DM], F32, O_XT)
    u_t = at("u_t", [128, 16, 128], BF16, O_XT + 4 * KB)
    O_WS = O_XT + 8 * KB

    wsl0 = at("wsl0", [128, 16, 128], BF16, O_WS)
    wsl1 = at("wsl1", [128, 16, 128], BF16, O_WS + 4 * KB)
    ffn_t = at("ffn_t", [128, 8, 512], BF16, O_WS)
    O_RSB = O_WS + 8 * KB

    RSB = at("RSB", [128, T], F32, O_RSB)
    O_S0 = O_RSB + 8 * KB

    S0 = at("S0", [128, 16, 128], F32, O_S0)
    outtm = at("outtm", [128, DM], F32, O_S0)
    O_DT = O_S0 + 8 * KB

    dtrawH = at("dtrawH", [16, 1024], BF16, O_DT)
    dttmp = at("dttmp", [16, 512], F32, O_DT + 2 * KB)
    dtsp = at("dtsp", [16, 512], F32, O_DT + 4 * KB)
    dtda = at("dtda", [16, 512], BF16, O_DT + 6 * KB)
    dt16 = at("dt16", [16, 512], BF16, O_DT + 7 * KB)
    am_t = at("am_t", [128, 2, 512], BF16, O_DT)
    wsl2 = at("wsl2", [128, 16, 128], BF16, O_DT)
    wsl3 = at("wsl3", [128, 16, 128], BF16, O_DT + 4 * KB)
    O_W8 = O_DT + 8 * KB

    w8tm = at("w8tm", [128, 16, 8], F32, O_W8)
    O_STR = O_W8 + 4 * KB

    rowb = at("rowb", [128, DM], BF16, O_STR)
    xnt = at("xnt", [128, 2, 512], BF16, O_STR + 4 * KB)
    sgt = at("sgt", [128, 2, 512], BF16, O_STR + 4 * KB)
    O_A = O_STR + 6 * KB

    A_xn = at("A_xn", [128, 16, T], BF16, O_A)
    A_xno = at("A_xno", [128, 16, 1024], BF16, O_A)
    A_ctmp = at("A_ctmp", [128, 2, 1024], F32, O_A + 32 * KB)
    A_rawo = at("A_rawo", [128, 4, 1027], BF16, O_A + 40 * KB)
    A_rawc = at("A_rawc", [128, 16, 128], BF16, O_A + 32 * KB)
    A_sswp = at("A_sswp", [128, 7424], BF16, O_A + 49 * KB)
    A_yub = at("A_yub", [128, 16, 512], BF16, O_A)
    A_ctm2 = at("A_ctm2", [128, 2, 128], F32, O_A + 16 * KB)
    A_raw = at("A_raw", [128, 32, 515], BF16, O_A + 17 * KB + 32)
    A_ssw2 = at("A_ssw2", [128, 7424], BF16, O_A + 49 * KB + 256)
    A_yur = at("A_yur", [128, 2, 16, 512], BF16, O_A)
    A_h2 = at("A_h2", [128, 16, T], BF16, O_A)
    A_hm = at("A_hm", [128, 64, 512], BF16, O_A)
    O_B = O_A + 64 * KB

    B_h1 = at("B_h1", [128, 16, T], BF16, O_B)
    B_h1o = at("B_h1o", [128, 16, 1024], BF16, O_B)
    B_silo = at("B_silo", [128, 16, 1024], BF16, O_B + 32 * KB)
    B_silnb = at("B_silnb", [128, 32, 512], BF16, O_B)
    B_siz = at("B_siz", [128, 2, 16, 128], BF16, O_B + 32 * KB)
    B_zr = at("B_zr", [128, 16, 512], BF16, O_B + 40 * KB)
    gctw = at("gctw", [128, 4, 512], F32, O_B + 56 * KB)
    cmb_t = at("cmb_t", [128, 2, 512], F32, O_B + 56 * KB)
    pf_t = at("pf_t", [128, 8, 512], BF16, O_B)
    pacc = at("pacc", [128, 2, 512], F32, O_B + 8 * KB)
    pout = at("pout", [128, 512], BF16, O_B + 12 * KB)
    B_inh = at("B_inh", [8, T], F32, O_B)
    B_rt = at("B_rt", [128, 9, 8], F32, O_B + 8 * KB)
    B_ofm = at("B_ofm", [128, 16, 512], F32, O_B)
    w2r = at("w2r", [128, 4, 1024], BF16, O_B + 32 * KB)
    O_END = O_B + 64 * KB
    assert O_END <= ARENA_BYTES

    def ssd_views(tag, base_off):
        v = {}
        v["xv_tm"] = at(f"{tag}xv", [128, 8, 128], BF16, base_off)
        v["b_tm"] = at(f"{tag}bt", [128, 8, 128], BF16, base_off + 2 * KB)
        v["sbf"] = at(f"{tag}sb", [128, 16, 128], BF16, base_off + 4 * KB)
        v["ltp"] = at(f"{tag}lp", [128, 2, 128], F32, base_off + 8 * KB)
        v["lt"] = at(f"{tag}lt", [128, 2, 128], F32, base_off + 9 * KB)
        v["ecsb"] = at(f"{tag}ec", [128, 2, 128], BF16, base_off + 10 * KB)
        v["cp"] = at(f"{tag}cp", [128, 2, 128], BF16, base_off + 10 * KB + 512)
        v["mt"] = at(f"{tag}mt", [128, 2, 128], BF16, base_off + 11 * KB)
        v["xb"] = at(f"{tag}xb", [128, 4, 128], BF16, base_off + 11 * KB + 512)
        v["cs_tm"] = at(f"{tag}cs", [128, 16], F32, base_off + 12 * KB + 512)
        v["ncs"] = at(f"{tag}nc", [128, 16], F32, base_off + 12 * KB + 576)
        v["dt_tm"] = at(f"{tag}dt", [128, 16], F32, base_off + 12 * KB + 640)
        v["dec_tm"] = at(f"{tag}de", [128, 16], F32, base_off + 12 * KB + 704)
        v["da_tm"] = at(f"{tag}da", [128, 16], BF16, base_off + 12 * KB + 768)
        v["da_tmF"] = at(f"{tag}dF", [128, 16], F32, base_off + 13 * KB + 960)
        v["da_bc"] = at(f"{tag}db", [128, 2, 128], BF16, base_off + 12 * KB + 800)
        v["cs_fm"] = at(f"{tag}cf", [16, 128], F32, base_off + 13 * KB + 320)
        v["dec_fm"] = at(f"{tag}df", [16, 128], F32, base_off + 13 * KB + 832)
        return v

    SSD_PRE = ssd_views("p_", O_A + 49 * KB)
    SSD_MAIN = ssd_views("m_", O_A + 49 * KB + 256)
    w8b = at("w8b", [128, T], BF16, O_XT)

    ltp4 = at("ltp4", [128, 512], F32, O_A + 17 * KB + 32)
    dbt4 = at("dbt4", [128, 512], BF16, O_A + 19 * KB + 32)
    ec4 = at("ec4", [128, 2, 512], BF16, O_A + 20 * KB + 32)
    cp4 = at("cp4", [128, 512], BF16, O_A + 22 * KB + 32)
    mt4 = at("mt4", [128, 512], BF16, O_A + 16 * KB)
    lt4 = at("lt4", [128, 2, 512], F32, O_W8)
    ut4 = at("ut4", [128, 4, 128], F32, 13 * KB + 512)
    maskneg4 = at("maskneg4", [128, 512], F32, O_STR)
    triu4 = at("triu4", [128, 512], BF16, O_STR + 2 * KB)
    triuN = at("triuN", [128, 128], BF16, O_STR + 3 * KB)
    rsytmp = at("rsytmp", [1, 128], F32, O_STR + 3 * KB + 256)
    da_bc4 = at("da_bc4", [128, 512], BF16, O_DT + 7 * KB)
    xb2 = at("xb2", [128, 4, 128], BF16, O_A + 23 * KB + 32)
    grow_bc = at("grow_bc", [128, 16], F32, 4448)
    sgt4a = at("sgt4a", [128, 2, 4, 128], BF16, O_STR + 4 * KB)

    import contextlib
    psctx = contextlib.ExitStack()
    PS = [psctx.enter_context(nc.psum_tensor(f"PS{i}", [128, 512], F32))
          for i in range(8)]
    nc._keep_ps = psctx

    PT = [PS[6].ap().bitcast(BF16), PS[7].ap().bitcast(BF16)]

    qg = DPool(nc, "qg", 12)
    ql = DPool(nc, "ql", 12)
    qs = DPool(nc, "qs", 12)
    sg = Ctr(nc, "sg")
    sv = Ctr(nc, "sv")
    ss = Ctr(nc, "ss")
    stc = Ctr(nc, "stc")

    names = dict(locals())
    names.pop("C")
    obj = type("K", (), {})()
    for k, v in names.items():
        setattr(obj, k, v)
    return obj


class EList(list):
    ctr = None


class Plan:
    def __init__(self):
        self.y = EList()
        self.g = EList()
        self.t = EList()
        self.s = EList()
        self.v = EList()


def W(lst, ctr, val):
    if val > 0:
        lst.append(lambda e, h=ctr.h, x=val: e.wait_ge(h, x))


def OP(lst, fn, ctr=None, k=1):
    auto = getattr(lst, "ctr", None)
    if auto is not None:
        c = ctr if ctr is not None else auto
        assert c is auto, "self-synced lists must use their own engine ctr"
        prev = c.n
        c.n += k

        def f2(e, f=fn, h=c.h, kk=k, p=prev):
            if p > 0:
                e.wait_ge(h, p)
            f(e).then_inc(h, kk)
        lst.append(f2)
        return c.n
    if ctr is None:
        lst.append(fn)
        return None
    ctr.n += k
    lst.append(lambda e, f=fn, h=ctr.h, kk=k: f(e).then_inc(h, kk))
    return ctr.n


class DPool:

    def __init__(self, nc, name, n):
        self.sems = [Ctr(nc, f"{name}{i}") for i in range(n)]
        self.i = 0

    def next(self):
        c = self.sems[self.i]
        self.i = (self.i + 1) % len(self.sems)
        return c


def DMA(lst, pool, dst, src):
    c = pool.next()
    prev = c.n
    c.n += 16
    tgt = c.n

    def f(e, d=dst, s=src, h=c.h, p=prev, k=16):
        if p > 0:
            e.wait_ge(h, p)
        e.dma_start(out=d, in_=s).then_inc(h, k)
    lst.append(f)
    return (c, tgt)


def WE(lst, evt):
    if not evt:
        return
    if isinstance(evt, list):
        for e in evt:
            WE(lst, e)
        return
    c, v = evt
    lst.append(lambda e, h=c.h, x=v: e.wait_ge(h, x))


def SNAP(pool):
    return [(c, c.n) for c in pool.sems if c.n > 0]


def emit(nc, plan):
    with nc.Block() as block:
        @block.sync
        def _(y):
            for f in plan.y:
                f(y)

        @block.gpsimd
        def _(g):
            for f in plan.g:
                f(g)

        @block.tensor
        def _(t):
            for f in plan.t:
                f(t)

        @block.scalar
        def _(s):
            for f in plan.s:
                f(s)

        @block.vector
        def _(v):
            for f in plan.v:
                f(v)


def ln_tiles(K, P, src_ap, dst, half_only, ev):
    tiles = range(8) if half_only is not None else range(16)
    stat = K.stat
    E_xn = 0
    E_rsrow_t = 0
    for i in tiles:
        ti = (half_only * 8 + i) if half_only is not None else i
        col0 = (i * 128) if half_only is not None else (ti * 128)
        W(P.y, K.sv, ev.get("xt_free_v", 0))
        W(P.y, K.ss, ev.get("xt_free_s", 0))
        ld = DMA(P.y, K.ql, K.xtile[:], src_ap[ti * 128:(ti + 1) * 128, :])
        WE(P.s, ld)
        WE(P.s, ev.get("consts_l"))
        W(P.s, K.stc, ev.get("rowb_free_t", 0))
        e_sq = OP(P.s, lambda s: s.activation(K.rowb[:], K.xtile[:], AF.Square),
                  K.ss)
        WE(P.v, ld)
        OP(P.v, lambda v: v.reduce_sum(stat[:, 0:1], K.xtile[:], axis=AX.X))
        W(P.v, K.ss, e_sq)
        OP(P.v, lambda v: v.reduce_sum(stat[:, 1:2], K.rowb[:], axis=AX.X))
        OP(P.v, lambda v: v.tensor_scalar_mul(stat[:, 2:3], stat[:, 0:1], 1.0 / DM))
        OP(P.v, lambda v: v.tensor_scalar_mul(stat[:, 3:4], stat[:, 1:2], 1.0 / DM))
        OP(P.v, lambda v: v.tensor_mul(stat[:, 4:5], stat[:, 2:3], stat[:, 2:3]))
        OP(P.v, lambda v: v.tensor_sub(stat[:, 5:6], stat[:, 3:4], stat[:, 4:5]))
        OP(P.v, lambda v: v.tensor_scalar_add(stat[:, 6:7], stat[:, 5:6], EPS))
        OP(P.v, lambda v: v.reciprocal(stat[:, 7:8], stat[:, 6:7]))
        e_var = OP(P.v, lambda v: v.tensor_mul(
            stat[:, 6:7], stat[:, 5:6], stat[:, 7:8]), K.sv)
        W(P.s, K.sv, e_var)
        OP(P.s, lambda s: s.activation(stat[:, 8:9], stat[:, 5:6], AF.Ln, bias=EPS))
        e_rstd = OP(P.s, lambda s: s.activation(stat[:, 9:10], stat[:, 8:9],
                                                AF.Exp, scale=-0.5), K.ss)
        OP(P.s, lambda s: s.activation(stat[:, 10:11], stat[:, 6:7], AF.Ln, bias=EPS))
        e_rs = OP(P.s, lambda s: s.activation(stat[:, 11:12], stat[:, 10:11],
                                              AF.Exp, scale=-0.5), K.ss)
        W(P.v, K.ss, e_rstd)
        W(P.v, K.stc, ev.get("rowb_free_t", 0))
        e_xn = OP(P.v, lambda v: v.tensor_scalar(
            K.rowb[:], K.xtile[:], stat[:, 2:3], stat[:, 9:10],
            ALU.subtract, ALU.mult), K.sv)
        e_ev = 0
        for k in range(16):
            par = k & 1
            W(P.t, K.sv, e_xn)
            W(P.t, K.sv, ev.get(f"pt{par}_free", 0))
            e_tp = OP(P.t, lambda t, kk=k, pp=par: t.transpose(
                K.PT[pp][:, 0:128], K.rowb[:, kk * 128:(kk + 1) * 128],
                K.identb[:]), K.stc)
            W(P.v, K.stc, e_tp)
            e_ev = OP(P.v, lambda v, kk=k, pp=par, c0=col0: v.tensor_copy(
                dst[:, kk, c0:c0 + 128], K.PT[pp][:, 0:128]), K.sv)
            ev[f"pt{par}_free"] = e_ev
            ev["rowb_free_t"] = e_tp
        W(P.s, K.sv, e_var)
        W(P.t, K.ss, e_rs)
        W(P.t, K.sv, ev.get("ps5_free", 0))
        e_tr = OP(P.t, lambda t: t.transpose(
            K.PS[5][0:1, 0:128], stat[:, 11:12], K.identf[:]), K.stc)
        W(P.v, K.stc, e_tr)
        e_rr = OP(P.v, lambda v, c=ti * 128: v.tensor_copy(
            K.rs_row[0:1, c:c + 128], K.PS[5][0:1, 0:128]), K.sv)
        ev["ps5_free"] = e_rr
        E_rsrow_t = e_rr
        ev["xt_free_v"] = e_xn
        ev["xt_free_s"] = e_rs
        ev["rowb_free_v"] = e_ev
        E_xn = e_ev
    return E_xn, E_rsrow_t


def build_rsb(K, P, cols0, ncols, e_ready_v):
    e_last = 0
    evs = {}
    for j in range(ncols // 512):
        c = cols0 + j * 512
        bank = K.PS[j % 2]
        W(P.t, K.sv, e_ready_v)
        W(P.t, K.sv, evs.get(j % 2, 0))
        e_mm = OP(P.t, lambda t, b=bank, cc=c: t.matmul(
            b[:, 0:512], K.ones1f[0:1, :], K.rs_row[0:1, cc:cc + 512],
            start=True, stop=True), K.stc)
        W(P.v, K.stc, e_mm)
        e_last = OP(P.v, lambda v, b=bank, cc=c: v.tensor_copy(
            K.RSB[:, cc:cc + 512], b[:, 0:512]), K.sv)
        evs[j % 2] = e_last
    return e_last


def softplus_block(K, P, dst_sp, dst_da, src_bf, n, src_evt):
    WE(P.s, [src_evt])
    OP(P.s, lambda s: s.activation(K.dttmp[:, 0:n], src_bf, AF.Abs, bias=K.dtb[:]))
    OP(P.s, lambda s: s.activation(K.dttmp[:, 0:n], K.dttmp[:, 0:n], AF.Exp,
                                   scale=-1.0))
    e_l = OP(P.s, lambda s: s.activation(K.dttmp[:, 0:n], K.dttmp[:, 0:n], AF.Ln,
                                         bias=1.0), K.ss)
    WE(P.v, [src_evt])
    OP(P.v, lambda v: v.tensor_scalar(dst_sp, src_bf, K.dtb[:], 0.0,
                                      ALU.add, ALU.max))
    W(P.v, K.ss, e_l)
    OP(P.v, lambda v: v.tensor_add(dst_sp, dst_sp, K.dttmp[:, 0:n]))
    e_da = OP(P.v, lambda v: v.tensor_scalar_mul(dst_da, dst_sp, K.aneg[:]),
              K.sv)
    return e_da


def chunk_prep(K, P, SW, da_ap, dt_ap, ev, need_dec):
    W(P.v, K.ss, K.ss.n)
    if ev.get("single_pt"):
        p = 0
    else:
        p = ev.get("prep_par", 0)
        ev["prep_par"] = 1 - p
    W(P.t, K.sv, ev.get(f"pt{p}_free", 0))
    W(P.v, K.sv, ev.get("da_src_v", 0))
    W(P.t, K.sv, ev.get("da_src_v", 0))
    e_tp = OP(P.t, lambda t, pp=p, a=da_ap: t.transpose(
        K.PT[pp][:, 0:16], a, K.identb[0:16, 0:16]), K.stc)
    W(P.v, K.stc, e_tp)
    OP(P.v, lambda v, pp=p: v.tensor_copy(SW["da_tmF"][:], K.PT[pp][:, 0:16]))
    e_da = OP(P.v, lambda v, pp=p: v.tensor_copy(
        SW["da_tm"][:], K.PT[pp][:, 0:16]), K.sv)
    ev[f"pt{p}_free"] = e_da
    W(P.t, K.sv, ev.get("ps3_free", 0))
    W(P.t, K.sv, ev.get("ps3h_free", 0))
    e_tp2 = OP(P.t, lambda t, a=dt_ap: t.transpose(
        K.PS[3][:, 0:16], a, K.identf[0:16, 0:16]), K.stc)
    W(P.v, K.stc, e_tp2)
    e_dt = OP(P.v, lambda v: v.tensor_copy(SW["dt_tm"][:], K.PS[3][:, 0:16]),
              K.sv)
    W(P.t, K.sv, e_dt)
    W(P.t, K.sv, e_da)
    e_cs = OP(P.t, lambda t: t.matmul(K.PS[3][:, 16:32], K.triu[:],
                                      SW["da_tm"][:], start=True, stop=True),
              K.stc)
    W(P.v, K.stc, e_cs)
    e_csv = OP(P.v, lambda v: v.tensor_copy(SW["cs_tm"][:], K.PS[3][:, 16:32]),
               K.sv)
    e_ncs = OP(P.v, lambda v: v.tensor_scalar_mul(SW["ncs"][:], SW["cs_tm"][:],
                                                  -1.0), K.sv)
    W(P.t, K.sv, e_csv)
    e_tp3 = OP(P.t, lambda t: t.transpose(K.PS[3][0:16, 32:160], SW["cs_tm"][:],
                                          K.identf[:]), K.stc)
    W(P.v, K.stc, e_tp3)
    e_csf = OP(P.v, lambda v: v.tensor_copy(SW["cs_fm"][:],
                                            K.PS[3][0:16, 32:160]), K.sv)
    W(P.s, K.sv, e_csf)
    e_g = OP(P.s, lambda s: s.activation(K.tmp16a[:], SW["cs_fm"][:, 127:128],
                                         AF.Exp), K.ss)
    W(P.t, K.ss, e_g)
    e_tp4 = OP(P.t, lambda t: t.transpose(K.PS[3][0:1, 160:176], K.tmp16a[:],
                                          K.identf[0:16, 0:16]), K.stc)
    W(P.v, K.stc, e_tp4)
    e_gr = OP(P.v, lambda v: v.tensor_copy(K.grow[:], K.PS[3][0:1, 160:176]),
              K.sv)
    ev["grow_v"] = e_gr
    if need_dec:
        e_d = OP(P.s, lambda s: s.activation(
            SW["dec_fm"][:], SW["cs_fm"][:], AF.Exp, scale=-1.0,
            bias=SW["cs_fm"][:, 127:128]), K.ss)
        W(P.t, K.ss, e_d)
        W(P.t, K.sv, e_gr)
        e_tp5 = OP(P.t, lambda t: t.transpose(K.PS[3][:, 176:192],
                                              SW["dec_fm"][:], K.identf[0:16, 0:16]),
                   K.stc)
        W(P.v, K.stc, e_tp5)
        e_dec = OP(P.v, lambda v: v.tensor_copy(SW["dec_tm"][:],
                                                K.PS[3][:, 176:192]), K.sv)
        ev["dec_v"] = e_dec
        ev["ps3_free"] = e_dec
    else:
        ev["ps3_free"] = e_gr
    ev["cs_v"] = e_csv
    ev["ncs_v"] = e_ncs
    ev["dt_v"] = e_dt
    ev["da_v"] = e_da
    return ev


def kv_transposes(K, P, SW, src_xv_ap_fn, src_bv_ap_fn, ev):
    e_x = e_b = 0
    for kv in range(8):
        if ev.get("single_pt"):
            p = 0
        else:
            p = ev.get("kvpar", 0)
            ev["kvpar"] = 1 - p
        W(P.t, K.sv, ev.get(f"pt{p}_free", 0))
        W(P.t, K.ss, ev.get("sil_s", 0))
        e1 = OP(P.t, lambda t, pp=p, a=src_xv_ap_fn(kv): t.transpose(
            K.PT[pp][:, 0:128], a, K.identb[:]), K.stc)
        W(P.v, K.stc, e1)
        e_x = OP(P.v, lambda v, pp=p, kk=kv: v.tensor_copy(
            SW["xv_tm"][:, kk, :], K.PT[pp][:, 0:128]), K.sv)
        ev[f"pt{p}_free"] = e_x
        if ev.get("single_pt"):
            p = 0
        else:
            p = ev.get("kvpar", 0)
            ev["kvpar"] = 1 - p
        W(P.t, K.sv, ev.get(f"pt{p}_free", 0))
        e2 = OP(P.t, lambda t, pp=p, a=src_bv_ap_fn(kv): t.transpose(
            K.PT[pp][:, 0:128], a, K.identb[:]), K.stc)
        W(P.v, K.stc, e2)
        e_b = OP(P.v, lambda v, pp=p, kk=kv: v.tensor_copy(
            SW["b_tm"][:, kk, :], K.PT[pp][:, 0:128]), K.sv)
        ev[f"pt{p}_free"] = e_b
    ev["xvtm_v"] = e_x
    ev["btm_v"] = e_b
    ev["sil_war_t"] = K.stc.n
    return ev


def states_scan_chunk(K, P, SW, ev):
    for h in range(NH):
        kv = h // 2
        sl = 2 * (h & 1)
        W(P.v, K.sv, max(ev["xvtm_v"], ev["dt_v"], ev["dec_v"], ev["btm_v"]))
        e_x = OP(P.v, lambda v, kk=kv, s=sl, hh=h: v.tensor_scalar_mul(
            SW["xb"][:, s, :], SW["xv_tm"][:, kk, :], SW["dt_tm"][:, hh:hh + 1]),
            K.sv)
        e_bd = OP(P.v, lambda v, kk=kv, s=sl, hh=h: v.tensor_scalar_mul(
            SW["xb"][:, s + 1, :], SW["b_tm"][:, kk, :],
            SW["dec_tm"][:, hh:hh + 1]), K.sv)
        W(P.t, K.sv, max(e_x, e_bd, ev.get("ps1_free", 0)))
        W(P.t, K.sv, ev["grow_v"])
        OP(P.t, lambda t, s=sl: t.matmul(K.PS[1][:, 0:128], SW["xb"][:, s + 1, :],
                                         SW["xb"][:, s, :], start=True, stop=True))
        e_mm = OP(P.t, lambda t, hh=h: t.matmul(
            K.PS[1][:, 128:129], K.ones1f[0:1, :], K.grow[0:1, hh:hh + 1],
            start=True, stop=True), K.stc)
        W(P.v, K.stc, e_mm)
        OP(P.v, lambda v, hh=h: v.tensor_scalar_mul(
            K.ut1[:], K.S0[:, hh, :], K.PS[1][:, 128:129]))
        e_s = OP(P.v, lambda v, hh=h: v.tensor_add(
            K.S0[:, hh, :], K.ut1[:], K.PS[1][:, 0:128]), K.sv)
        ev["ps1_free"] = e_s
    return ev


def conv_block(K, P, raw_ap3, sil_dst, g0, nkt, ncols, tmp2, ev):
    for ktl in range(nkt):
        g = g0 + ktl
        W(P.v, K.sv, ev.get("raw_v", 0))
        WE(P.v, ev.get("raw_l"))
        W(P.v, K.ss, ev.get("ctmp_free", 0))
        OP(P.v, lambda v, k=ktl, gg=g: v.tensor_scalar_mul(
            tmp2[:, 0, :], raw_ap3[:, k, 0:ncols], K.cw[:, gg, 0:1]))
        OP(P.v, lambda v, k=ktl, gg=g: v.scalar_tensor_tensor(
            tmp2[:, 1, :], raw_ap3[:, k, 1:1 + ncols], K.cw[:, gg, 1:2],
            tmp2[:, 0, :], ALU.mult, ALU.add))
        OP(P.v, lambda v, k=ktl, gg=g: v.scalar_tensor_tensor(
            tmp2[:, 0, :], raw_ap3[:, k, 2:2 + ncols], K.cw[:, gg, 2:3],
            tmp2[:, 1, :], ALU.mult, ALU.add))
        e_c = OP(P.v, lambda v, k=ktl, gg=g: v.scalar_tensor_tensor(
            tmp2[:, 1, :], raw_ap3[:, k, 3:3 + ncols], K.cw[:, gg, 3:4],
            tmp2[:, 0, :], ALU.mult, ALU.add), K.sv)
        W(P.s, K.sv, e_c)
        W(P.s, K.stc, ev.get("sil_war_t", 0))
        e_s = OP(P.s, lambda s, gg=g: s.activation(
            sil_dst(gg), tmp2[:, 1, :], AF.Silu, bias=K.cb[:, gg:gg + 1]), K.ss)
        ev["ctmp_free"] = e_s
        ev["sil_s"] = e_s
        ev["raw_read_v"] = e_c
    return ev


def phase_pre(K, P, ev):
    consts = [
        (K.identb, K.identb_in), (K.triu, K.triu_in), (K.identf, K.identf_in),
        (K.maskneg, K.maskneg_in), (K.ones1f, K.ones1f_in),
        (K.ones1b, K.ones1b_in), (K.onescol, K.onescol_in),
        (K.ln1w, K.ln1w_in), (K.ln2w, K.ln2w_in), (K.normw, K.normw_in),
        (K.cw, K.cw_in), (K.cb, K.cb_in), (K.dtb, K.dtb_in),
        (K.alog, K.alog_in), (K.db, K.db_in), (K.flagc, K.flag_in),
        (K.tempv, K.temp_in), (K.ones128b, K.ones128b_in),
        (K.constv, K.constv_in),
    ]
    cl = []
    for d, s in consts:
        cl.append(DMA(P.y, K.ql, d[:], s))
    ev["consts_l"] = cl
    WE(P.s, cl)
    e1 = OP(P.s, lambda s: s.activation(K.tmp16a[:], K.alog[:], AF.Exp), K.ss)
    W(P.v, K.ss, e1)
    ev["aneg_v"] = OP(P.v, lambda v: v.tensor_scalar_mul(K.aneg[:], K.tmp16a[:],
                                                         -1.0), K.sv)
    ev["s0_init"] = OP(P.v, lambda v: v.memset(K.S0[:], 0.0), K.sv)

    for hp in (0, 1):
        e_xn, e_rr = ln_tiles(K, P, K.xprev_in, K.A_xno, hp, ev)
        e_rsb = build_rsb(K, P, hp * 1024, 1024, e_rr)
        e_h1 = 0
        for k in range(NKT):
            WE(P.v, ev["consts_l"])
            W(P.v, K.sv, e_rsb)
            W(P.v, K.stc, ev.get("h1o_war_t", 0))
            e_h1 = OP(P.v, lambda v, kk=k, c0=hp * 1024: v.scalar_tensor_tensor(
                K.B_h1o[:, kk, :], K.A_xno[:, kk, :], K.ln1w[:, kk:kk + 1],
                K.RSB[:, c0:c0 + 1024], ALU.mult, ALU.mult), K.sv)
        ev["h1o_v"] = e_h1
        for fq in range(4):
            for ml in range(4):
                g = fq * 4 + ml
                mcol = 2048 + fq * 512 + ml * 128
                slab = K.wsl0 if (ml & 1) == 0 else K.wsl1
                W(P.g, K.stc, ev.get(f"slab{ml & 1}_free", 0))
                lm = DMA(P.g, K.qg, slab[:],
                         K.W_in[:, mcol:mcol + 128].rearrange(
                             "(kt p) m -> p kt m", p=128))
                e_mm_last = 0
                for nn2 in range(2):
                    bank = K.PS[nn2]
                    WE(P.t, lm)
                    W(P.t, K.sv, e_h1)
                    W(P.t, K.sv, ev.get(f"rawb{nn2}_free", 0))
                    for k in range(NKT):
                        e_mm_last = OP(P.t, lambda t, sl=slab, kk=k, b=bank,
                                       c0=nn2 * 512: t.matmul(
                            b[:, 0:512], sl[:, kk, :],
                            K.B_h1o[:, kk, c0:c0 + 512],
                            start=(kk == 0), stop=(kk == NKT - 1)),
                            K.stc if k == NKT - 1 else None)
                    W(P.v, K.stc, e_mm_last)
                    e_ev = OP(P.v, lambda v, m=ml, b=bank, c0=3 + nn2 * 512:
                              v.tensor_copy(K.A_rawo[:, m, c0:c0 + 512],
                                            b[:, 0:512]), K.sv)
                    ev[f"rawb{nn2}_free"] = e_ev
                ev[f"slab{ml & 1}_free"] = e_mm_last
            if hp == 0:
                e_pad = OP(P.v, lambda v: v.memset(K.A_rawo[:, :, 0:3], 0.0),
                           K.sv)
            else:
                W(P.v, K.sv, ev.get("pretail_v", 0))
                e_pad = OP(P.v, lambda v, f=fq: v.tensor_copy(
                    K.A_rawo[:, :, 0:3], K.pretail[:, f * 4:f * 4 + 4, :]),
                    K.sv)
            ev["raw_l"] = None
            ev["raw_v"] = max(e_pad, ev.get(f"rawb0_free", 0),
                              ev.get(f"rawb1_free", 0))
            e_pt = OP(P.v, lambda v, f=fq: v.tensor_copy(
                K.pretail[:, f * 4:f * 4 + 4, :], K.A_rawo[:, :, 1024:1027]),
                K.sv)
            if fq == 3:
                ev["pretail_v"] = e_pt
            ev = conv_block(
                K, P, K.A_rawo,
                lambda gg, c0=hp * 0: K.B_silo[:, gg, :],
                fq * 4, 4, 1024, K.A_ctmp, ev)

        if hp == 1:
            for m in range(16):
                mcol = 4096 + m * 128
                slab = K.wsl0 if (m & 1) == 0 else K.wsl1
                W(P.g, K.stc, ev.get(f"slab{m & 1}_free", 0))
                lm = DMA(P.g, K.qg, slab[:],
                         K.W_in[:, mcol:mcol + 128].rearrange(
                             "(kt p) m -> p kt m", p=128))
                WE(P.t, lm)
                W(P.t, K.sv, ev.get("rawc_free", 0))
                e_mm = 0
                for k in range(NKT):
                    e_mm = OP(P.t, lambda t, sl=slab, kk=k: t.matmul(
                        K.PS[2][:, 0:128], sl[:, kk, :],
                        K.B_h1o[:, kk, 896:1024],
                        start=(kk == 0), stop=(kk == NKT - 1)),
                        K.stc if k == NKT - 1 else None)
                ev[f"slab{m & 1}_free"] = e_mm
                W(P.v, K.stc, e_mm)
                W(P.v, K.ss, ev.get("sil_s", 0))
                e_ev = OP(P.v, lambda v, mm_=m: v.tensor_copy(
                    K.A_rawc[:, mm_, :], K.PS[2][:, 0:128]), K.sv)
                ev["rawc_free"] = e_ev
            ev["rawc_v"] = ev["rawc_free"]

        slab = K.wsl0
        W(P.g, K.stc, ev.get("slab0_free", 0))
        lm = DMA(P.g, K.qg, slab[:, 0:16, 0:16],
                 K.W_in[:, 6144:6160].rearrange("(kt p) m -> p kt m", p=128))
        e_mm = 0
        for nn2 in range(2):
            WE(P.t, lm)
            W(P.t, K.sv, ev.get("dt_ps_free", 0))
            for k in range(NKT):
                e_mm = OP(P.t, lambda t, sl=slab, kk=k, b=nn2: t.matmul(
                    K.PS[4][0:16, 0:512], sl[:, kk, 0:16],
                    K.B_h1o[:, kk, b * 512:b * 512 + 512],
                    start=(kk == 0), stop=(kk == NKT - 1)),
                    K.stc if k == NKT - 1 else None)
            W(P.v, K.stc, e_mm)
            e_dt = OP(P.v, lambda v, b=nn2: v.tensor_copy(
                K.dtrawH[:, b * 512:b * 512 + 512], K.PS[4][0:16, 0:512]),
                K.sv)
            ev["dt_ps_free"] = e_dt
        ev["slab0_free"] = e_mm
        ev["dtraw_v"] = ev["dt_ps_free"]
        ev["h1o_war_t"] = K.stc.n

        for nb2 in range(2):
            c0 = nb2 * 512
            W(P.v, K.sv, ev["aneg_v"])
            e_da = softplus_block(
                K, P, K.dtsp[:, 0:512], K.dtda[:],
                K.dtrawH[:, c0:c0 + 512], 512, (K.sv, ev["dtraw_v"]))
            ev["da_src_v"] = e_da
            for c in range(4):
                t0 = c0 + c * 128
                ev = chunk_prep(K, P, K.SSD_PRE,
                                K.dtda[:, c * 128:(c + 1) * 128],
                                K.dtsp[:, c * 128:(c + 1) * 128], ev, True)
                ev = kv_transposes(
                    K, P, K.SSD_PRE,
                    lambda kv, tt=t0: K.B_silo[:, kv, tt:tt + 128],
                    lambda kv, tt=t0: K.B_silo[:, 8 + kv, tt:tt + 128], ev)
                ev = states_scan_chunk(K, P, K.SSD_PRE, ev)

    W(P.v, K.sv, max(ev["pretail_v"], ev["rawc_v"]))
    WE(P.v, ev["consts_l"])
    OP(P.v, lambda v: v.tensor_scalar_mul(K.tails[:, 0:16, :],
                                          K.pretail[:], K.flagc[:]))
    OP(P.v, lambda v: v.tensor_scalar_mul(
        K.tails[:, 16:32, :], K.A_rawc[:, :, 125:128], K.flagc[:]))
    e_s0 = OP(P.v, lambda v: v.tensor_scalar_mul(K.S0[:], K.S0[:],
                                                 K.flagc[:]), K.sv)
    ev["tails_v"] = e_s0
    ev["s0_v"] = e_s0
    if K.debug and "d_s0" in K.dbg:
        W(P.y, K.sv, e_s0)
        DMA(P.y, K.qs, K.dbg["d_s0"].rearrange("h p n -> p h n"), K.S0[:])
    return ev


def barrier(K, P, ev):
    for lst in (P.t, P.v, P.s, P.g, P.y):
        W(lst, K.sv, K.sv.n)
        W(lst, K.ss, K.ss.n)
        W(lst, K.stc, K.stc.n)


GL = 0.5 - float(np.log(4.0 / 3.0))
GU = 0.5
GCENT = float(1.0 / np.e)


def phase_ph0(K, P, ev):
    barrier(K, P, ev)
    e_xn, e_rr = ln_tiles(K, P, K.x_in, K.A_xn, None, ev)
    ev["xn_v"] = e_xn
    xs = []
    for k in range(NKT):
        W(P.y, K.sv, e_xn)
        xs.append(DMA(P.y, K.qs, K.xn_sp[k * 128:(k + 1) * 128, :],
                      K.A_xn[:, k, :]))
    ev["xnsp_q"] = xs
    e_rsb = build_rsb(K, P, 0, T, e_rr)
    ev["rsb_v"] = e_rsb
    if K.debug:
        W(P.y, K.sv, e_rsb)
        DMA(P.y, K.qs, K.dbg["d_rsb"], K.RSB[:])
        DMA(P.y, K.qs, K.dbg["d_rsrow"], K.rs_row[:])

    W(P.g, K.stc, K.stc.n)
    lw = DMA(P.g, K.qg, K.wsl0[:, :, 0:8],
             K.Wr.rearrange("(kt p) e -> p kt e", p=128))
    WE(P.v, ev["consts_l"])
    e_rc = OP(P.v, lambda v: v.reciprocal(K.rtemp[0:1, 0:1], K.tempv[:]), K.sv)
    W(P.t, K.sv, e_rc)
    e_bb = OP(P.t, lambda t: t.matmul(K.PS[4][0:8, 0:1], K.ones1f[0:1, 0:8],
                                      K.rtemp[0:1, 0:1], start=True, stop=True),
              K.stc)
    W(P.v, K.stc, e_bb)
    e_rt = OP(P.v, lambda v: v.tensor_copy(K.rtemp[:, 0:1], K.PS[4][0:8, 0:1]),
              K.sv)
    e_sig = 0
    for nn in range(NNB):
        WE(P.t, lw)
        W(P.t, K.sv, e_xn)
        W(P.t, K.ss, e_sig)
        W(P.t, K.sv, e_rt)
        e_mm = 0
        for k in range(NKT):
            e_mm = OP(P.t, lambda t, kk=k, c0=nn * 512: t.matmul(
                K.PS[4][0:8, 0:512], K.wsl0[:, kk, 0:8],
                K.A_xn[:, kk, c0:c0 + 512], start=(kk == 0),
                stop=(kk == NKT - 1)), K.stc if k == NKT - 1 else None)
        W(P.s, K.stc, e_mm)
        W(P.s, K.sv, e_rt)
        e_sig = OP(P.s, lambda s, c0=nn * 512: s.activation(
            K.B_inh[:, c0:c0 + 512], K.PS[4][0:8, 0:512], AF.Sigmoid,
            scale=K.rtemp[:, 0:1]), K.ss)
    ev["inh_s"] = e_sig
    if K.debug:
        W(P.y, K.ss, e_sig)
        W(P.y, K.sv, e_rt)
        DMA(P.y, K.qs, K.dbg["d_inh"], K.B_inh[:])
        DMA(P.y, K.qs, K.dbg["d_rt"], K.rtemp[:])

    R = K.B_rt
    e_ones = OP(P.v, lambda v: v.memset(R[:, 8, :], 1.0), K.sv)
    e_tp_prev = 0
    for tt in range(16):
        W(P.t, K.ss, e_sig)
        W(P.t, K.sv, ev.get("ps5_free", 0))
        e_tp = OP(P.t, lambda t, c0=tt * 128: t.transpose(
            K.PS[5][:, 0:8], K.B_inh[:, c0:c0 + 128], K.identf[0:8, 0:8]),
            K.stc)
        W(P.v, K.stc, e_tp)
        OP(P.v, lambda v: v.tensor_copy(R[:, 0, :], K.PS[5][:, 0:8]))
        e_inh = OP(P.v, lambda v: v.engine_nop(), K.sv)
        ev["ps5_free"] = e_inh
        OP(P.v, lambda v: v.tensor_single_scalar(R[:, 1, :], R[:, 0, :], GL,
                                                 ALU.is_ge))
        OP(P.v, lambda v: v.tensor_single_scalar(R[:, 7, :], R[:, 0, :], GU,
                                                 ALU.is_le))
        OP(P.v, lambda v: v.tensor_mul(R[:, 1, :], R[:, 1, :], R[:, 7, :]))
        e_d = OP(P.v, lambda v: v.tensor_single_scalar(
            R[:, 6, :], R[:, 0, :], GCENT, ALU.subtract), K.sv)
        W(P.s, K.sv, e_d)
        OP(P.s, lambda s: s.activation(R[:, 2, :], R[:, 6, :], AF.Abs))
        OP(P.s, lambda s: s.activation(R[:, 3, :], R[:, 2, :], AF.Exp,
                                       scale=-10.0))
        e_e = OP(P.s, lambda s: s.activation(R[:, 4, :], R[:, 2, :], AF.Exp,
                                             scale=-10.0 / 3.0), K.ss)
        W(P.v, K.ss, e_e)
        OP(P.v, lambda v: v.tensor_mul(R[:, 3, :], R[:, 3, :], R[:, 1, :]))
        OP(P.v, lambda v: v.reduce_sum(R[:, 5, 0:1], R[:, 3, :], axis=AX.X))
        OP(P.v, lambda v: v.tensor_single_scalar(R[:, 5, 5:6], R[:, 5, 0:1],
                                                 1e-8, ALU.is_lt))
        e_m1 = OP(P.v, lambda v: v.reduce_max(R[:, 5, 2:3], R[:, 4, :],
                                              axis=AX.X), K.sv)
        W(P.v, K.sv, e_m1)
        OP(P.v, lambda v: v.tensor_single_scalar(R[:, 2, :], R[:, 4, :],
                                                 R[:, 5, 2:3], ALU.is_ge))
        OP(P.v, lambda v: v.scalar_tensor_tensor(R[:, 6, :], R[:, 2, :], -1e30,
                                                 R[:, 4, :], ALU.mult, ALU.add))
        e_m2 = OP(P.v, lambda v: v.reduce_max(R[:, 5, 3:4], R[:, 6, :],
                                              axis=AX.X), K.sv)
        W(P.v, K.sv, e_m2)
        OP(P.v, lambda v: v.tensor_single_scalar(R[:, 7, :], R[:, 6, :],
                                                 R[:, 5, 3:4], ALU.is_ge))
        OP(P.v, lambda v: v.tensor_add(R[:, 7, :], R[:, 7, :], R[:, 2, :]))
        OP(P.v, lambda v: v.tensor_mul(R[:, 7, :], R[:, 7, :], R[:, 4, :]))
        OP(P.v, lambda v: v.reduce_sum(R[:, 5, 4:5], R[:, 7, :], axis=AX.X))
        OP(P.v, lambda v: v.tensor_single_scalar(R[:, 5, 4:5], R[:, 5, 4:5],
                                                 1e-8, ALU.max))
        e_r1 = OP(P.v, lambda v: v.reciprocal(R[:, 5, 6:7], R[:, 5, 4:5]),
                  K.sv)
        W(P.v, K.sv, e_r1)
        OP(P.v, lambda v: v.tensor_single_scalar(R[:, 7, :], R[:, 7, :],
                                                 R[:, 5, 6:7], ALU.mult))
        OP(P.v, lambda v: v.tensor_single_scalar(R[:, 5, 1:2], R[:, 5, 0:1],
                                                 1e-8, ALU.max))
        e_r2 = OP(P.v, lambda v: v.reciprocal(R[:, 5, 7:8], R[:, 5, 1:2]),
                  K.sv)
        W(P.v, K.sv, e_r2)
        OP(P.v, lambda v: v.tensor_single_scalar(R[:, 3, :], R[:, 3, :],
                                                 R[:, 5, 7:8], ALU.mult))
        e_ne = OP(P.v, lambda v: v.tensor_single_scalar(R[:, 2, :], R[:, 8, :],
                                                        R[:, 5, 5:6], ALU.mult),
                  K.sv)
        W(P.v, K.sv, e_ne)
        OP(P.v, lambda v: v.tensor_sub(R[:, 6, :], R[:, 7, :], R[:, 3, :]))
        OP(P.v, lambda v: v.tensor_mul(R[:, 6, :], R[:, 6, :], R[:, 2, :]))
        OP(P.v, lambda v: v.tensor_add(R[:, 3, :], R[:, 3, :], R[:, 6, :]))
        OP(P.v, lambda v: v.reduce_sum(R[:, 5, 0:1], R[:, 3, :], axis=AX.X))
        OP(P.v, lambda v: v.tensor_single_scalar(R[:, 5, 1:2], R[:, 5, 0:1],
                                                 1e-8, ALU.max))
        e_r3 = OP(P.v, lambda v: v.reciprocal(R[:, 5, 6:7], R[:, 5, 1:2]),
                  K.sv)
        W(P.v, K.sv, e_r3)
        OP(P.v, lambda v: v.tensor_single_scalar(R[:, 3, :], R[:, 3, :],
                                                 R[:, 5, 6:7], ALU.mult))
        e_w8 = OP(P.v, lambda v, t0=tt: v.tensor_copy(K.w8tm[:, t0, :],
                                                      R[:, 3, :]), K.sv)
        ev["w8_v"] = e_w8
    if K.debug and "d_w8" in K.dbg:
        W(P.y, K.sv, ev["w8_v"])
        DMA(P.y, K.qs, K.dbg["d_w8"].rearrange("t p e -> p t e"), K.w8tm[:])
    return ev


def phase_ph1(K, P, ev):
    e_dn_prev = K.stc.n
    e_hp_last = 0
    for e in range(NE):
        for tt in range(16):
            W(P.t, K.sv, max(ev.get("ps5_free", 0), ev.get("w8_v", 0)))
            e_tp = OP(P.t, lambda t, t0=tt, ee=e: t.transpose(
                K.PS[5][0:1, 0:128], K.w8tm[:, t0, ee:ee + 1], K.identf[:]),
                K.stc)
            W(P.v, K.stc, e_tp)
            e_wr = OP(P.v, lambda v, t0=tt: v.tensor_copy(
                K.w8row[0:1, t0 * 128:(t0 + 1) * 128], K.PS[5][0:1, 0:128]),
                K.sv)
            ev["ps5_free"] = e_wr
        e_w8b = 0
        for nn in range(NNB):
            W(P.t, K.sv, ev["ps5_free"])
            W(P.t, K.sv, ev.get("w8b_ps_free", 0))
            e_bm = OP(P.t, lambda t, c0=nn * 512: t.matmul(
                K.PS[4][:, 0:512], K.ones1b[0:1, :], K.w8row[0:1, c0:c0 + 512],
                start=True, stop=True), K.stc)
            W(P.v, K.stc, e_bm)
            e_w8b = OP(P.v, lambda v, c0=nn * 512: v.tensor_copy(
                K.w8b[:, c0:c0 + 512], K.PS[4][:, 0:512]), K.sv)
            ev["w8b_ps_free"] = e_w8b
        ev["w8b_v"] = e_w8b

        g_done = {}
        u_done = {}
        for m in range(16):
            mc = m * 128
            gsl = K.wsl0 if (m & 1) == 0 else K.wsl2
            usl = K.wsl1 if (m & 1) == 0 else K.wsl3
            W(P.g, K.stc, g_done.get(m - 2, e_dn_prev))
            gl = DMA(P.g, K.qg, gsl[:],
                     K.Wg[e][:, mc:mc + 128].rearrange("(kt p) m -> p kt m",
                                                       p=128))
            W(P.g, K.stc, u_done.get(m - 2, e_dn_prev))
            ul = DMA(P.g, K.qg, usl[:],
                     K.Wu[e][:, mc:mc + 128].rearrange("(kt p) m -> p kt m",
                                                       p=128))
            for nn in range(NNB):
                c0 = nn * 512
                WE(P.t, gl)
                W(P.t, K.sv, ev["xn_v"])
                W(P.t, K.sv, ev.get(f"hp_free{nn}", 0))
                W(P.t, K.ss, ev.get(f"sg_free{nn}", 0))
                W(P.t, K.sv, ev.get(f"eo_free{nn}", 0))
                e_g = 0
                for k in range(NKT):
                    e_g = OP(P.t, lambda t, sl=gsl, kk=k, b=nn, cc=c0: t.matmul(
                        K.PS[b][:, 0:512], sl[:, kk, :],
                        K.A_xn[:, kk, cc:cc + 512], start=(kk == 0),
                        stop=(kk == NKT - 1)), K.stc if k == NKT - 1 else None)
                g_done[m] = e_g
                WE(P.t, ul)
                W(P.t, K.ss, ev.get(f"sg_free{nn}", 0))
                e_u = 0
                for k in range(NKT):
                    e_u = OP(P.t, lambda t, sl=usl, kk=k, b=4 + nn, cc=c0: t.matmul(
                        K.PS[b][:, 0:512], sl[:, kk, :],
                        K.A_xn[:, kk, cc:cc + 512], start=(kk == 0),
                        stop=(kk == NKT - 1)), K.stc if k == NKT - 1 else None)
                u_done[m] = e_u
                W(P.s, K.stc, e_g)
                W(P.s, K.sv, ev.get(f"sgt_slot{nn & 1}", 0))
                e_sg = OP(P.s, lambda s, b=nn, sl=nn & 1: s.activation(
                    K.sgt[:, sl, :], K.PS[b][:, 0:512], AF.Silu), K.ss)
                ev[f"sg_free{nn}"] = e_sg
                W(P.v, K.ss, e_sg)
                W(P.v, K.stc, e_u)
                W(P.v, K.stc, e_dn_prev)
                e_hp = OP(P.v, lambda v, b=4 + nn, sl=nn & 1, m0=m, cc=c0:
                          v.tensor_tensor(K.B_h1[:, m0, cc:cc + 512],
                                          K.sgt[:, sl, :], K.PS[b][:, 0:512],
                                          ALU.mult), K.sv)
                ev[f"sgt_slot{nn & 1}"] = e_hp
                ev[f"hp_free{nn}"] = e_hp
                e_hp_last = e_hp

        e_dn = 0
        dn_done = {}
        for m in range(16):
            mc = m * 128
            slab = [K.wsl0, K.wsl1, K.wsl2, K.wsl3][m % 4]
            W(P.g, K.stc, [g_done[14], u_done[14], g_done[15], u_done[15]][m % 4])
            W(P.g, K.stc, dn_done.get(m - 4, 0))
            dl = DMA(P.g, K.qg, slab[:],
                     K.Wd[e][:, mc:mc + 128].rearrange("(kt p) m -> p kt m",
                                                       p=128))
            for nn in range(NNB):
                c0 = nn * 512
                WE(P.t, dl)
                W(P.t, K.sv, e_hp_last)
                W(P.t, K.sv, ev.get(f"eo_free{nn}", 0))
                W(P.t, K.ss, ev.get(f"sg_free{nn}", 0))
                e_d = 0
                for k in range(NKT):
                    e_d = OP(P.t, lambda t, sl=slab, kk=k, b=nn, cc=c0:
                             t.matmul(K.PS[b][:, 0:512], sl[:, kk, :],
                                      K.B_h1[:, kk, cc:cc + 512],
                                      start=(kk == 0), stop=(kk == NKT - 1)),
                             K.stc if k == NKT - 1 else None)
                e_dn = e_d
                dn_done[m] = e_d
                W(P.v, K.stc, e_d)
                W(P.v, K.sv, ev["w8b_v"])
                WE(P.v, ev.get("rowb_st"))
                e_eo = OP(P.v, lambda v, b=nn, cc=c0: v.tensor_tensor(
                    K.rowb[:, cc:cc + 512], K.PS[b][:, 0:512],
                    K.w8b[:, cc:cc + 512], ALU.mult), K.sv)
                ev[f"eo_free{nn}"] = e_eo
            W(P.y, K.sv, ev["eo_free3"])
            ev["rowb_st"] = DMA(P.y, K.qs, K.ffn_sp[e][mc:mc + 128, :],
                                K.rowb[:])
        e_dn_prev = e_dn
    ev["ph1_t"] = K.stc.n
    ev["ph1_v"] = K.sv.n
    ev["ffn_st_all"] = SNAP(K.qs)
    return ev


def phase_ph2a(K, P, ev):
    e_h1 = 0
    for k in range(NKT):
        W(P.v, K.stc, ev["ph1_t"])
        W(P.v, K.sv, ev["rsb_v"])
        e_h1 = OP(P.v, lambda v, kk=k: v.scalar_tensor_tensor(
            K.B_h1[:, kk, :], K.A_xn[:, kk, :], K.ln1w[:, kk:kk + 1],
            K.RSB[:], ALU.mult, ALU.mult), K.sv)
    ev["h1_v"] = e_h1

    e_slab = {0: ev["ph1_t"], 1: ev["ph1_t"]}
    for m in range(49):
        mc = m * 128
        mw = 16 if m == 48 else 128
        pr = m & 1
        slab = K.wsl0 if pr == 0 else K.wsl1
        W(P.g, K.stc, e_slab[pr])
        lm = DMA(P.g, K.qg, slab[:, :, 0:mw],
                 K.W_in[:, mc:mc + mw].rearrange("(kt p) m -> p kt m", p=128))
        e_mm = 0
        for nn in range(NNB):
            c0 = nn * 512
            WE(P.t, lm)
            W(P.t, K.sv, e_h1)
            W(P.t, K.sv, ev.get(f"zx_bank{nn}", 0))
            for k in range(NKT):
                e_mm = OP(P.t, lambda t, sl=slab, kk=k, b=nn, cc=c0, w=mw:
                          t.matmul(K.PS[b][0:w, 0:512], sl[:, kk, 0:w],
                                   K.B_h1[:, kk, cc:cc + 512],
                                   start=(kk == 0), stop=(kk == NKT - 1)),
                          K.stc if k == NKT - 1 else None)
            W(P.v, K.stc, e_mm)
            if m == 48:
                WE(P.v, ev.get("dt16_st"))
                e_ev = OP(P.v, lambda v, b=nn: v.tensor_copy(
                    K.dt16[:, 0:512], K.PS[b][0:16, 0:512]), K.sv)
                W(P.y, K.sv, e_ev)
                ev["dt16_st"] = DMA(P.y, K.qs,
                                    K.zx_sp[6144:6160, c0:c0 + 512],
                                    K.dt16[:, 0:512])
                ev[f"zx_bank{nn}"] = e_ev
            else:
                WE(P.v, ev.get("rowb_st"))
                e_ev = OP(P.v, lambda v, b=nn, cc=c0: v.tensor_copy(
                    K.rowb[:, cc:cc + 512], K.PS[b][:, 0:512]), K.sv)
                ev[f"zx_bank{nn}"] = e_ev
                ev["rowb_v"] = e_ev
        e_slab[pr] = e_mm
        if m < 48:
            W(P.y, K.sv, ev["rowb_v"])
            ev["rowb_st"] = DMA(P.y, K.qs, K.zx_sp[mc:mc + 128, :], K.rowb[:])
    ev["ph2a_t"] = K.stc.n
    ev["ph2a_st"] = SNAP(K.qs)
    return ev


def presum_unit(K, P, ev, mp):
    m, n = mp // 4, mp % 4
    mc, cc = m * 128, n * 512
    lf = []
    for e in range(NE):
        W(P.y, K.sv, ev.get("pf_free", 0))
        WE(P.y, ev["ffn_st_all"])
        lf.append(DMA(P.y, K.ql, K.pf_t[:, e, :],
                      K.ffn_sp[e][mc:mc + 128, cc:cc + 512]))
    WE(P.v, lf)
    WE(P.v, ev.get("psum_st"))
    OP(P.v, lambda v: v.tensor_add(K.pacc[:, 0, :], K.pf_t[:, 0, :],
                                   K.pf_t[:, 1, :]), K.sv)
    for e in range(2, NE):
        OP(P.v, lambda v, ee=e: v.tensor_add(
            K.pacc[:, (ee + 1) % 2, :], K.pacc[:, ee % 2, :],
            K.pf_t[:, ee, :]), K.sv)
    e_po = OP(P.v, lambda v: v.tensor_copy(K.pout[:], K.pacc[:, 0, :]),
              K.sv)
    ev["pf_free"] = e_po
    W(P.y, K.sv, e_po)
    ev["psum_st"] = DMA(P.y, K.qs, K.ffnsum_sp[mc:mc + 128, cc:cc + 512],
                        K.pout[:])



def phase_ph2b(K, P, ev):
    SW = K.SSD_MAIN
    W(P.g, K.sv, ev["s0_v"])
    W(P.g, K.stc, ev["ph2a_t"])
    e_sbf0 = OP(P.g, lambda e: e.tensor_copy(SW["sbf"][:], K.S0[:]))
    for h in range(NH):
        ev[f"sbf_v{h}"] = e_sbf0
    ev["prep_par"] = 0
    ev["single_pt"] = True
    e_zs = 0

    WE(P.y, ev.get("rowb_st"))
    ld_m4 = DMA(P.y, K.ql, K.maskneg4[:], K.maskneg4_in)
    ld_t4 = DMA(P.y, K.ql, K.triu4[:], K.triu4_in)
    ld_tn = DMA(P.y, K.ql, K.triuN[:], K.triuN_in)
    ev["c2b_v"] = [ld_m4, ld_t4]
    ev["c2b_t"] = [ld_tn]
    if TRUNC2B == "consts":
        return ev

    for nb in range(NNB):
        c0 = nb * 512
        rl = []
        for fh in range(2):
            W(P.y, K.sv, ev.get("raw_read_v", 0))
            W(P.y, K.ss, ev.get("araw_s", 0))
            W(P.y, K.stc, ev.get("araw_t", 0))
            W(P.y, K.sv, ev.get("araw_v", 0))
            r0 = 2048 + fh * 2048
            if nb == 0:
                ld = DMA(P.y, K.ql, K.A_raw[:, fh * 16:(fh + 1) * 16, 3:515],
                         K.zx_sp[r0:r0 + 2048, 0:512].rearrange(
                             "(kt p) t -> p kt t", p=128))
                rl.append(ld)
                W(P.v, K.sv, ev["tails_v"])
                WE(P.v, ld)
                e_tl = OP(P.v, lambda v, f=fh: v.tensor_copy(
                    K.A_raw[:, f * 16:(f + 1) * 16, 0:3],
                    K.tails[:, f * 16:(f + 1) * 16, :]), K.sv)
                ev["raw_v"] = e_tl
            else:
                ld = DMA(P.y, K.ql, K.A_raw[:, fh * 16:(fh + 1) * 16, :],
                         K.zx_sp[r0:r0 + 2048, c0 - 3:c0 + 512].rearrange(
                             "(kt p) t -> p kt t", p=128))
                rl.append(ld)
                ev["raw_v"] = 0
        ev["raw_l"] = rl
        W(P.y, K.ss, e_zs)
        zl = DMA(P.y, K.ql, K.B_zr[:],
                 K.zx_sp[0:2048, c0:c0 + 512].rearrange("(kt p) t -> p kt t",
                                                        p=128))
        W(P.y, K.sv, ev.get("dtraw_free", 0))
        dtl = DMA(P.y, K.ql, K.dtrawH[:, 0:512],
                  K.zx_sp[6144:6160, c0:c0 + 512])
        W(P.v, K.sv, ev["aneg_v"])
        e_da = softplus_block(K, P, K.dtsp[:, 0:512], K.dtda[:],
                              K.dtrawH[:, 0:512], 512, dtl)
        ev["da_src_v"] = e_da
        ev["dtraw_free"] = e_da

        e_cread = 0
        for g in range(32):
            gp = (g & 1) * 2
            W(P.v, K.sv, ev.get("raw_v", 0))
            WE(P.v, ev["raw_l"])
            W(P.v, K.ss, ev.get("ctmp_s", 0))
            W(P.v, K.stc, ev.get("sil_war_t", 0))
            OP(P.v, lambda v, gg=g, b=gp: v.tensor_scalar_mul(
                K.gctw[:, b, :], K.A_raw[:, gg, 0:512],
                K.cw[:, gg, 0:1]), K.sv)
            OP(P.v, lambda v, gg=g, b=gp: v.scalar_tensor_tensor(
                K.gctw[:, b + 1, :], K.A_raw[:, gg, 1:513],
                K.cw[:, gg, 1:2], K.gctw[:, b, :], ALU.mult, ALU.add), K.sv)
            OP(P.v, lambda v, gg=g, b=gp: v.scalar_tensor_tensor(
                K.gctw[:, b, :], K.A_raw[:, gg, 2:514],
                K.cw[:, gg, 2:3], K.gctw[:, b + 1, :], ALU.mult, ALU.add),
                K.sv)
            e_cread = OP(P.v, lambda v, gg=g, b=gp: v.scalar_tensor_tensor(
                K.gctw[:, b + 1, :], K.A_raw[:, gg, 3:515],
                K.cw[:, gg, 3:4], K.gctw[:, b, :], ALU.mult, ALU.add), K.sv)
            W(P.s, K.sv, e_cread)
            W(P.s, K.stc, ev.get("sil_war_t", 0))
            e_sil = OP(P.s, lambda s, gg=g, b=gp: s.activation(
                K.B_silnb[:, gg, :], K.gctw[:, b + 1, :], AF.Silu,
                bias=K.cb[:, gg:gg + 1]), K.ss)
            ev["ctmp_s"] = e_sil
        ev["raw_read_v"] = e_cread
        ev["sil_s"] = e_sil
        if TRUNC2B == f"{nb}.conv":
            return ev
        for c in range(4):
            tc = c * 128
            pc = c & 1
            WE(P.s, [zl])
            W(P.s, K.sv, ev.get(f"sizpar{pc}_v", 0))
            e_zs = OP(P.s, lambda s, p_=pc, t1=tc: s.activation(
                K.B_siz[:, p_], K.B_zr[:, :, t1:t1 + 128], AF.Silu), K.ss)
            W(P.t, K.sv, max(ev.get("gt_free0", 0), ev.get("gt_free1", 0)))
            ev = chunk_prep(K, P, SW, K.dtda[:, tc:tc + 128],
                            K.dtsp[:, tc:tc + 128], ev, False)
            ev = kv_transposes(K, P, SW,
                               lambda kv, t1=tc: K.B_silnb[:, kv, t1:t1 + 128],
                               lambda kv, t1=tc: K.B_silnb[:, 8 + kv, t1:t1 + 128],
                               ev)
            if TRUNC2B == f"{nb}.{c}.kv":
                return ev
            stA = {}
            W(P.t, K.sv, max(ev["grow_v"], ev.get("ps1_free", 0)))
            e_gbc_mm = OP(P.t, lambda t: t.matmul(
                K.PS[1][:, 0:16], K.ones1f[0:1, :], K.grow[0:1, 0:16],
                start=True, stop=True), K.stc)
            W(P.v, K.stc, e_gbc_mm)
            e_gbc = OP(P.v, lambda v: v.tensor_copy(K.grow_bc[:],
                                                    K.PS[1][:, 0:16]), K.sv)
            ev["ps1_free"] = e_gbc
            ev["gt_free0"] = max(ev.get("gt_free0", 0), K.sv.n)
            ev["gt_free1"] = max(ev.get("gt_free1", 0), K.sv.n)
            ev["zs_e"] = e_zs

            def front(g):
                pg = g & 1
                GTB = K.PS[3] if pg == 0 else K.PS[6]
                W(P.v, K.stc, ev.get("db_free", 0))
                if not ev.get("c2b_done"):
                    WE(P.v, ev.get("dt16_st"))
                    WE(P.v, ev["c2b_v"])
                    ev["c2b_done"] = True
                for k in range(4):
                    OP(P.v, lambda v, g_=g, k_=k: v.tensor_scalar_mul(
                        K.da_bc4[:, k_ * 128:(k_ + 1) * 128], K.ones128b[:],
                        SW["da_tmF"][:, 4 * g_ + k_:4 * g_ + k_ + 1]), K.sv)
                e_dbt = OP(P.v, lambda v: v.tensor_tensor(
                    K.dbt4[:], K.da_bc4[:], K.triu4[:], ALU.mult), K.sv)
                W(P.t, K.sv, e_dbt)
                W(P.t, K.ss, ev.get("ecA_free", 0))
                e_A = OP(P.t, lambda t: t.matmul(
                    K.PS[0][:, 0:512], K.ones128b[:], K.dbt4[:],
                    start=True, stop=True), K.stc)
                W(P.t, K.sv, ev.get("B_free", 0))
                if ev.get("c2b_t"):
                    WE(P.t, ev["c2b_t"])
                    ev["c2b_t"] = None
                OP(P.t, lambda t: t.matmul(
                    K.PS[4][:, 0:512], K.ones128b[:], K.dbt4[:],
                    start=True, stop=False))
                e_B = OP(P.t, lambda t: t.matmul(
                    K.PS[4][:, 0:512], K.triuN[:], K.da_bc4[:],
                    start=False, stop=True), K.stc)
                ev["db_free"] = e_B
                W(P.s, K.stc, e_A)
                W(P.s, K.sv, ev.get(f"ec_free{pg}", 0))
                e_ec4 = OP(P.s, lambda s, p_=pg: s.activation(
                    K.ec4[:, p_, :], K.PS[0][:, 0:512], AF.Exp), K.ss)
                ev["ecA_free"] = e_ec4
                W(P.v, K.stc, e_B)
                W(P.v, K.ss, ev.get("ltp4_free", 0))
                e_lp4 = OP(P.v, lambda v: v.tensor_tensor(
                    K.ltp4[:], K.PS[4][:, 0:512], K.maskneg4[:], ALU.add), K.sv)
                ev["B_free"] = e_lp4
                W(P.s, K.sv, max(e_lp4, ev.get(f"lt4_free{pg}", 0)))
                e_lt4 = OP(P.s, lambda s, p_=pg: s.activation(
                    K.lt4[:, p_, :], K.ltp4[:], AF.Exp), K.ss)
                ev["ltp4_free"] = e_lt4
                ev["lt4_s"] = e_lt4
                W(P.t, K.sv, ev.get(f"gt_free{pg}", 0))
                W(P.t, K.ss, ev["sil_s"])
                e_gt = 0
                for k in range(4):
                    e_gt = OP(P.t, lambda t, k_=k, h_=4 * g + k, t1=tc, b=GTB:
                        t.matmul(b[:, k_ * 128:(k_ + 1) * 128],
                                 K.B_silnb[:, 8 + h_ // 2, t1:t1 + 128],
                                 K.B_silnb[:, 16 + h_, t1:t1 + 128],
                                 start=True, stop=True), K.stc)
                stA[g] = dict(lt=e_lt4, ec=e_ec4, gt=e_gt)

            def back(g):
                pg = g & 1
                GTB = K.PS[3] if pg == 0 else K.PS[6]
                W(P.v, K.stc, max(stA[g]["gt"], ev.get("yb_free", 0)))
                W(P.v, K.ss, stA[g]["lt"])
                e_mt4 = OP(P.v, lambda v, b=GTB, p_=pg: v.tensor_tensor(
                    K.mt4[:], b[:, 0:512], K.lt4[:, p_, :], ALU.mult), K.sv)
                ev[f"gt_free{pg}"] = e_mt4
                W(P.v, K.ss, stA[g]["ec"])
                e_cp = 0
                for k in range(4):
                    e_cp = OP(P.v, lambda v, k_=k, h_=4 * g + k, t1=tc, p_=pg:
                        v.tensor_tensor(
                            K.cp4[:, k_ * 128:(k_ + 1) * 128],
                            K.B_silnb[:, 16 + h_, t1:t1 + 128],
                            K.ec4[:, p_, k_ * 128:(k_ + 1) * 128], ALU.mult),
                        K.sv)
                ev[f"ec_free{pg}"] = e_cp
                W(P.v, K.stc, ev.get("xb_free", 0))
                e_bd = 0
                for k in range(4):
                    h = 4 * g + k
                    kv = h // 2
                    OP(P.v, lambda v, h_=h, kk=k, k_=kv:
                        v.tensor_scalar_mul(SW["xb"][:, kk, :],
                                            SW["xv_tm"][:, k_, :],
                                            SW["dt_tm"][:, h_:h_ + 1]), K.sv)
                    e_bd = OP(P.v, lambda v, k_=kv, p_=pg, kk=k:
                        v.tensor_scalar_mul(
                            K.xb2[:, kk, :], SW["b_tm"][:, k_, :],
                            K.lt4[:, p_, kk * 128 + 127:kk * 128 + 128]), K.sv)
                e_x = e_bd
                W(P.t, K.sg, ev[f"sbf_v{g}"])
                e_y = {}
                for k in range(4):
                    h = 4 * g + k
                    hp = k & 1
                    YB = K.PS[2] if hp == 0 else K.PS[7]
                    W(P.t, K.sv, max(e_mt4, e_cp, e_x,
                                     ev.get(f"y_free{hp}", 0)))
                    OP(P.t, lambda t, kk=k, b=YB: t.matmul(
                        b[:, 0:128], SW["xb"][:, kk, :],
                        K.mt4[:, kk * 128:(kk + 1) * 128], start=True,
                        stop=False))
                    e_yk = OP(P.t, lambda t, h_=h, kk=k, b=YB: t.matmul(
                        b[:, 0:128], SW["sbf"][:, h_, :],
                        K.cp4[:, kk * 128:(kk + 1) * 128], start=False,
                        stop=True), K.stc)
                    e_y[k] = e_yk
                    W(P.v, K.stc, e_yk)
                    W(P.v, K.ss, ev["zs_e"])
                    OP(P.v, lambda v, h_=h, kk=k, t1=tc, b=YB:
                        v.scalar_tensor_tensor(
                            K.ut4[:, kk, :], K.B_silnb[:, h_ // 2, t1:t1 + 128],
                            K.db[:, h_:h_ + 1], b[:, 0:128], ALU.mult,
                            ALU.add), K.sv)
                    ev[f"y_free{hp}"] = K.sv.n
                W(P.t, K.sv, max(e_bd, ev.get("ps1_free", 0)))
                e_smm = 0
                for k in range(4):
                    e_smm = OP(P.t, lambda t, kk=k: t.matmul(
                        K.PS[1][:, kk * 128:(kk + 1) * 128], K.xb2[:, kk, :],
                        SW["xb"][:, kk, :], start=True, stop=True), K.stc)
                ev["xb_free"] = K.stc.n
                W(P.v, K.stc, e_smm)
                e_sup = 0
                for k in range(4):
                    h = 4 * g + k
                    OP(P.v, lambda v, h_=h: v.tensor_scalar_mul(
                        K.ut1[:], K.S0[:, h_, :], K.grow_bc[:, h_:h_ + 1]))
                    e_sup = OP(P.v, lambda v, h_=h, kk=k: v.tensor_add(
                        K.S0[:, h_, :], K.ut1[:],
                        K.PS[1][:, kk * 128:(kk + 1) * 128]), K.sv)
                ev["ps1_free"] = e_sup
                ev["yb_free"] = K.stc.n
                ev[f"lt4_free{pg}"] = K.sv.n
                W(P.g, K.stc, e_y[3])
                W(P.g, K.sv, ev["ps1_free"])
                ev[f"sbf_v{g}"] = OP(P.g, lambda e, g_=g: e.tensor_copy(
                    SW["sbf"][:, 4 * g_:4 * g_ + 4, :],
                    K.S0[:, 4 * g_:4 * g_ + 4, :]))
                e_u4 = OP(P.v, lambda v, g_=g, p_=pc: v.tensor_tensor(
                    K.u_t[:, 4 * g_:4 * g_ + 4, :], K.ut4[:],
                    K.B_siz[:, p_, 4 * g_:4 * g_ + 4, :], ALU.mult), K.sv)
                ev[f"sizpar{pc}_v"] = e_u4
                W(P.s, K.sv, e_u4)
                W(P.s, K.stc, ev.get(f"sq_free{pg}", 0))
                e_sq4 = OP(P.s, lambda s, g_=g, p_=pg: s.activation(
                    K.sgt4a[:, p_], K.u_t[:, 4 * g_:4 * g_ + 4, :],
                    AF.Square), K.ss)
                W(P.t, K.ss, e_sq4)
                if g == 0:
                    W(P.t, K.sv, ev.get("rs_read", 0))
                e_rs = OP(P.t, lambda t, g_=g, p_=pg: t.matmul(
                    K.PS[5][0:1, 0:512], K.onescol[:], K.sgt4a[:, p_],
                    start=(g_ == 0), stop=(g_ == 3)), K.stc)
                ev[f"sq_free{pg}"] = e_rs
                ev["rs_last"] = e_rs

            for tg2, fn2 in (("f0", lambda: front(0)), ("f1", lambda: front(1)),
                             ("b0", lambda: back(0)), ("f2", lambda: front(2)),
                             ("b1", lambda: back(1)), ("f3", lambda: front(3)),
                             ("b2", lambda: back(2)), ("b3", lambda: back(3))):
                fn2()
                if TRUNC2B == f"{nb}.{c}.{tg2}":
                    return ev
            W(P.v, K.stc, ev["rs_last"])
            OP(P.v, lambda v: v.tensor_copy(K.rsyrow[:], K.PS[5][0:1, 0:128]))
            OP(P.v, lambda v: v.tensor_tensor(
                K.rsytmp[:], K.rsyrow[:], K.PS[5][0:1, 128:256], ALU.add))
            OP(P.v, lambda v: v.tensor_tensor(
                K.rsyrow2[:], K.rsytmp[:], K.PS[5][0:1, 256:384], ALU.add))
            e_rsv = OP(P.v, lambda v: v.tensor_tensor(
                K.rsyrow[:], K.rsyrow2[:], K.PS[5][0:1, 384:512], ALU.add),
                K.sv)
            ev["rs_read"] = e_rsv
            W(P.s, K.sv, e_rsv)
            OP(P.s, lambda s: s.activation(K.rsytmp[:], K.rsyrow[:],
                                           AF.Ln, scale=1.0 / DM, bias=EPS))
            e_rsy = OP(P.s, lambda s: s.activation(K.rsyrow2[:], K.rsytmp[:],
                                                   AF.Exp, scale=-0.5), K.ss)
            W(P.t, K.ss, e_rsy)
            W(P.t, K.sv, max(ev.get("y_free0", 0), ev.get("y_free1", 0)))
            e_rb = OP(P.t, lambda t: t.matmul(
                K.PS[2][:, 256:384], K.ones1f[0:1, :], K.rsyrow2[0:1, :],
                start=True, stop=True), K.stc)
            for h in range(NH):
                W(P.v, K.stc, e_rb)
                WE(P.v, ev.get("yub_st"))
                e_yu_last = OP(P.v, lambda v, hh=h, t1=tc: v.scalar_tensor_tensor(
                    K.A_yub[:, hh, t1:t1 + 128], K.u_t[:, hh, :],
                    K.normw[:, hh:hh + 1], K.PS[2][:, 256:384],
                    ALU.mult, ALU.mult), K.sv)
            ev["y_free0"] = e_yu_last
            ev["sil_war_t"] = K.stc.n
            if TRUNC2B == f"{nb}.{c}.yu":
                return ev
        ev["araw_s"] = K.ss.n
        ev["araw_t"] = K.stc.n
        ev["araw_v"] = K.sv.n
        W(P.y, K.sv, e_yu_last)
        ysts = []
        for half in range(2):
            r0 = half * 1024
            ysts.append(DMA(P.y, K.qs,
                            K.yu_sp[r0:r0 + 1024, c0:c0 + 512].rearrange(
                                "(kt p) t -> p kt t", p=128),
                            K.A_yub[:, half * 8:(half + 1) * 8, :]))
        ev["yub_st"] = ysts
    ev["ph2b_st"] = SNAP(K.qs)
    ev["ph2b_t"] = K.stc.n
    return ev


def phase_ph2c(K, P, ev):
    W(P.v, K.stc, ev["ph2b_t"])
    e_clr = OP(P.v, lambda v: v.memset(K.rs2acc[:], 0.0), K.sv)
    e_yl = {}
    e_slab = {0: ev["ph2b_t"], 1: ev["ph2b_t"]}
    e_sq_last = 0
    for nn in range(NNB):
        c0 = nn * 512
        pn = nn & 1
        W(P.y, K.stc, ev["ph2b_t"] if nn < 2 else e_yl.get(nn - 2, 0))
        WE(P.y, ev["ph2b_st"])
        ly = DMA(P.y, K.ql, K.A_yur[:, pn],
                 K.yu_sp[:, c0:c0 + 512].rearrange("(kt p) t -> p kt t", p=128))
        e_mm_all = 0
        for m in range(16):
            mc = m * 128
            pm = m & 1
            slab = K.wsl0 if pm == 0 else K.wsl1
            W(P.g, K.stc, e_slab[pm])
            lm = DMA(P.g, K.qg, slab[:],
                     K.W_out_w[:, mc:mc + 128].rearrange("(kt p) m -> p kt m",
                                                         p=128))
            W(P.y, K.sv, ev.get(f"xnt_free{pm}", 0))
            WE(P.y, ev["xnsp_q"][mc // 128])
            lx = DMA(P.y, K.ql, K.xnt[:, pm],
                     K.xn_sp[mc:mc + 128, c0:c0 + 512])
            WE(P.t, lm)
            WE(P.t, ly)
            W(P.t, K.sv, ev.get(f"am_bank{pm}", 0))
            e_mm = 0
            for k in range(NKT):
                e_mm = OP(P.t, lambda t, sl=slab, kk=k, b=pm, p_=pn: t.matmul(
                    K.PS[b][:, 0:512], sl[:, kk, :], K.A_yur[:, p_, kk, :],
                    start=(kk == 0), stop=(kk == NKT - 1)),
                    K.stc if k == NKT - 1 else None)
            e_slab[pm] = e_mm
            e_mm_all = e_mm
            W(P.v, K.stc, e_mm)
            WE(P.v, lx)
            WE(P.v, ev.get(f"am_st{pm}"))
            e_am = OP(P.v, lambda v, b=pm: v.tensor_tensor(
                K.am_t[:, b, :], K.PS[b][:, 0:512], K.xnt[:, b],
                ALU.add), K.sv)
            ev[f"am_bank{pm}"] = e_am
            ev[f"xnt_free{pm}"] = e_am
            W(P.y, K.sv, e_am)
            ev[f"am_st{pm}"] = DMA(P.y, K.qs,
                                   K.am_sp[mc:mc + 128, c0:c0 + 512],
                                   K.am_t[:, pm])
            W(P.s, K.sv, e_am)
            W(P.s, K.stc, ev.get("sq_slab_t", 0))
            e_sq = OP(P.s, lambda s, b=pm: s.activation(
                K.sgt[:, b, :], K.am_t[:, b, :], AF.Square), K.ss)
            e_sq_last = e_sq
            W(P.t, K.ss, e_sq)
            W(P.t, K.sv, e_clr)
            e_rg = OP(P.t, lambda t, b=pm, first=(m == 0), last=(m == 15):
                      t.matmul(K.PS[2][0:1, 0:512], K.onescol[:],
                               K.sgt[:, b, :], start=first, stop=last),
                      K.stc if m == 15 else None)
            if m == 15:
                W(P.v, K.stc, e_rg)
                e_r2 = OP(P.v, lambda v, cc=c0: v.tensor_copy(
                    K.rs2acc[0:1, cc:cc + 512], K.PS[2][0:1, 0:512]), K.sv)
                ev["rs2_v"] = e_r2
            ev["sq_slab_t"] = K.stc.n
            presum_unit(K, P, ev, nn * 16 + m)
        e_yl[nn] = e_mm_all
    W(P.s, K.sv, ev["rs2_v"])
    W(P.s, K.sg, K.sg.n)
    OP(P.s, lambda s: s.activation(K.outtm[0:1, :], K.rs2acc[:], AF.Ln,
                                   scale=1.0 / DM, bias=EPS))
    e_r2b = OP(P.s, lambda s: s.activation(K.rs2acc[:], K.outtm[0:1, :],
                                           AF.Exp, scale=-0.5), K.ss)
    if K.debug and "d_rs2" in K.dbg:
        W(P.y, K.ss, e_r2b)
        DMA(P.y, K.qs, K.dbg["d_rs2"], K.rs2acc[:])
    e_rsb2 = 0
    for nn in range(NNB):
        W(P.t, K.ss, e_r2b)
        W(P.t, K.sv, ev["h1_v"])
        W(P.t, K.sv, ev.get("rsb2_ps", 0))
        e_bm = OP(P.t, lambda t, cc=nn * 512: t.matmul(
            K.PS[4][:, 0:512], K.ones1f[0:1, :], K.rs2acc[0:1, cc:cc + 512],
            start=True, stop=True), K.stc)
        W(P.v, K.stc, e_bm)
        e_rsb2 = OP(P.v, lambda v, cc=nn * 512: v.tensor_copy(
            K.RSB[:, cc:cc + 512], K.PS[4][:, 0:512]), K.sv)
        ev["rsb2_ps"] = e_rsb2
    ev["rsb2_v"] = e_rsb2
    ev["ph2c_t"] = K.stc.n
    ev["ph2c_ss"] = e_sq_last
    ev["am_st_all"] = SNAP(K.qs)
    ev["ffnsum_st_all"] = SNAP(K.qs)
    return ev


def phase_ph3a(K, P, ev):
    e_h2 = 0
    for k in range(NKT):
        mc = k * 128
        WE(P.y, ev["am_st_all"])
        W(P.y, K.sv, ev.get("rowb3_v", 0))
        W(P.y, K.stc, ev["ph2c_t"])
        lr = DMA(P.y, K.ql, K.rowb[:], K.am_sp[mc:mc + 128, :])
        WE(P.v, lr)
        W(P.v, K.sv, ev["rsb2_v"])
        e_h2 = OP(P.v, lambda v, kk=k: v.scalar_tensor_tensor(
            K.A_h2[:, kk, :], K.rowb[:], K.ln2w[:, kk:kk + 1], K.RSB[:],
            ALU.mult, ALU.mult), K.sv)
        ev["rowb3_v"] = e_h2
    ev["h2_v"] = e_h2

    e_slab = {0: ev["ph2c_t"], 1: ev["ph2c_t"]}
    for m in range(64):
        mc = m * 128
        W(P.g, K.stc, e_slab[0])
        l1 = DMA(P.g, K.qg, K.wsl0[:],
                 K.w1[:, mc:mc + 128].rearrange("(kt p) m -> p kt m", p=128))
        W(P.g, K.stc, e_slab[1])
        l3 = DMA(P.g, K.qg, K.wsl1[:],
                 K.w3[:, mc:mc + 128].rearrange("(kt p) m -> p kt m", p=128))
        for nn in range(NNB):
            c0 = nn * 512
            WE(P.t, l1)
            W(P.t, K.sv, e_h2)
            W(P.t, K.ss, ev.get(f"g_bank{nn}", 0))
            e_g = 0
            for k in range(NKT):
                e_g = OP(P.t, lambda t, kk=k, b=nn, cc=c0: t.matmul(
                    K.PS[b][:, 0:512], K.wsl0[:, kk, :],
                    K.A_h2[:, kk, cc:cc + 512], start=(kk == 0),
                    stop=(kk == NKT - 1)), K.stc if k == NKT - 1 else None)
            e_slab[0] = e_g
            WE(P.t, l3)
            W(P.t, K.sv, ev.get(f"u_bank{nn}", 0))
            e_u = 0
            for k in range(NKT):
                e_u = OP(P.t, lambda t, kk=k, b=4 + nn, cc=c0: t.matmul(
                    K.PS[b][:, 0:512], K.wsl1[:, kk, :],
                    K.A_h2[:, kk, cc:cc + 512], start=(kk == 0),
                    stop=(kk == NKT - 1)), K.stc if k == NKT - 1 else None)
            e_slab[1] = e_u
            W(P.s, K.stc, e_g)
            W(P.s, K.sv, ev.get(f"sgt3_{nn & 1}", 0))
            e_sg = OP(P.s, lambda s, b=nn, sl=nn & 1: s.activation(
                K.sgt[:, sl, :], K.PS[b][:, 0:512], AF.Silu), K.ss)
            ev[f"g_bank{nn}"] = e_sg
            W(P.v, K.ss, e_sg)
            W(P.v, K.stc, e_u)
            WE(P.v, ev.get("hm_st"))
            e_hm = OP(P.v, lambda v, b=4 + nn, sl=nn & 1, cc=c0:
                      v.tensor_tensor(K.rowb[:, cc:cc + 512], K.sgt[:, sl, :],
                                      K.PS[b][:, 0:512], ALU.mult), K.sv)
            ev[f"u_bank{nn}"] = e_hm
            ev[f"sgt3_{nn & 1}"] = e_hm
            ev["hm_v"] = e_hm
        W(P.y, K.sv, ev["hm_v"])
        ev["hm_st"] = DMA(P.y, K.qs, K.hm_sp[mc:mc + 128, :], K.rowb[:])
    ev["ph3a_t"] = K.stc.n
    ev["hm_st_all"] = SNAP(K.qs)
    return ev


def phase_ph3b(K, P, ev):
    e_q_t = ev["ph3a_t"]
    bank_free = {ml: K.sv.n for ml in range(8)}
    for q in range(NNB):
        c0 = q * 512
        lh = []
        for j in range(8):
            W(P.y, K.stc, e_q_t)
            WE(P.y, ev["hm_st_all"])
            lh.append(DMA(P.y, K.ql, K.A_hm[:, j * 8:(j + 1) * 8, :],
                          K.hm_sp[j * 1024:(j + 1) * 1024,
                                  c0:c0 + 512].rearrange(
                              "(kt p) t -> p kt t", p=128)))
        e_mg_prev = e_q_t
        for mg in range(2):
            ring = {}
            for k in range(64):
                slot = k % 4
                W(P.g, K.stc, ring.get(k - 4, e_mg_prev))
                lw = DMA(P.g, K.qg, K.w2r[:, slot, :],
                         K.w2[k * 128:(k + 1) * 128,
                              mg * 1024:(mg + 1) * 1024])
                WE(P.t, lw)
                if k == 0:
                    WE(P.t, lh)
                    for ml in range(8):
                        W(P.t, K.sv, bank_free.get(ml, 0))
                e_k = 0
                for ml in range(8):
                    e_k = OP(P.t, lambda t, s=slot, m_=ml, kk=k: t.matmul(
                        K.PS[m_][:, 0:512], K.w2r[:, s, m_ * 128:(m_ + 1) * 128],
                        K.A_hm[:, kk, :], start=(kk == 0), stop=(kk == 63)),
                        K.stc if ml == 7 else None)
                ring[k] = e_k
            for ml in range(8):
                m = mg * 8 + ml
                mc = m * 128
                pm = ml & 1
                W(P.y, K.sv, ev.get(f"cmb_free{pm}", 0))
                WE(P.y, ev["ffnsum_st_all"])
                lf = DMA(P.y, K.ql, K.ffn_t[:, pm, :],
                         K.ffnsum_sp[mc:mc + 128, c0:c0 + 512])
                WE(P.y, ev["am_st_all"])
                la = DMA(P.y, K.ql, K.am_t[:, pm], K.am_sp[mc:mc + 128,
                                                           c0:c0 + 512])
                W(P.v, K.stc, ring[63])
                WE(P.v, lf)
                WE(P.v, la)
                W(P.v, K.stc, ev.get("ofm_rd_t", 0))
                OP(P.v, lambda v, m_=ml, b=pm: v.tensor_tensor(
                    K.cmb_t[:, b, :], K.PS[m_][:, 0:512], K.am_t[:, b],
                    ALU.add), K.sv)
                e_cb = OP(P.v, lambda v, m0=m, b=pm: v.tensor_tensor(
                    K.B_ofm[:, m0, :], K.cmb_t[:, b, :], K.ffn_t[:, b, :],
                    ALU.add), K.sv)
                bank_free[ml] = e_cb
                ev[f"cmb_free{pm}"] = e_cb
                ev["ofm_v"] = e_cb
            e_mg_prev = ring[63]
        for tl in range(4):
            tt = q * 4 + tl
            W(P.y, K.sv, ev.get("xt3_free", 0))
            lx = DMA(P.y, K.ql, K.xtile[:], K.x_in[tt * 128:(tt + 1) * 128, :])
            for m in range(16):
                pb = 6 + (m & 1)
                W(P.t, K.sv, ev["ofm_v"])
                W(P.t, K.sv, bank_free.get(pb, 0))
                W(P.t, K.sv, ev.get(f"pt3_free{pb}", 0))
                e_tp = OP(P.t, lambda t, m0=m, t1=tl * 128, b=pb: t.transpose(
                    K.PS[b][:, 0:128], K.B_ofm[:, m0, t1:t1 + 128],
                    K.identf[:]), K.stc)
                W(P.v, K.stc, e_tp)
                WE(P.v, lx)
                WE(P.v, ev.get("out_st"))
                e_ot = OP(P.v, lambda v, m0=m, b=pb: v.tensor_tensor(
                    K.outtm[:, m0 * 128:(m0 + 1) * 128], K.PS[b][:, 0:128],
                    K.xtile[:, m0 * 128:(m0 + 1) * 128], ALU.add), K.sv)
                ev[f"pt3_free{pb}"] = e_ot
                bank_free[pb] = e_ot
                ev["out_v"] = e_ot
            ev["xt3_free"] = ev["out_v"]
            W(P.y, K.sv, ev["out_v"])
            ev["out_st"] = DMA(P.y, K.qs, K.out[tt * 128:(tt + 1) * 128, :],
                               K.outtm[:])
        ev["ofm_rd_t"] = K.stc.n
        e_q_t = K.stc.n
    WE(P.g, SNAP(K.qs))
    WE(P.g, SNAP(K.ql))
    WE(P.g, SNAP(K.qg))
    W(P.g, K.sv, K.sv.n)
    W(P.g, K.ss, K.ss.n)
    W(P.g, K.stc, K.stc.n)
    return ev


def build_full(debug=False, stop=None):
    K = build_program(debug)
    P = Plan()
    P.v.ctr = K.sv
    P.s.ctr = K.ss
    P.g.ctr = K.sg
    ev = {}
    phases = [("pre", phase_pre), ("ph0", phase_ph0), ("ph1", phase_ph1),
              ("ph2a", phase_ph2a), ("ph2b", phase_ph2b),
              ("ph2c", phase_ph2c), ("ph3a", phase_ph3a),
              ("ph3b", phase_ph3b)]
    for name, fn in phases:
        ev = fn(K, P, ev)
        if stop == name:
            break
    emit(K.nc, P)
    return K


def _host_inputs(inputs):
    x = np.ascontiguousarray(inputs["L0_attn_hidden_states"])
    f32 = np.float32
    common = {
        "W_in": np.ascontiguousarray(inputs["L0_attn_W_in"], f32),
        "W_out_w": np.ascontiguousarray(inputs["L0_attn_W_out"], f32),
        "w1": np.ascontiguousarray(inputs["L0_attn_w1"], f32),
        "w3": np.ascontiguousarray(inputs["L0_attn_w3"], f32),
        "w2": np.ascontiguousarray(inputs["L0_attn_w2"], f32),
        "Wg": np.ascontiguousarray(inputs["L0_ffn_Wg"], f32),
        "Wu": np.ascontiguousarray(inputs["L0_ffn_Wu"], f32),
        "Wd": np.ascontiguousarray(inputs["L0_ffn_Wd"], f32),
        "Wr": np.ascontiguousarray(inputs["L0_ffn_Wr"], f32),
        "ln1w_in": np.ascontiguousarray(
            np.asarray(inputs["L0_attn_ln1_w"], f32).reshape(16, 128).T),
        "ln2w_in": np.ascontiguousarray(
            np.asarray(inputs["L0_attn_ln2_w"], f32).reshape(16, 128).T),
        "normw_in": np.ascontiguousarray(
            np.asarray(inputs["L0_attn_norm_w"], f32).reshape(16, 128).T),
        "cw_in": np.ascontiguousarray(
            np.asarray(inputs["L0_attn_conv_w"], f32).T.reshape(32, 128, 4)
            .transpose(1, 0, 2)),
        "cb_in": np.ascontiguousarray(
            np.asarray(inputs["L0_attn_conv_b"], f32).reshape(32, 128).T),
        "dtb_in": np.ascontiguousarray(
            np.asarray(inputs["L0_attn_dt_bias"], f32).reshape(16, 1)),
        "alog_in": np.ascontiguousarray(
            np.asarray(inputs["L0_attn_A_log"], f32).reshape(16, 1)),
        "db_in": np.ascontiguousarray(np.broadcast_to(
            np.asarray(inputs["L0_attn_D"], f32)[None, :], (128, 16))),
        "temp_in": np.asarray(inputs["L0_ffn_temperature"],
                              f32).reshape(1, 1),
        "identb_in": np.eye(128, dtype=f32).astype(BF),
        "identf_in": np.eye(128, dtype=f32),
        "triu_in": np.triu(np.ones((128, 128), f32)).astype(BF),
        "maskneg_in": np.where(np.triu(np.ones((128, 128), bool)), 0.0,
                               -1e30).astype(f32),
        "ones1f_in": np.ones((1, 128), f32),
        "ones1b_in": np.ones((1, 128), f32).astype(BF),
        "onescol_in": np.ones((128, 1), f32).astype(BF),
        "ones128b_in": np.ones((128, 128), f32).astype(BF),
        "maskneg4_in": np.ascontiguousarray(np.tile(
            np.where(np.triu(np.ones((128, 128), bool)), 0.0,
                     -1e30).astype(f32), (1, 4))),
        "triu4_in": np.ascontiguousarray(np.tile(
            np.triu(np.ones((128, 128), f32)), (1, 4))).astype(BF),
        "triuN_in": (-np.triu(np.ones((128, 128), f32))).astype(BF),
        "constv_in": np.ascontiguousarray(np.broadcast_to(
            np.array([EPS, 1.0, 0.0, 0.0], f32)[None, :], (128, 4))),
    }
    in_maps = []
    for c in range(8):
        b, half = c // 2, c % 2
        m = dict(common)
        m["x_in"] = np.ascontiguousarray(x[b, half * T:(half + 1) * T, :], f32)
        prev = 0 if half == 0 else 0
        m["xprev_in"] = np.ascontiguousarray(
            x[b, (half - 1) * T:half * T, :] if half == 1 else m["x_in"], f32)
        m["flag_in"] = np.full((128, 1), 1.0 if half == 1 else 0.0, f32)
        in_maps.append(m)
    return in_maps


TRUNC2B = None
_NC_CACHE = {}


def kernel(**inputs):
    if "nc" not in _NC_CACHE:
        _NC_CACHE["nc"] = build_full(DEBUG)
    K = _NC_CACHE["nc"]
    in_maps = _host_inputs(inputs)
    res = run_bass_kernel_spmd(K.nc, in_maps, core_ids=list(range(8)))
    outs = np.empty((4, 4096, 2048), np.float32)
    for c in range(8):
        b, half = c // 2, c % 2
        outs[b, half * T:(half + 1) * T, :] = res.results[c]["out"]
    return outs
